# version 10
# baseline (speedup 1.0000x reference)
import numpy as np
from contextlib import ExitStack

import concourse.bass as bass
import concourse.mybir as mybir
from concourse.bass_utils import run_bass_kernel_spmd

F32 = mybir.dt.float32
BF16 = mybir.dt.bfloat16
I32 = mybir.dt.int32
OP = mybir.AluOpType
AF = mybir.ActivationFunctionType
AX = mybir.AxisListType

NCORES = 8
T = 16384
TC = T // NCORES
D = 1024
E2 = 2048
LC = E2 // NCORES
NST = 16
DTR = 64
A4 = 4096
H2 = 2048
NEXP = 6
NTERMS = 20
RG = [list(range(NCORES))]
NEG = -1e30

TAPS = ()


class Pipe:

    def __init__(self, kb, name, engines, leads=None):
        self.kb = kb
        self.name = name
        self.engines = list(engines)
        self.leads = leads or {}
        self.steps = []

    def step(self, **eng_fns):
        for e in eng_fns:
            assert e in self.engines, (self.name, e)
        self.steps.append(eng_fns)

    def emit(self, block, eng_objs):
        kb = self.kb
        nsteps = len(self.steps)
        done = {}
        for f in self.engines:
            c, arr = 0, []
            for s in range(nsteps):
                if f in self.steps[s]:
                    c += 1
                arr.append(c)
            done[f] = arr
        sems = {f: kb.sem(f"{self.name}_{f}") for f in self.engines}

        def body_for(e):
            def body(eng):
                my = 0
                for s in range(nsteps):
                    fn = self.steps[s].get(e)
                    if fn is None:
                        continue
                    for f in self.engines:
                        if f == e:
                            continue
                        j = s - 1 - self.leads.get((e, f), 0)
                        tgt = done[f][j] if j >= 0 else 0
                        if tgt > 0:
                            eng.wait_ge(sems[f], tgt)
                    last = fn(eng)
                    my += 1
                    if e in ("v", "s"):
                        eng.drain().then_inc(sems[e], 1)
                    elif last is not None:
                        last.then_inc(sems[e], 1)
                    else:
                        eng.sem_inc(sems[e], 1)
            return body

        for e in self.engines:
            getattr(block, {"t": "tensor", "v": "vector", "s": "scalar",
                            "y": "sync", "g": "gpsimd"}[e])(body_for(e))


class KB:
    def __init__(self, nc, ctx):
        self.nc = nc
        self.ctx = ctx
        self.sems = {}

    def sem(self, name):
        if name not in self.sems:
            self.sems[name] = self.ctx.enter_context(self.nc.semaphore(name))
        return self.sems[name]


def dma(eng, out, in_, kb, tag):
    sem = kb.sem(tag)
    n = kb.__dict__.setdefault("_dmacnt", {})
    n[tag] = n.get(tag, 0) + 16
    eng.dma_start(out=out, in_=in_).then_inc(sem, 16)
    return n[tag]


class DmaBatch:

    def __init__(self, kb, eng, tag):
        self.kb, self.eng, self.tag = kb, eng, tag

    def go(self, pairs):
        tgt = None
        for out, in_ in pairs:
            tgt = dma(self.eng, out, in_, self.kb, self.tag)
        self.eng.wait_ge(self.kb.sem(self.tag), tgt)
        return None


def build(taps=()):
    nc = bass.Bass("TRN2", target_bir_lowering=False)
    ctx = ExitStack()
    kb = KB(nc, ctx)
    g = {}

    def inp(name, shape, dtype=F32):
        g[name] = nc.dram_tensor(name, shape, dtype, kind="ExternalInput")

    inp("residT", [D, TC]); inp("norm_w_t", [128, 8]); inp("ln_g_t", [128, 8])
    inp("ln_b_t", [128, 8]); inp("gate_w_t", [128, 8 * NEXP]); inp("gate_b_t", [NEXP, 1])
    inp("in_wT", [D, LC]); inp("skip_wT", [D, LC]); inp("wd1T", [LC, DTR])
    inp("wbT", [LC, NST]); inp("wcT", [LC, NST]); inp("wd2T", [DTR, LC])
    inp("a_log_t", [128, 2 * NST]); inp("conv_w_t", [128, 2 * 4])
    inp("conv_b_t", [128, 2]); inp("wd2_b_t", [128, 2]); inp("w_d_t", [128, 2])
    inp("out_wT", [E2, D]); inp("aw1", [D, A4]); inp("ab1_t", [128, 32])
    inp("aw2", [A4, D]); inp("ab2_t", [128, 8]); inp("ew1", [NEXP, D, H2])
    inp("eb1_t", [128, NEXP * 16]); inp("ew2", [NEXP, H2, D]); inp("eb2_n", [NEXP, D])
    inp("gens_n", [D, D]); inp("gcw_t", [128, 16]); inp("gc_b_t", [1, 1])
    inp("ident", [128, 128], BF16); inp("id8f", [8, 8], F32)
    inp("oh16", [NST, NST * 128], BF16); inp("oh8", [8, 8 * 128], BF16)
    inp("ones1", [128, 1], BF16); inp("ones1r", [1, 128], BF16)
    inp("tok0", [1, 1], I32); inp("cpool", [128, 32], F32)

    g["outT"] = nc.dram_tensor("outT", [D, TC], F32, kind="ExternalOutput")

    def intern(name, shape, dtype):
        g[name] = nc.dram_tensor(name, shape, dtype, kind="Internal")

    intern("cc_xn2_in", [D * TC], BF16); intern("cc_xn2_out", [NCORES * D * TC], BF16)
    intern("cc_bcd_in", [96 * T], F32); intern("cc_bcd_out", [96 * T], F32)
    intern("cc_gy_in", [LC * T], BF16); intern("cc_gy_out", [NCORES * LC * T], BF16)
    intern("cc_t_in", [D * D], BF16); intern("cc_t_out", [NCORES * D * D], BF16)
    intern("base_dr", [D, TC], F32); intern("xnf_dr", [D, TC], BF16)
    intern("gyw_dr", [E2, TC], BF16)
    intern("bt_dr", [NST, T], BF16); intern("ct_dr", [NST, T], BF16)
    intern("d1b_dr", [DTR, T], BF16)
    intern("x_dr", [LC, T], F32); intern("sskip_dr", [LC, T], F32)
    intern("abel_dr", [D, TC], BF16)

    taps = tuple(taps)
    tap_handles = {}

    def tap(name, shape, dtype=F32):
        if name in taps:
            tap_handles[name] = nc.dram_tensor("tap_" + name, shape, dtype,
                                               kind="ExternalOutput")
            return tap_handles[name]
        return None

    with ctx:
        E = ctx.enter_context
        sb = {}

        def sbuf(name, shape, dtype=F32):
            sb[name] = E(nc.sbuf_tensor(name, shape, dtype))
            return sb[name]

        sbuf("ident_s", [128, 128], BF16); sbuf("id8f_s", [8, 8], F32)
        sbuf("ones1_s", [128, 1], BF16); sbuf("ones1r_s", [1, 128], BF16)
        sbuf("oh8_s", [8, 8 * 128], BF16)
        sbuf("nrm_s", [128, 8]); sbuf("lng_s", [128, 8]); sbuf("lnb_s", [128, 8])
        sbuf("gw_s", [128, 8 * NEXP]); sbuf("gb_s", [NEXP, 1])
        sbuf("w6_s", [8, TC], BF16)
        sbuf("alog_s", [128, 2 * NST]); sbuf("acoef_s", [128, 2 * NST])
        sbuf("cw_s", [128, 8]); sbuf("cb_s", [128, 2]); sbuf("wd2b_s", [128, 2])
        sbuf("wdt_s", [128, 2]); sbuf("tok0_s", [1, 1], I32)
        sbuf("cp_s", [128, 32]); sbuf("gcb_s", [1, 1])
        sbuf("eb1_s", [128, NEXP * 16]); sbuf("ab1_s", [128, 32]); sbuf("ab2_s", [128, 8])

        def CP(col, p=128):
            return sb["cp_s"][0:p, col:col + 1]
        g["CP"] = CP
        psum = E(nc.psum_tensor("psum", [128, 4096], F32))

        def PS(bank, n=512, p=128):
            return psum[0:p, 512 * bank: 512 * bank + n]

        g["PS"] = PS
        g["psum"] = psum
        g["sb"] = sb
        g["tap"] = tap

        with nc.Block() as blk:
            @blk.sync
            def _(sy):
                d = DmaBatch(kb, sy, "c0")
                d.go([(sb["ident_s"][:], g["ident"][:]), (sb["id8f_s"][:], g["id8f"][:]),
                      (sb["ones1_s"][:], g["ones1"][:]), (sb["ones1r_s"][:], g["ones1r"][:]),
                      (sb["oh8_s"][:], g["oh8"][:]),
                      (sb["nrm_s"][:], g["norm_w_t"][:]), (sb["lng_s"][:], g["ln_g_t"][:]),
                      (sb["lnb_s"][:], g["ln_b_t"][:]), (sb["gw_s"][:], g["gate_w_t"][:]),
                      (sb["gb_s"][:], g["gate_b_t"][:]), (sb["alog_s"][:], g["a_log_t"][:]),
                      (sb["cw_s"][:], g["conv_w_t"][:]), (sb["cb_s"][:], g["conv_b_t"][:]),
                      (sb["wd2b_s"][:], g["wd2_b_t"][:]), (sb["wdt_s"][:], g["w_d_t"][:]),
                      (sb["tok0_s"][:], g["tok0"][:]),
                      (sb["cp_s"][:], g["cpool"][:]),
                      (sb["gcb_s"][:], g["gc_b_t"][:]),
                      (sb["eb1_s"][:], g["eb1_t"][:]), (sb["ab1_s"][:], g["ab1_t"][:]),
                      (sb["ab2_s"][:], g["ab2_t"][:])])

            @blk.scalar
            def _(sc):
                sc.wait_ge(kb.sem("c0"), 17 * 16)
                sc.activation(out=sb["acoef_s"][:], in_=sb["alog_s"][:], func=AF.Exp)
                sc.activation(out=sb["acoef_s"][:], in_=sb["acoef_s"][:],
                              func=AF.Copy, scale=sb["cp_s"][:, 2:3])

        phase0(kb, g)
        phase_expm(kb, g)
        phase2(kb, g)
        phase3(kb, g)
        phase4(kb, g)
        phase_ffn(kb, g)

    return nc, tap_handles


def phase0(kb, g):
    nc = kb.nc
    sb = g["sb"]
    PS = g["PS"]
    tap = g["tap"]
    SP = 512
    NCH = TC // SP
    CP = g["CP"]
    t_xn = tap("xn", [D, TC]); t_base = tap("base", [D, TC])
    t_xn2 = tap("xn2", [D, TC], BF16); t_xnf = tap("xnf", [D, TC], BF16)
    t_logits = tap("logits", [8, TC]); t_w6 = tap("w6", [8, TC], BF16)

    with ExitStack() as ctx:
        E = ctx.enter_context
        rt = [E(nc.sbuf_tensor(f"p0rt{k}", [128, SP], F32)) for k in range(8)]
        xn = [E(nc.sbuf_tensor(f"p0xn{k}", [128, SP], F32)) for k in range(8)]
        xb = [E(nc.sbuf_tensor(f"p0xb{k}", [128, SP], BF16)) for k in range(8)]
        sq = [E(nc.sbuf_tensor(f"p0sq{k}", [128, SP], BF16)) for k in range(8)]
        x2b = [E(nc.sbuf_tensor(f"p0x2{k}", [128, SP], BF16)) for k in range(8)]
        xfb = [E(nc.sbuf_tensor(f"p0xf{k}", [128, SP], BF16)) for k in range(8)]
        xf32 = [E(nc.sbuf_tensor(f"p0xf32{k}", [128, SP], F32)) for k in range(8)]
        row = E(nc.sbuf_tensor("p0row", [1, 10 * SP], F32))
        rowb = E(nc.sbuf_tensor("p0rowb", [1, 10 * SP], BF16))
        lg6 = E(nc.sbuf_tensor("p0lg6", [8, SP], F32))
        ms8 = E(nc.sbuf_tensor("p0ms8", [128, 8], F32))
        mr1 = E(nc.sbuf_tensor("p0mr1", [128, 8], F32))
        mr2 = E(nc.sbuf_tensor("p0mr2", [128, 8], F32))
        lt0 = E(nc.sbuf_tensor("p0lt0", [128, 8], F32))
        lt1 = E(nc.sbuf_tensor("p0lt1", [128, 8], F32))
        lt2 = E(nc.sbuf_tensor("p0lt2", [128, 8], F32))
        mk1 = E(nc.sbuf_tensor("p0mk1", [128, 8], F32))
        wdd = E(nc.sbuf_tensor("p0wd", [128, 8], F32))
        mks = E(nc.sbuf_tensor("p0mks", [128, 32], F32))
        wds = E(nc.sbuf_tensor("p0wds", [128, 32], F32))
        qq = E(nc.sbuf_tensor("p0q", [128, 4], F32))
        wtok = E(nc.sbuf_tensor("p0wtok", [128, 32], F32))
        wtokb = E(nc.sbuf_tensor("p0wtokb", [128, 32], BF16))

        def R(slot, n=SP):
            return row[0:1, slot * SP: slot * SP + n]

        def Rb(slot, n=SP):
            return rowb[0:1, slot * SP: slot * SP + n]

        pipe = Pipe(kb, "p0", ["y", "t", "s", "v"])
        for i in range(NCH):
            c0, c1 = SP * i, SP * (i + 1)

            def ld(sy, i=i, c0=c0, c1=c1):
                return DmaBatch(kb, sy, "p0ld").go(
                    [(rt[k][:], g["residT"][128 * k:128 * (k + 1), c0:c1])
                     for k in range(8)])
            pipe.step(y=ld)
            def casts(sc):
                for k in range(8):
                    sc.activation(out=xb[k][:], in_=rt[k][:], func=AF.Copy)
                for k in range(7):
                    sc.activation(out=sq[k][:], in_=rt[k][:], func=AF.Square)
                return sc.activation(out=sq[7][:], in_=rt[7][:], func=AF.Square)
            pipe.step(s=casts)

            def sums(tn):
                for k in range(8):
                    tn.matmul(PS(0, SP, 1), sb["ones1_s"][:], xb[k][:],
                              start=(k == 0), stop=(k == 7))
                for k in range(8):
                    m = tn.matmul(PS(1, SP, 1), sb["ones1_s"][:], sq[k][:],
                                  start=(k == 0), stop=(k == 7))
                return m
            pipe.step(t=sums)

            def stat1(sc):
                sc.activation(out=R(0), in_=PS(0, SP, 1), func=AF.Copy,
                              scale=CP(1, 1))
                sc.activation(out=R(6), in_=PS(1, SP, 1), func=AF.Copy,
                              scale=CP(1, 1))
                return sc.activation(out=R(5), in_=R(0), func=AF.Square)
            pipe.step(s=stat1)

            def var1(ve):
                return ve.tensor_sub(R(5), R(6), R(5))
            pipe.step(v=var1)

            def stat2(sc):
                sc.activation(out=R(9), in_=R(5), func=AF.Sqrt, bias=CP(0, 1))
                return sc.activation(out=Rb(0), in_=R(0), func=AF.Copy)
            pipe.step(s=stat2)

            def lo1(ve):
                ve.reciprocal(out=R(1), in_=R(9))
                return ve.tensor_sub(R(5), R(0), Rb(0))
            pipe.step(v=lo1)

            def lo1c(sc):
                sc.activation(out=Rb(5), in_=R(5), func=AF.Copy)
                return sc.activation(out=Rb(1), in_=R(1), func=AF.Copy)
            pipe.step(s=lo1c)

            def lo1b(ve):
                return ve.tensor_sub(R(6), R(1), Rb(1))
            pipe.step(v=lo1b)

            def lo1d(sc):
                return sc.activation(out=Rb(6), in_=R(6), func=AF.Copy)
            pipe.step(s=lo1d)

            def bc1(tn):
                tn.matmul(PS(2, SP), sb["ones1r_s"][:], Rb(0), start=True, stop=False)
                tn.matmul(PS(2, SP), sb["ones1r_s"][:], Rb(5), start=False, stop=True)
                tn.matmul(PS(3, SP), sb["ones1r_s"][:], Rb(1), start=True, stop=False)
                return tn.matmul(PS(3, SP), sb["ones1r_s"][:], Rb(6), start=False, stop=True)
            pipe.step(t=bc1)

            def mkxn(ve):
                for k in range(8):
                    ve.tensor_sub(xn[k][:], rt[k][:], PS(2, SP))
                for k in range(8):
                    ve.tensor_mul(xn[k][:], xn[k][:], PS(3, SP))
                m = None
                for k in range(8):
                    m = ve.scalar_tensor_tensor(out=rt[k][:], in0=xn[k][:], scalar=2.0,
                                                in1=rt[k][:], op0=OP.mult, op1=OP.add)
                return m
            pipe.step(v=mkxn)

            def stbase(sy, i=i, c0=c0, c1=c1):
                pairs = [(g["base_dr"][128 * k:128 * (k + 1), c0:c1], rt[k][:])
                         for k in range(8)]
                if t_base is not None:
                    pairs += [(t_base[128 * k:128 * (k + 1), c0:c1], rt[k][:])
                              for k in range(8)]
                if t_xn is not None:
                    pairs += [(t_xn[128 * k:128 * (k + 1), c0:c1], xn[k][:])
                              for k in range(8)]
                return DmaBatch(kb, sy, "p0st").go(pairs)
            pipe.step(y=stbase)

            def casts2(sc):
                for k in range(8):
                    sc.activation(out=xb[k][:], in_=xn[k][:], func=AF.Copy)
                for k in range(8):
                    m = sc.activation(out=sq[k][:], in_=xn[k][:], func=AF.Square)
                return m
            pipe.step(s=casts2)
            pipe.step(t=sums)

            def stat3(sc):
                sc.activation(out=R(2), in_=PS(1, SP, 1), func=AF.Sqrt,
                              scale=CP(1, 1), bias=CP(0, 1))
                sc.activation(out=R(3), in_=PS(0, SP, 1), func=AF.Copy,
                              scale=CP(1, 1))
                sc.activation(out=R(6), in_=PS(1, SP, 1), func=AF.Copy,
                              scale=CP(1, 1))
                return sc.activation(out=R(5), in_=R(3), func=AF.Square)
            pipe.step(s=stat3)

            def var2(ve):
                ve.reciprocal(out=R(2), in_=R(2))
                return ve.tensor_sub(R(5), R(6), R(5))
            pipe.step(v=var2)

            def stat4(sc):
                sc.activation(out=R(9), in_=R(5), func=AF.Sqrt, bias=CP(0, 1))
                sc.activation(out=Rb(2), in_=R(2), func=AF.Copy)
                return sc.activation(out=Rb(3), in_=R(3), func=AF.Copy)
            pipe.step(s=stat4)

            def lo2(ve):
                ve.reciprocal(out=R(4), in_=R(9))
                ve.tensor_sub(R(5), R(2), Rb(2))
                return ve.tensor_sub(R(6), R(3), Rb(3))
            pipe.step(v=lo2)

            def lo2c(sc):
                sc.activation(out=Rb(5), in_=R(5), func=AF.Copy)
                sc.activation(out=Rb(6), in_=R(6), func=AF.Copy)
                return sc.activation(out=Rb(4), in_=R(4), func=AF.Copy)
            pipe.step(s=lo2c)

            def lo2b(ve):
                return ve.tensor_sub(R(7), R(4), Rb(4))
            pipe.step(v=lo2b)

            def lo2d(sc):
                return sc.activation(out=Rb(7), in_=R(7), func=AF.Copy)
            pipe.step(s=lo2d)

            def bc2(tn):
                tn.matmul(PS(2, SP), sb["ones1r_s"][:], Rb(2), start=True, stop=False)
                tn.matmul(PS(2, SP), sb["ones1r_s"][:], Rb(5), start=False, stop=True)
                tn.matmul(PS(3, SP), sb["ones1r_s"][:], Rb(3), start=True, stop=False)
                tn.matmul(PS(3, SP), sb["ones1r_s"][:], Rb(6), start=False, stop=True)
                tn.matmul(PS(4, SP), sb["ones1r_s"][:], Rb(4), start=True, stop=False)
                return tn.matmul(PS(4, SP), sb["ones1r_s"][:], Rb(7), start=False, stop=True)
            pipe.step(t=bc2)

            def mkx2f(ve):
                for k in range(8):
                    ve.scalar_tensor_tensor(out=x2b[k][:], in0=xn[k][:],
                                            scalar=sb["nrm_s"][:, k:k + 1],
                                            in1=PS(2, SP), op0=OP.mult, op1=OP.mult)
                for k in range(8):
                    ve.tensor_sub(xf32[k][:], xn[k][:], PS(3, SP))
                for k in range(8):
                    ve.tensor_mul(xf32[k][:], xf32[k][:], PS(4, SP))
                m = None
                for k in range(8):
                    ve.tensor_scalar(out=xfb[k][:], in0=xf32[k][:],
                                     scalar1=sb["lng_s"][:, k:k + 1],
                                     scalar2=sb["lnb_s"][:, k:k + 1],
                                     op0=OP.mult, op1=OP.add)
                    m = ve.tensor_scalar(out=xf32[k][:], in0=xf32[k][:],
                                         scalar1=sb["lng_s"][:, k:k + 1],
                                         scalar2=sb["lnb_s"][:, k:k + 1],
                                         op0=OP.mult, op1=OP.add)
                return m
            pipe.step(v=mkx2f)

            def stx2f(sy, i=i, c0=c0, c1=c1):
                pairs = []
                for k in range(8):
                    pairs.append((g["cc_xn2_in"].ap().rearrange(
                        "(d t) -> d t", d=D)[128 * k:128 * (k + 1), c0:c1], x2b[k][:]))
                    pairs.append((g["xnf_dr"][128 * k:128 * (k + 1), c0:c1], xfb[k][:]))
                if t_xn2 is not None:
                    pairs += [(t_xn2[128 * k:128 * (k + 1), c0:c1], x2b[k][:]) for k in range(8)]
                if t_xnf is not None:
                    pairs += [(t_xnf[128 * k:128 * (k + 1), c0:c1], xfb[k][:]) for k in range(8)]
                return DmaBatch(kb, sy, "p0st").go(pairs)
            pipe.step(y=stx2f)

            def glog(tn):
                for k in range(8):
                    m = tn.matmul(PS(6, SP, NEXP), sb["gw_s"][:, 6 * k:6 * (k + 1)],
                                  xf32[k][:], start=(k == 0), stop=(k == 7))
                return m
            pipe.step(t=glog)

            def gevac(ve):
                ve.memset(lg6[:], NEG)
                return ve.tensor_scalar(out=lg6[0:NEXP, :], in0=PS(6, SP, NEXP),
                                        scalar1=sb["gb_s"][:], scalar2=None, op0=OP.add)
            pipe.step(v=gevac)

            def gtr(tn):
                m = None
                for j in range(SP // 128):
                    m = tn.matmul(PS(7, 128, 128)[:, 8 * j:8 * (j + 1)][0:128, 0:8],
                                  lg6[:, 128 * j:128 * (j + 1)], sb["id8f_s"][:],
                                  start=True, stop=True)
                return m
            pipe.step(t=gtr)

            def top2b(ve):
                m = None
                for j in range(SP // 128):
                    pcol = PS(7, 128, 128)[:, 8 * j:8 * (j + 1)]
                    ve.tensor_copy(out=lt0[:], in_=pcol)
                    ve.drain()
                    ve.max(out=ms8[:], in_=lt0[:])
                    ve.memset(mr1[:], NEG)
                    ve.memset(mr2[:], NEG)
                    ve.drain()
                    ve.tensor_copy(out=mr1[:, 0:1], in_=ms8[:, 0:1])
                    ve.tensor_copy(out=mr2[:, 0:1], in_=ms8[:, 1:2])
                    ve.drain()
                    ve.match_replace(out=lt1[:], in_to_replace=mr1[:], in_values=lt0[:],
                                     imm_value=NEG)
                    ve.drain()
                    ve.match_replace(out=lt2[:], in_to_replace=mr2[:], in_values=lt1[:],
                                     imm_value=NEG)
                    ve.drain()
                    ve.tensor_tensor(out=mk1[:], in0=lt0[:], in1=lt1[:], op=OP.not_equal)
                    ve.tensor_tensor(out=wdd[:], in0=lt1[:], in1=lt2[:], op=OP.not_equal)
                    ve.drain()
                    ve.tensor_sub(wdd[:], wdd[:], mk1[:])
                    ve.drain()
                    ve.tensor_copy(out=mks[:, 8 * j:8 * (j + 1)], in_=mk1[:])
                    ve.tensor_copy(out=wds[:, 8 * j:8 * (j + 1)], in_=wdd[:])
                    m = ve.tensor_sub(qq[:, j:j + 1], ms8[:, 1:2], ms8[:, 0:1])
                return m
            pipe.step(v=top2b)

            def sigq(sc):
                return sc.activation(out=qq[:], in_=qq[:], func=AF.Sigmoid)
            pipe.step(s=sigq)

            def wcomb(ve):
                m = None
                for j in range(SP // 128):
                    m = ve.scalar_tensor_tensor(
                        out=wtok[:, 8 * j:8 * (j + 1)],
                        in0=wds[:, 8 * j:8 * (j + 1)],
                        scalar=qq[:, j:j + 1],
                        in1=mks[:, 8 * j:8 * (j + 1)],
                        op0=OP.mult, op1=OP.add)
                return m
            pipe.step(v=wcomb)

            def wcast(ve):
                return ve.tensor_copy(out=wtokb[:], in_=wtok[:])
            pipe.step(v=wcast)

            def wtr(tn):
                m = None
                for j in range(SP // 128):
                    m = tn.matmul(PS(6, 512)[0:8, 128 * j:128 * (j + 1)],
                                  wtokb[:, 8 * j:8 * (j + 1)], sb["ident_s"][:],
                                  start=True, stop=True)
                return m
            pipe.step(t=wtr)

            def wev(ve, i=i, c0=c0, c1=c1):
                return ve.tensor_copy(out=sb["w6_s"][:, c0:c1], in_=PS(6, SP, 8))
            pipe.step(v=wev)

            if t_logits is not None:
                def lgst(sy, i=i, c0=c0, c1=c1):
                    return DmaBatch(kb, sy, "p0st").go([(t_logits[:, c0:c1], lg6[:])])
                pipe.step(y=lgst)

        if t_w6 is not None:
            def w6st(sy):
                return DmaBatch(kb, sy, "p0st").go([(t_w6[:], sb["w6_s"][:])])
            pipe.step(y=w6st)

        with nc.Block() as blk:
            pipe.emit(blk, None)


def phase_expm(kb, g):
    nc = kb.nc
    sb = g["sb"]
    PS = g["PS"]
    CP = g["CP"]
    tap = g["tap"]
    t_tg = tap("tg", [D, D], BF16)

    with ExitStack() as ctx:
        E = ctx.enter_context
        gsf = [E(nc.sbuf_tensor(f"exgf{r}", [128, D], F32)) for r in range(8)]
        gsb = [E(nc.sbuf_tensor(f"exgb{r}", [128, D], BF16)) for r in range(8)]
        pb = [[E(nc.sbuf_tensor(f"expP{p}_{r}", [128, D], BF16)) for r in range(8)]
              for p in range(2)]

        leads = {(e, "g"): 100000 for e in "ytsv"}
        pipe = Pipe(kb, "pex", ["y", "t", "s", "v", "g"], leads=leads)

        def agx(gp):
            gp.collective_compute(
                "AllGather", OP.bypass, replica_groups=RG,
                ins=[g["cc_xn2_in"][:]], outs=[g["cc_xn2_out"][:]]).then_inc(
                kb.sem("agx"), 1)
            gp.wait_ge(kb.sem("agx"), 1)
            return None
        pipe.step(g=agx)

        def ldg(sy):
            return DmaBatch(kb, sy, "exld").go(
                [(gsf[r][:], g["gens_n"][128 * r:128 * (r + 1), :]) for r in range(8)])
        pipe.step(y=ldg)

        def cst(sc):
            m = None
            for r in range(8):
                m = sc.activation(out=gsb[r][:], in_=gsf[r][:], func=AF.Copy)
            return m
        pipe.step(s=cst)

        def initp(ve):
            par = NTERMS % 2
            for r in range(8):
                ve.memset(pb[par][r][:], 0.0)
            ve.drain()
            m = None
            for r in range(8):
                m = ve.tensor_copy(out=pb[par][r][:, 128 * r:128 * (r + 1)],
                                   in_=sb["ident_s"][:])
            return m
        pipe.step(v=initp)

        for k in range(NTERMS, 0, -1):
            src, dst = k % 2, 1 - (k % 2)
            for h in range(2):
                lo = 512 * h

                def mstep(tn, src=src, lo=lo):
                    m = None
                    for mo in range(8):
                        for r in range(8):
                            m = tn.matmul(PS(mo),
                                          gsb[r][:, 128 * mo:128 * (mo + 1)],
                                          pb[src][r][:, lo:lo + 512],
                                          start=(r == 0), stop=(r == 7))
                    return m
                pipe.step(t=mstep)

                def estep(sc, k=k, dst=dst, lo=lo):
                    m = None
                    for mo in range(8):
                        m = sc.activation(out=pb[dst][mo][:, lo:lo + 512],
                                          in_=PS(mo), func=AF.Copy, scale=CP(3 + k))
                    return m
                pipe.step(s=estep)

                def astep(ve, dst=dst, h=h):
                    m = None
                    for mo in range(4 * h, 4 * h + 4):
                        m = ve.tensor_tensor(
                            out=pb[dst][mo][:, 128 * mo:128 * (mo + 1)],
                            in0=pb[dst][mo][:, 128 * mo:128 * (mo + 1)],
                            in1=sb["ident_s"][:], op=OP.add)
                    return m
                pipe.step(v=astep)

        def stT(gp):
            pairs = [(g["cc_t_in"].ap().rearrange("(d n) -> d n", d=D)
                      [128 * r:128 * (r + 1), :], pb[0][r][:]) for r in range(8)]
            if t_tg is not None:
                pairs += [(t_tg[128 * r:128 * (r + 1), :], pb[0][r][:]) for r in range(8)]
            DmaBatch(kb, gp, "exst").go(pairs)
            gp.collective_compute(
                "AllGather", OP.bypass, replica_groups=RG,
                ins=[g["cc_t_in"][:]], outs=[g["cc_t_out"][:]]).then_inc(
                kb.sem("agt"), 1)
            gp.wait_ge(kb.sem("agt"), 1)
            return None
        pipe.step(g=stT)

        with nc.Block() as blk:
            pipe.emit(blk, None)


def phase2(kb, g):
    nc = kb.nc
    sb = g["sb"]
    PS = g["PS"]
    tap = g["tap"]
    E0 = kb.ctx.enter_context
    sb["rtot_s"] = E0(nc.sbuf_tensor("rtot_s", [128, 8 * D], BF16))
    t_rtot = tap("rtot", [128, 8 * D], BF16)
    t_x = tap("x_", [LC, T])
    t_ss = tap("sskip", [LC, T])

    ccx = g["cc_xn2_out"].ap().rearrange("(s d t) -> s d t", s=NCORES, d=D)
    cct = g["cc_t_out"].ap().rearrange("(s d n) -> s d n", s=NCORES, d=D)

    with ExitStack() as ctx:
        E = ctx.enter_context
        tz = [E(nc.sbuf_tensor(f"p2tz{r}", [128, D], BF16)) for r in range(8)]
        zb = [[E(nc.sbuf_tensor(f"p2z{p}_{r}", [128, D], BF16)) for r in range(8)]
              for p in range(2)]
        wstg = [E(nc.sbuf_tensor(f"p2wstg{r}", [128, LC], F32)) for r in range(8)]
        inwb = [E(nc.sbuf_tensor(f"p2inw{r}", [128, LC], BF16)) for r in range(8)]
        skwb = [E(nc.sbuf_tensor(f"p2skw{r}", [128, LC], BF16)) for r in range(8)]
        wd1b = [E(nc.sbuf_tensor(f"p2wd1{m}", [128, DTR], BF16)) for m in range(2)]
        wbb = [E(nc.sbuf_tensor(f"p2wbb{m}", [128, NST], BF16)) for m in range(2)]
        wcb = [E(nc.sbuf_tensor(f"p2wcb{m}", [128, NST], BF16)) for m in range(2)]
        xg = [E(nc.sbuf_tensor(f"p2xg{r}", [128, 512], BF16)) for r in range(8)]
        zext = [E(nc.sbuf_tensor(f"p2ze{m}", [128, 515], F32)) for m in range(2)]
        xc = [E(nc.sbuf_tensor(f"p2xc{m}", [128, 512], F32)) for m in range(2)]
        xf = [E(nc.sbuf_tensor(f"p2xf{m}", [128, 512], F32)) for m in range(2)]
        xbf = [E(nc.sbuf_tensor(f"p2xb{m}", [128, 512], BF16)) for m in range(2)]
        ssk = [E(nc.sbuf_tensor(f"p2ss{m}", [128, 512], F32)) for m in range(2)]
        pd1 = E(nc.sbuf_tensor("p2pd1", [DTR, 512], F32))
        pB = E(nc.sbuf_tensor("p2pB", [NST, 512], F32))
        pC = E(nc.sbuf_tensor("p2pC", [NST, 512], F32))

        pipe = Pipe(kb, "p2", ["y", "t", "s", "v"])

        def ldT(sy, blkid=3):
            return DmaBatch(kb, sy, "p2ld").go(
                [(tz[r][:], cct[blkid, 128 * r:128 * (r + 1), :]) for r in range(8)])
        pipe.step(y=ldT)

        for h in range(2):
            def ztr(tn, h=h):
                m = None
                for mo in range(8):
                    for j in range(4):
                        nc_ = 4 * h + j
                        m = tn.matmul(PS(mo)[:, 128 * j:128 * (j + 1)],
                                      tz[nc_][:, 128 * mo:128 * (mo + 1)],
                                      sb["ident_s"][:], start=True, stop=True)
                return m
            pipe.step(t=ztr)

            def zev(sc, h=h):
                m = None
                for mo in range(8):
                    m = sc.activation(out=zb[0][mo][:, 512 * h:512 * h + 512],
                                      in_=PS(mo), func=AF.Copy)
                return m
            pipe.step(s=zev)

        par = 0
        for stage, blkid in enumerate([2, 1, 0]):
            def ldTs(sy, blkid=blkid):
                return DmaBatch(kb, sy, "p2ld").go(
                    [(tz[r][:], cct[blkid, 128 * r:128 * (r + 1), :]) for r in range(8)])
            pipe.step(y=ldTs)
            last = stage == 2
            for h in range(2):
                def zmm(tn, par=par, h=h):
                    m = None
                    for mo in range(8):
                        for r in range(8):
                            m = tn.matmul(PS(mo),
                                          tz[r][:, 128 * mo:128 * (mo + 1)],
                                          zb[par][r][:, 512 * h:512 * h + 512],
                                          start=(r == 0), stop=(r == 7))
                    return m
                pipe.step(t=zmm)

                def zev2(sc, par=par, h=h, last=last):
                    m = None
                    for mo in range(8):
                        if last:
                            dst = sb["rtot_s"][:, D * mo + 512 * h: D * mo + 512 * h + 512]
                        else:
                            dst = zb[1 - par][mo][:, 512 * h:512 * h + 512]
                        m = sc.activation(out=dst, in_=PS(mo), func=AF.Copy)
                    return m
                pipe.step(s=zev2)
            par = 1 - par

        def ldw1(sy):
            return DmaBatch(kb, sy, "p2ld").go(
                [(wstg[r][:], g["in_wT"][128 * r:128 * (r + 1), :]) for r in range(8)])
        pipe.step(y=ldw1)

        def cw1(sc):
            m = None
            for r in range(8):
                m = sc.activation(out=inwb[r][:], in_=wstg[r][:], func=AF.Copy)
            return m
        pipe.step(s=cw1)

        def ldw2(sy):
            return DmaBatch(kb, sy, "p2ld").go(
                [(wstg[r][:], g["skip_wT"][128 * r:128 * (r + 1), :]) for r in range(8)])
        pipe.step(y=ldw2)

        def cw2(sc):
            m = None
            for r in range(8):
                m = sc.activation(out=skwb[r][:], in_=wstg[r][:], func=AF.Copy)
            return m
        pipe.step(s=cw2)

        def ldw3(sy):
            return DmaBatch(kb, sy, "p2ld").go(
                [(wstg[0][:, 0:DTR], g["wd1T"][0:128, :]),
                 (wstg[1][:, 0:DTR], g["wd1T"][128:256, :]),
                 (wstg[2][:, 0:NST], g["wbT"][0:128, :]),
                 (wstg[3][:, 0:NST], g["wbT"][128:256, :]),
                 (wstg[4][:, 0:NST], g["wcT"][0:128, :]),
                 (wstg[5][:, 0:NST], g["wcT"][128:256, :])])
        pipe.step(y=ldw3)

        def cw3(sc):
            for m in range(2):
                sc.activation(out=wd1b[m][:], in_=wstg[m][:, 0:DTR], func=AF.Copy)
                sc.activation(out=wbb[m][:], in_=wstg[2 + m][:, 0:NST], func=AF.Copy)
                sc.activation(out=wcb[m][:], in_=wstg[4 + m][:, 0:NST], func=AF.Copy)
            return sc.activation(out=wcb[1][:], in_=wstg[5][:, 0:NST], func=AF.Copy)
        pipe.step(s=cw3)

        def zinit(ve):
            ve.memset(zext[0][:, 0:3], 0.0)
            return ve.memset(zext[1][:, 0:3], 0.0)
        pipe.step(v=zinit)

        NSEG = T // 512
        for i in range(NSEG):
            src, col = (512 * i) // TC, (512 * i) % TC

            def ldx(sy, src=src, col=col):
                return DmaBatch(kb, sy, "p2ld").go(
                    [(xg[r][:], ccx[src, 128 * r:128 * (r + 1), col:col + 512])
                     for r in range(8)])
            pipe.step(y=ldx)

            def zmm2(tn):
                m = None
                for mo in range(2):
                    for r in range(8):
                        tn.matmul(PS(mo), inwb[r][:, 128 * mo:128 * (mo + 1)],
                                  xg[r][:], start=(r == 0), stop=(r == 7))
                for mo in range(2):
                    for r in range(8):
                        m = tn.matmul(PS(2 + mo), skwb[r][:, 128 * mo:128 * (mo + 1)],
                                      xg[r][:], start=(r == 0), stop=(r == 7))
                return m
            pipe.step(t=zmm2)

            def zevac(sc):
                for m in range(2):
                    sc.activation(out=zext[m][:, 3:515], in_=PS(m), func=AF.Copy)
                sc.activation(out=ssk[0][:], in_=PS(2), func=AF.Silu)
                return sc.activation(out=ssk[1][:], in_=PS(3), func=AF.Silu)
            pipe.step(s=zevac)

            def conv(ve):
                m = None
                for mm in range(2):
                    ve.tensor_scalar(out=xc[mm][:], in0=zext[mm][:, 3:515],
                                     scalar1=sb["cw_s"][:, 4 * mm + 3:4 * mm + 4],
                                     scalar2=sb["cb_s"][:, mm:mm + 1],
                                     op0=OP.mult, op1=OP.add)
                    for kk in range(3):
                        ve.scalar_tensor_tensor(
                            out=xc[mm][:], in0=zext[mm][:, kk:kk + 512],
                            scalar=sb["cw_s"][:, 4 * mm + kk:4 * mm + kk + 1],
                            in1=xc[mm][:], op0=OP.mult, op1=OP.add)
                    ve.drain()
                    m = ve.tensor_copy(out=zext[mm][:, 0:3], in_=zext[mm][:, 512:515])
                return m
            pipe.step(v=conv)

            def silux(sc):
                for m in range(2):
                    sc.activation(out=xf[m][:], in_=xc[m][:], func=AF.Silu)
                for m in range(2):
                    sc.activation(out=xbf[m][:], in_=xc[m][:], func=AF.Silu)
                return sc.activation(out=xbf[1][:], in_=xc[1][:], func=AF.Silu)
            pipe.step(s=silux)

            def pmm(tn):
                for m in range(2):
                    tn.matmul(PS(4, 512, DTR), wd1b[m][:], xbf[m][:],
                              start=(m == 0), stop=(m == 1))
                for m in range(2):
                    tn.matmul(PS(5, 512, NST), wbb[m][:], xbf[m][:],
                              start=(m == 0), stop=(m == 1))
                m2 = None
                for m in range(2):
                    m2 = tn.matmul(PS(6, 512, NST), wcb[m][:], xbf[m][:],
                                   start=(m == 0), stop=(m == 1))
                return m2
            pipe.step(t=pmm)

            def pev(ve):
                ve.tensor_copy(out=pd1[:], in_=PS(4, 512, DTR))
                ve.tensor_copy(out=pB[:], in_=PS(5, 512, NST))
                return ve.tensor_copy(out=pC[:], in_=PS(6, 512, NST))
            pipe.step(v=pev)

            def stx(sy, i=i):
                c0 = 512 * i
                bcd = g["cc_bcd_in"].ap().rearrange("(r t) -> r t", r=96)
                pairs = [(g["x_dr"][128 * m:128 * (m + 1), c0:c0 + 512], xf[m][:])
                         for m in range(2)]
                pairs += [(g["sskip_dr"][128 * m:128 * (m + 1), c0:c0 + 512], ssk[m][:])
                          for m in range(2)]
                pairs += [(bcd[0:DTR, c0:c0 + 512], pd1[:]),
                          (bcd[DTR:DTR + NST, c0:c0 + 512], pB[:]),
                          (bcd[DTR + NST:96, c0:c0 + 512], pC[:])]
                if t_x is not None:
                    pairs += [(t_x[128 * m:128 * (m + 1), c0:c0 + 512], xf[m][:])
                              for m in range(2)]
                if t_ss is not None:
                    pairs += [(t_ss[128 * m:128 * (m + 1), c0:c0 + 512], ssk[m][:])
                              for m in range(2)]
                return DmaBatch(kb, sy, "p2st").go(pairs)
            pipe.step(y=stx)

        if t_rtot is not None:
            def strt(sy):
                return DmaBatch(kb, sy, "p2st").go([(t_rtot[:], sb["rtot_s"][:])])
            pipe.step(y=strt)

        with nc.Block() as blk:
            pipe.emit(blk, None)


def phase3(kb, g):
    nc = kb.nc
    tap = g["tap"]
    t_bcd = tap("bcd", [96, T])
    with nc.Block() as blk:
        @blk.gpsimd
        def _(gp):
            gp.collective_compute(
                "AllReduce", OP.add, replica_groups=RG,
                ins=[g["cc_bcd_in"][:]], outs=[g["cc_bcd_out"][:]]).then_inc(
                kb.sem("p3g"), 1)
            gp.wait_ge(kb.sem("p3g"), 1)
            if t_bcd is not None:
                tgt = dma(gp, t_bcd[:],
                          g["cc_bcd_out"].ap().rearrange("(r t) -> r t", r=96),
                          kb, "p3st")
                gp.wait_ge(kb.sem("p3st"), tgt)


def phase4(kb, g):
    nc = kb.nc
    sb = g["sb"]
    PS = g["PS"]
    tap = g["tap"]
    t_y = tap("yscan", [LC, T])
    t_gy = tap("gy", [LC, T], BF16)
    t_dl = tap("delta", [LC, T])
    t_d2 = tap("d2raw", [LC, T])
    t_hb = tap("hb", [128, NST * 2 * 512])

    bcd = g["cc_bcd_out"].ap().rearrange("(r t) -> r t", r=96)
    ggy = g["cc_gy_in"].ap().rearrange("(l t) -> l t", l=LC)
    S = 512
    NSEG = T // S

    with ExitStack() as ctx:
        E = ctx.enter_context
        oh16_s = E(nc.sbuf_tensor("p4oh16", [NST, NST * 128], BF16))
        wd2f = E(nc.sbuf_tensor("p4wd2f", [DTR, LC], F32))
        wd2b = E(nc.sbuf_tensor("p4wd2b", [DTR, LC], BF16))
        d1f = E(nc.sbuf_tensor("p4d1f", [DTR, S], F32))
        d1b = E(nc.sbuf_tensor("p4d1b", [DTR, S], BF16))
        btf = E(nc.sbuf_tensor("p4btf", [NST, S], F32))
        btb = E(nc.sbuf_tensor("p4btb", [NST, S], BF16))
        ctf = E(nc.sbuf_tensor("p4ctf", [NST, S], F32))
        ctb = E(nc.sbuf_tensor("p4ctb", [NST, S], BF16))
        dlt = [E(nc.sbuf_tensor(f"p4dl{m}", [128, S], F32)) for m in range(2)]
        sr1 = [E(nc.sbuf_tensor(f"p4sr{m}", [128, S], F32)) for m in range(2)]
        d2r = [E(nc.sbuf_tensor(f"p4d2r{m}", [128, S], F32)) for m in range(2)]
        sa1 = [E(nc.sbuf_tensor(f"p4sa{m}", [128, S], F32)) for m in range(2)]
        dx = [E(nc.sbuf_tensor(f"p4dx{m}", [128, S], F32)) for m in range(2)]
        xw = [E(nc.sbuf_tensor(f"p4xw{m}", [128, S], F32)) for m in range(2)]
        ssk = [E(nc.sbuf_tensor(f"p4ss{m}", [128, S], F32)) for m in range(2)]
        aa = E(nc.sbuf_tensor("p4aa", [128, 8 * S], F32))
        bb = [E(nc.sbuf_tensor(f"p4bb{m}", [128, S], F32)) for m in range(2)]
        hbuf = E(nc.sbuf_tensor("p4hb", [128, NST * 2 * S], F32))
        state = E(nc.sbuf_tensor("p4st", [128, 32], F32))
        yy = [E(nc.sbuf_tensor(f"p4yy{m}", [128, S], F32)) for m in range(2)]
        gyb = [E(nc.sbuf_tensor(f"p4gy{m}", [128, S], BF16)) for m in range(2)]

        hb3 = hbuf.ap().rearrange("p (n q) -> p n q", q=2 * S)
        hb3r = hbuf.ap().rearrange("p (n q) -> p q n", q=2 * S)

        pipe = Pipe(kb, "p4", ["y", "t", "s", "v"])

        def ldw(sy):
            return DmaBatch(kb, sy, "p4ld").go([(wd2f[:], g["wd2T"][:]),
                                                (oh16_s[:], g["oh16"][:])])
        pipe.step(y=ldw)

        def cwz(sc):
            return sc.activation(out=wd2b[:], in_=wd2f[:], func=AF.Copy)
        pipe.step(s=cwz)

        def sinit(ve):
            return ve.memset(state[:], 0.0)
        pipe.step(v=sinit)

        for i in range(NSEG):
            c0 = S * i

            def ld(sy, c0=c0):
                return DmaBatch(kb, sy, "p4ld").go(
                    [(d1f[:], bcd[0:DTR, c0:c0 + S]),
                     (btf[:], bcd[DTR:DTR + NST, c0:c0 + S]),
                     (ctf[:], bcd[DTR + NST:96, c0:c0 + S]),
                     (xw[0][:], g["x_dr"][0:128, c0:c0 + S]),
                     (xw[1][:], g["x_dr"][128:256, c0:c0 + S]),
                     (ssk[0][:], g["sskip_dr"][0:128, c0:c0 + S]),
                     (ssk[1][:], g["sskip_dr"][128:256, c0:c0 + S])])
            pipe.step(y=ld)

            def cst(sc):
                sc.activation(out=d1b[:], in_=d1f[:], func=AF.Copy)
                sc.activation(out=btb[:], in_=btf[:], func=AF.Copy)
                return sc.activation(out=ctb[:], in_=ctf[:], func=AF.Copy)
            pipe.step(s=cst)

            def d2mm(tn):
                m = None
                for et in range(2):
                    m = tn.matmul(PS(et), wd2b[:, 128 * et:128 * (et + 1)],
                                  d1b[:], start=True, stop=True)
                return m
            pipe.step(t=d2mm)

            def spl(sc):
                m = None
                for et in range(2):
                    if t_d2 is not None:
                        sc.activation(out=d2r[et][:], in_=PS(et), func=AF.Copy)
                    sc.activation(out=sr1[et][:], in_=PS(et), func=AF.Relu,
                                  bias=sb["wd2b_s"][:, et:et + 1])
                    sc.activation(out=sa1[et][:], in_=PS(et), func=AF.Abs,
                                  bias=sb["wd2b_s"][:, et:et + 1])
                    sc.activation(out=sa1[et][:], in_=sa1[et][:], func=AF.Exp,
                                  scale=CP(2, 1) if False else sb["cp_s"][:, 2:3])
                    m = sc.activation(out=sa1[et][:], in_=sa1[et][:], func=AF.Ln,
                                      bias=sb["cp_s"][:, 3:4])
                return m
            pipe.step(s=spl)

            def mdx(ve):
                ve.tensor_add(dlt[0][:], sr1[0][:], sa1[0][:])
                ve.tensor_add(dlt[1][:], sr1[1][:], sa1[1][:])
                ve.drain()
                ve.tensor_mul(dx[0][:], dlt[0][:], xw[0][:])
                return ve.tensor_mul(dx[1][:], dlt[1][:], xw[1][:])
            pipe.step(v=mdx)

            for ng in range(4):
                def reps(tn, ng=ng):
                    m = None
                    for j in range(4):
                        n = 4 * ng + j
                        m = tn.matmul(PS(2 * j), oh16_s[:, 128 * n:128 * (n + 1)],
                                      btb[:], start=True, stop=True)
                        m = tn.matmul(PS(2 * j + 1), oh16_s[:, 128 * n:128 * (n + 1)],
                                      ctb[:], start=True, stop=True)
                    return m
                pipe.step(t=reps)

                def aexp(sc, ng=ng):
                    m = None
                    for j in range(4):
                        n = 4 * ng + j
                        for et in range(2):
                            m = sc.activation(
                                out=aa[:, (2 * j + et) * S:(2 * j + et + 1) * S],
                                in_=dlt[et][:], func=AF.Exp,
                                scale=sb["acoef_s"][:, 16 * et + n:16 * et + n + 1])
                    return m
                pipe.step(s=aexp)

                def scans(ve, ng=ng):
                    for j in range(4):
                        n = 4 * ng + j
                        for et in range(2):
                            hcol = n * 2 * S + et * S
                            ve.tensor_mul(bb[et][:], dx[et][:], PS(2 * j))
                            ve.tensor_tensor_scan(
                                out=hbuf[:, hcol:hcol + S],
                                data0=aa[:, (2 * j + et) * S:(2 * j + et + 1) * S],
                                data1=bb[et][:],
                                initial=state[:, 16 * et + n:16 * et + n + 1],
                                op0=OP.mult, op1=OP.add)
                    ve.drain()
                    for et in range(2):
                        ve.tensor_copy(
                            out=state[:, 16 * et + 4 * ng:16 * et + 4 * ng + 4],
                            in_=hb3[:, 4 * ng:4 * ng + 4, et * S + S - 1:et * S + S])
                    ve.drain()
                    m = None
                    for j in range(4):
                        n = 4 * ng + j
                        for et in range(2):
                            hcol = n * 2 * S + et * S
                            m = ve.tensor_tensor(
                                out=hbuf[:, hcol:hcol + S],
                                in0=hbuf[:, hcol:hcol + S],
                                in1=PS(2 * j + 1), op=OP.mult)
                    return m
                pipe.step(v=scans)


            def redu(ve):
                m = None
                for et in range(2):
                    m = ve.reduce_sum(out=yy[et][:],
                                      in_=hb3r[:, et * S:(et + 1) * S, :],
                                      axis=AX.X)
                return m
            pipe.step(v=redu)

            def gate(ve):
                m = None
                for et in range(2):
                    ve.scalar_tensor_tensor(out=yy[et][:], in0=xw[et][:],
                                            scalar=sb["wdt_s"][:, et:et + 1],
                                            in1=yy[et][:], op0=OP.mult, op1=OP.add)
                    m = ve.tensor_mul(gyb[et][:], yy[et][:], ssk[et][:])
                return m
            pipe.step(v=gate)

            def stg(sy, i=i, c0=c0):
                pairs = [(ggy[128 * et:128 * (et + 1), c0:c0 + S], gyb[et][:])
                         for et in range(2)]
                if t_dl is not None:
                    pairs += [(t_dl[128 * et:128 * (et + 1), c0:c0 + S], dlt[et][:])
                              for et in range(2)]
                if t_d2 is not None:
                    pairs += [(t_d2[128 * et:128 * (et + 1), c0:c0 + S], d2r[et][:])
                              for et in range(2)]
                if t_hb is not None and i == 0:
                    pairs += [(t_hb[:], hbuf[:])]
                if t_y is not None:
                    pairs += [(t_y[128 * et:128 * (et + 1), c0:c0 + S], yy[et][:])
                              for et in range(2)]
                if t_gy is not None:
                    pairs += [(t_gy[128 * et:128 * (et + 1), c0:c0 + S], gyb[et][:])
                              for et in range(2)]
                return DmaBatch(kb, sy, "p4st").go(pairs)
            pipe.step(y=stg)

        with nc.Block() as blk:
            pipe.emit(blk, None)

        with nc.Block() as blk2:
            @blk2.gpsimd
            def _(gp):
                gp.collective_compute(
                    "AllGather", OP.bypass, replica_groups=RG,
                    ins=[g["cc_gy_in"][:]], outs=[g["cc_gy_out"][:]]).then_inc(
                    kb.sem("aggy"), 1)
                gp.wait_ge(kb.sem("aggy"), 1)
                v = gp.value_load(sb["tok0_s"][0:1, 0:1])
                gg = g["cc_gy_out"].ap().rearrange("(r t) -> r t", r=NCORES * LC)
                tgt = None
                for r in range(16):
                    tgt = dma(gp, g["gyw_dr"][128 * r:128 * (r + 1), :],
                              gg[128 * r:128 * (r + 1), bass.ds(v, TC)], kb, "gyw")
                gp.wait_ge(kb.sem("gyw"), tgt)


def phase_ffn(kb, g):
    nc = kb.nc
    sb = g["sb"]
    PS = g["PS"]
    tap = g["tap"]
    CP = g["CP"]
    t_abel = tap("abel", [D, TC], BF16)
    t_nona = tap("nona", [D, TC], BF16)
    t_attn = tap("attn", [D, TC])


    with ExitStack() as ctx:
        E = ctx.enter_context
        xnfb = [E(nc.sbuf_tensor(f"pfxn{r}", [128, 1024], BF16)) for r in range(8)]
        wkf = [E(nc.sbuf_tensor(f"pfwf{r}", [128, 128], F32)) for r in range(16)]
        wkb = [E(nc.sbuf_tensor(f"pfwb{r}", [128, 128], BF16)) for r in range(16)]
        he = [E(nc.sbuf_tensor(f"pfhe{r}", [128, 1024], BF16)) for r in range(16)]
        moe = [E(nc.sbuf_tensor(f"pfmo{r}", [128, 1024], F32)) for r in range(8)]
        nonab = [E(nc.sbuf_tensor(f"pfnb{r}", [128, 1024], BF16)) for r in range(8)]
        abelb = [E(nc.sbuf_tensor(f"pfab{r}", [128, 1024], BF16)) for r in range(8)]
        gyr = [E(nc.sbuf_tensor(f"pfgy{r}", [128, 1024], BF16)) for r in range(16)]
        basef = [E(nc.sbuf_tensor(f"pfbs{r}", [128, 1024], F32)) for r in range(2)]
        wrepb = E(nc.sbuf_tensor("pfwr", [128, 1024], BF16))
        eb2b = E(nc.sbuf_tensor("pfeb2", [NEXP, D], BF16))
        gcwb = E(nc.sbuf_tensor("pfgcw", [128, 16], BF16))
        grow = E(nc.sbuf_tensor("pfgrow", [1, 1024], F32))
        ghi = E(nc.sbuf_tensor("pfghi", [1, 1024], BF16))
        glo = E(nc.sbuf_tensor("pfglo", [1, 1024], F32))
        glob = E(nc.sbuf_tensor("pfglob", [1, 1024], BF16))

        pipe = Pipe(kb, "pf", ["y", "t", "s", "v"])

        def ldsm(sy):
            return DmaBatch(kb, sy, "pfld").go([(basef[0][0:NEXP, :], g["eb2_n"][:])])
        pipe.step(y=ldsm)

        def cssm(sc):
            return sc.activation(out=eb2b[:], in_=basef[0][0:NEXP, :], func=AF.Copy)
        pipe.step(s=cssm)

        def ldsm2(sy):
            return DmaBatch(kb, sy, "pfld").go([(basef[0][0:128, 0:16], g["gcw_t"][:])])
        pipe.step(y=ldsm2)

        def cssm2(sc):
            return sc.activation(out=gcwb[:], in_=basef[0][:, 0:16], func=AF.Copy)
        pipe.step(s=cssm2)

        for H in range(2):
            hc0 = 1024 * H

            def ldxn(sy, hc0=hc0):
                return DmaBatch(kb, sy, "pfld").go(
                    [(xnfb[r][:], g["xnf_dr"][128 * r:128 * (r + 1), hc0:hc0 + 1024])
                     for r in range(8)])
            pipe.step(y=ldxn)

            for e in range(NEXP):
                def wrep(tn, e=e, hc0=hc0):
                    m = None
                    for tt in range(2):
                        m = tn.matmul(PS(6 + tt), sb["oh8_s"][:, 128 * e:128 * (e + 1)],
                                      sb["w6_s"][:, hc0 + 512 * tt:hc0 + 512 * (tt + 1)],
                                      start=True, stop=True)
                    return m
                pipe.step(t=wrep)

                def wrev(sc):
                    sc.activation(out=wrepb[:, 0:512], in_=PS(6), func=AF.Copy)
                    return sc.activation(out=wrepb[:, 512:1024], in_=PS(7), func=AF.Copy)
                pipe.step(s=wrev)

                for mt in range(16):
                    def lde1(sy, e=e, mt=mt):
                        return DmaBatch(kb, sy, "pfld").go(
                            [(wkf[r][:], g["ew1"][e, 128 * r:128 * (r + 1),
                                                  128 * mt:128 * (mt + 1)])
                             for r in range(8)])
                    pipe.step(y=lde1)

                    def cse1(sc):
                        m = None
                        for r in range(8):
                            m = sc.activation(out=wkb[r][:], in_=wkf[r][:], func=AF.Copy)
                        return m
                    pipe.step(s=cse1)

                    def mme1(tn):
                        m = None
                        for tt in range(2):
                            for r in range(8):
                                m = tn.matmul(PS(tt), wkb[r][:], xnfb[r][:, 512 * tt:512 * (tt + 1)],
                                              start=(r == 0), stop=(r == 7))
                        return m
                    pipe.step(t=mme1)

                    def gev(sc, e=e, mt=mt):
                        m = None
                        for tt in range(2):
                            m = sc.activation(out=he[mt][:, 512 * tt:512 * (tt + 1)],
                                              in_=PS(tt), func=AF.Gelu,
                                              bias=sb["eb1_s"][:, 16 * e + mt:16 * e + mt + 1])
                        return m
                    pipe.step(s=gev)

                def wsc(ve):
                    m = None
                    for mt in range(16):
                        m = ve.tensor_mul(he[mt][:], he[mt][:], wrepb[:])
                    return m
                pipe.step(v=wsc)

                for mo in range(8):
                    def lde2(sy, e=e, mo=mo):
                        return DmaBatch(kb, sy, "pfld").go(
                            [(wkf[r][:], g["ew2"][e, 128 * r:128 * (r + 1),
                                                  128 * mo:128 * (mo + 1)])
                             for r in range(16)])
                    pipe.step(y=lde2)

                    def cse2(sc):
                        m = None
                        for r in range(16):
                            m = sc.activation(out=wkb[r][:], in_=wkf[r][:], func=AF.Copy)
                        return m
                    pipe.step(s=cse2)

                    def mme2(tn, e=e, mo=mo, hc0=hc0):
                        m = None
                        for tt in range(2):
                            for r in range(16):
                                m = tn.matmul(PS(2 + tt), wkb[r][:],
                                              he[r][:, 512 * tt:512 * (tt + 1)],
                                              start=(r == 0),
                                              stop=(r == 15 and e != 0))
                            if e == 0:
                                m = tn.matmul(
                                    PS(2 + tt), eb2b[0:NEXP, 128 * mo:128 * (mo + 1)],
                                    sb["w6_s"][0:NEXP, hc0 + 512 * tt:hc0 + 512 * (tt + 1)],
                                    start=False, stop=True)
                        return m
                    pipe.step(t=mme2)

                    def acc2(ve, e=e, mo=mo):
                        for tt in range(2):
                            if e == 0:
                                ve.tensor_copy(out=moe[mo][:, 512 * tt:512 * (tt + 1)],
                                               in_=PS(2 + tt))
                            else:
                                ve.tensor_add(moe[mo][:, 512 * tt:512 * (tt + 1)],
                                              moe[mo][:, 512 * tt:512 * (tt + 1)],
                                              PS(2 + tt))
                        return None
                    pipe.step(v=acc2)

            def csmoe(sc):
                m = None
                for mo in range(8):
                    m = sc.activation(out=gyr[mo][:], in_=moe[mo][:], func=AF.Copy)
                return m
            pipe.step(s=csmoe)

            for mo in range(8):
                def mmna(tn, mo=mo):
                    m = None
                    for tt in range(2):
                        for r in range(8):
                            m = tn.matmul(PS(2 + tt),
                                          sb["rtot_s"][:, D * r + 128 * mo:D * r + 128 * (mo + 1)],
                                          gyr[r][:, 512 * tt:512 * (tt + 1)],
                                          start=(r == 0), stop=(r == 7))
                    return m
                pipe.step(t=mmna)

                def evna(sc, mo=mo):
                    sc.activation(out=nonab[mo][:, 0:512], in_=PS(2), func=AF.Copy)
                    return sc.activation(out=nonab[mo][:, 512:1024], in_=PS(3), func=AF.Copy)
                pipe.step(s=evna)

            for q in range(2):
                qc = 512 * q
                for mt in range(32):
                    h1q = he[mt % 16][:, 512 * (mt // 16):512 * (mt // 16) + 512]

                    def lda1(sy, mt=mt):
                        return DmaBatch(kb, sy, "pfld").go(
                            [(wkf[r][:], g["aw1"][128 * r:128 * (r + 1),
                                                  128 * mt:128 * (mt + 1)])
                             for r in range(8)])
                    pipe.step(y=lda1)

                    def csa1(sc):
                        m = None
                        for r in range(8):
                            m = sc.activation(out=wkb[r][:], in_=wkf[r][:], func=AF.Copy)
                        return m
                    pipe.step(s=csa1)

                    def mma1(tn, qc=qc):
                        m = None
                        for r in range(8):
                            m = tn.matmul(PS(0), wkb[r][:], xnfb[r][:, qc:qc + 512],
                                          start=(r == 0), stop=(r == 7))
                        return m
                    pipe.step(t=mma1)

                    def gev1(sc, mt=mt, h1q=h1q):
                        return sc.activation(out=h1q, in_=PS(0), func=AF.Gelu,
                                             bias=sb["ab1_s"][:, mt:mt + 1])
                    pipe.step(s=gev1)

                for mo in range(8):
                    def lda2(sy, mo=mo):
                        return DmaBatch(kb, sy, "pfld").go(
                            [(wkf[r][:], g["aw2"][128 * r:128 * (r + 1),
                                                  128 * mo:128 * (mo + 1)])
                             for r in range(16)])
                    pipe.step(y=lda2)

                    def csa2(sc):
                        m = None
                        for r in range(16):
                            m = sc.activation(out=wkb[r][:], in_=wkf[r][:], func=AF.Copy)
                        return m
                    pipe.step(s=csa2)

                    def lda2b(sy, mo=mo):
                        return DmaBatch(kb, sy, "pfld").go(
                            [(wkf[r][:], g["aw2"][128 * (16 + r):128 * (17 + r),
                                                  128 * mo:128 * (mo + 1)])
                             for r in range(16)])
                    pipe.step(y=lda2b)

                    def mma2(tn):
                        m = None
                        for r in range(16):
                            m = tn.matmul(PS(2), wkb[r][:],
                                          he[r % 16][:, 512 * (r // 16):512 * (r // 16) + 512],
                                          start=(r == 0), stop=False)
                        return m
                    pipe.step(t=mma2)

                    def csa2b(sc):
                        m = None
                        for r in range(16):
                            m = sc.activation(out=wkb[r][:], in_=wkf[r][:], func=AF.Copy)
                        return m
                    pipe.step(s=csa2b)

                    def mma2b(tn):
                        m = None
                        for r in range(16):
                            m = tn.matmul(PS(2), wkb[r][:],
                                          he[(16 + r) % 16][:, 512 * ((16 + r) // 16):512 * ((16 + r) // 16) + 512],
                                          start=False, stop=(r == 15))
                        return m
                    pipe.step(t=mma2b)

                    def eva2(sc, mo=mo, qc=qc):
                        return sc.activation(out=abelb[mo][:, qc:qc + 512], in_=PS(2),
                                             func=AF.Identity,
                                             bias=sb["ab2_s"][:, mo:mo + 1])
                    pipe.step(s=eva2)

            def gcm(tn, hc0=hc0):
                m = None
                for tt in range(2):
                    for mo in range(8):
                        tn.matmul(PS(4 + tt, 512, 1), gcwb[:, mo:mo + 1],
                                  abelb[mo][:, 512 * tt:512 * (tt + 1)],
                                  start=(mo == 0), stop=False)
                    for mo in range(8):
                        m = tn.matmul(PS(4 + tt, 512, 1), gcwb[:, 8 + mo:9 + mo],
                                      nonab[mo][:, 512 * tt:512 * (tt + 1)],
                                      start=False, stop=(mo == 7))
                return m
            pipe.step(t=gcm)

            def gsig(sc):
                sc.activation(out=grow[0:1, 0:512], in_=PS(4, 512, 1), func=AF.Sigmoid,
                              bias=sb_gcb(sb))
                sc.activation(out=grow[0:1, 512:1024], in_=PS(5, 512, 1), func=AF.Sigmoid,
                              bias=sb_gcb(sb))
                return sc.activation(out=ghi[:], in_=grow[:], func=AF.Copy)
            pipe.step(s=gsig)

            def gl(ve):
                return ve.tensor_sub(glo[:], grow[:], ghi[:])
            pipe.step(v=gl)

            def glc(sc):
                return sc.activation(out=glob[:], in_=glo[:], func=AF.Copy)
            pipe.step(s=glc)

            def gbc(tn):
                m = None
                for tt in range(2):
                    tn.matmul(PS(4 + tt), sb["ones1r_s"][:], ghi[0:1, 512 * tt:512 * (tt + 1)],
                              start=True, stop=False)
                    m = tn.matmul(PS(4 + tt), sb["ones1r_s"][:], glob[0:1, 512 * tt:512 * (tt + 1)],
                                  start=False, stop=True)
                return m
            pipe.step(t=gbc)

            def mixv(ve):
                m = None
                for mo in range(8):
                    ve.tensor_copy(out=moe[mo][:], in_=abelb[mo][:])
                    ve.tensor_copy(out=basef[0][:], in_=nonab[mo][:])
                    ve.drain()
                    ve.tensor_sub(basef[0][:], basef[0][:], moe[mo][:])
                    for tt in range(2):
                        ve.tensor_mul(basef[0][:, 512 * tt:512 * (tt + 1)],
                                      basef[0][:, 512 * tt:512 * (tt + 1)], PS(4 + tt))
                    ve.drain()
                    m = ve.tensor_add(moe[mo][:], moe[mo][:], basef[0][:])
                return m
            pipe.step(v=mixv)

            for r in range(16):
                def ldor(sy, r=r):
                    return DmaBatch(kb, sy, "pfld").go(
                        [(basef[0][:], g["out_wT"][128 * r:128 * (r + 1), :])])
                pipe.step(y=ldor)

                def csor(sc, r=r):
                    return sc.activation(out=he[r][:], in_=basef[0][:], func=AF.Copy)
                pipe.step(s=csor)

            def ldgy(sy, hc0=hc0):
                pairs = [(gyr[r][:],
                          g["gyw_dr"][128 * r:128 * (r + 1), hc0:hc0 + 1024])
                         for r in range(16)]
                return DmaBatch(kb, sy, "pfld").go(pairs)
            pipe.step(y=ldgy)

            for mo in range(0, 8, 2):
                def ldbs(sy, mo=mo, hc0=hc0):
                    return DmaBatch(kb, sy, "pfld").go(
                        [(basef[j][:], g["base_dr"][128 * (mo + j):128 * (mo + j + 1),
                                                    hc0:hc0 + 1024]) for j in range(2)])
                pipe.step(y=ldbs)

                def mmat(tn, mo=mo):
                    m = None
                    for j in range(2):
                        for tt in range(2):
                            for r in range(16):
                                m = tn.matmul(PS(2 * j + tt), he[r][:, 128 * (mo + j):128 * (mo + j + 1)],
                                              gyr[r][:, 512 * tt:512 * (tt + 1)],
                                              start=(r == 0), stop=(r == 15))
                    return m
                pipe.step(t=mmat)

                def fsum(ve, mo=mo):
                    m = None
                    for j in range(2):
                        for tt in range(2):
                            ve.tensor_add(basef[j][:, 512 * tt:512 * (tt + 1)],
                                          basef[j][:, 512 * tt:512 * (tt + 1)],
                                          PS(2 * j + tt))
                        ve.drain()
                        m = ve.tensor_add(basef[j][:], basef[j][:], moe[mo + j][:])
                    return m
                pipe.step(v=fsum)

                def stout(sy, mo=mo, hc0=hc0):
                    pairs = [(g["outT"][128 * (mo + j):128 * (mo + j + 1), hc0:hc0 + 1024],
                              basef[j][:]) for j in range(2)]
                    if t_attn is not None:
                        pass
                    return DmaBatch(kb, sy, "pfst").go(pairs)
                pipe.step(y=stout)

            if t_abel is not None:
                def stab(sy, hc0=hc0):
                    return DmaBatch(kb, sy, "pfst").go(
                        [(t_abel[128 * mo:128 * (mo + 1), hc0:hc0 + 1024], abelb[mo][:])
                         for mo in range(8)])
                pipe.step(y=stab)
            if t_nona is not None:
                def stna(sy, hc0=hc0):
                    return DmaBatch(kb, sy, "pfst").go(
                        [(t_nona[128 * mo:128 * (mo + 1), hc0:hc0 + 1024], nonab[mo][:])
                         for mo in range(8)])
                pipe.step(y=stna)

        with nc.Block() as blk:
            pipe.emit(blk, None)


def sb_gcb(sb):
    return sb["gcb_s"][0:1, 0:1]


def _stage_inputs(i):
    import ml_dtypes
    bf = ml_dtypes.bfloat16
    resid = i["L0_attn_resid"][0]
    cores = []
    eye128 = np.eye(128, dtype=bf)
    id8f = np.eye(8, dtype=np.float32)
    oh16 = np.zeros((NST, NST * 128), dtype=bf)
    for n in range(NST):
        oh16[n, 128 * n:128 * (n + 1)] = 1
    oh8 = np.zeros((8, 8 * 128), dtype=bf)
    for n in range(8):
        oh8[n, 128 * n:128 * (n + 1)] = 1
    ones1 = np.ones((128, 1), dtype=bf)
    ones1r = np.ones((1, 128), dtype=bf)
    cvals = [1e-5, 1.0 / D, -1.0, 1.0] + [1.0 / (k + 1) for k in range(20)] + [0.0] * 8
    cpool = np.tile(np.array(cvals, np.float32)[None, :], (128, 1))

    def pack_cols(v, ncol):
        return np.ascontiguousarray(v.reshape(ncol, 128).T)

    aT = lambda x: np.ascontiguousarray(np.asarray(x).T)
    C = np.ascontiguousarray
    for c in range(NCORES):
        tsl = slice(TC * c, TC * (c + 1))
        lsl = slice(LC * c, LC * (c + 1))
        m = {
            "residT": aT(resid[tsl]),
            "norm_w_t": pack_cols(i["L0_attn_norm_w"], 8),
            "ln_g_t": pack_cols(i["L0_ffn_ln_g"], 8),
            "ln_b_t": pack_cols(i["L0_ffn_ln_b"], 8),
            "gate_w_t": C(i["L0_ffn_gate_w"].reshape(8, 128, NEXP).transpose(1, 0, 2)
                          .reshape(128, 8 * NEXP)),
            "gate_b_t": C(i["L0_ffn_gate_b"].reshape(NEXP, 1)),
            "in_wT": aT(i["L0_attn_in_w"][lsl]),
            "skip_wT": aT(i["L0_attn_skip_w"][lsl]),
            "wd1T": aT(i["L0_attn_wd1"][:, lsl]),
            "wbT": aT(i["L0_attn_wb"][:, lsl]),
            "wcT": aT(i["L0_attn_wc"][:, lsl]),
            "wd2T": aT(i["L0_attn_wd2"][lsl]),
            "a_log_t": C(i["L0_attn_A_log"][lsl].reshape(2, 128, NST)
                         .transpose(1, 0, 2).reshape(128, 2 * NST)),
            "conv_w_t": C(i["L0_attn_conv_w"][lsl].reshape(2, 128, 4)
                          .transpose(1, 0, 2).reshape(128, 8)),
            "conv_b_t": pack_cols(i["L0_attn_conv_b"][lsl], 2),
            "wd2_b_t": pack_cols(i["L0_attn_wd2_b"][lsl], 2),
            "w_d_t": pack_cols(i["L0_attn_W_D"][lsl], 2),
            "out_wT": aT(i["L0_attn_out_w"]),
            "aw1": C(i["L0_ffn_aw1"]),
            "ab1_t": pack_cols(i["L0_ffn_ab1"], 32),
            "aw2": C(i["L0_ffn_aw2"]),
            "ab2_t": pack_cols(i["L0_ffn_ab2"], 8),
            "ew1": C(i["L0_ffn_ew1"]),
            "eb1_t": C(i["L0_ffn_eb1"].reshape(NEXP, 16, 128)
                       .transpose(2, 0, 1).reshape(128, NEXP * 16)),
            "ew2": C(i["L0_ffn_ew2"]),
            "eb2_n": C(i["L0_ffn_eb2"]),
            "gens_n": C(i["L0_ffn_gens"][c % 4]),
            "gcw_t": pack_cols(i["L0_ffn_gc_w"][:, 0], 16),
            "gc_b_t": C(i["L0_ffn_gc_b"].reshape(1, 1)),
            "ident": eye128, "id8f": id8f, "oh16": oh16, "oh8": oh8,
            "ones1": ones1, "ones1r": ones1r,
            "tok0": np.array([[TC * c]], np.int32),
            "cpool": cpool,
        }
        cores.append(m)
    return cores


_BUILT = {}


def kernel(**inputs):
    taps = tuple(TAPS)
    nc, tap_handles = build(taps)
    in_maps = _stage_inputs(inputs)
    res = run_bass_kernel_spmd(nc, in_maps, core_ids=list(range(NCORES)))
    outs = [np.asarray(r["outT"]) for r in res.results]
    full = np.concatenate([o.T for o in outs], axis=0)[None]
    if taps:
        kernel.tap_results = [{k: np.asarray(r["tap_" + k]) for k in taps}
                              for r in res.results]
    return np.ascontiguousarray(full, dtype=np.float32)


# version 24
# speedup vs baseline: 1.4828x; 1.4828x over previous
import numpy as np
from contextlib import ExitStack

import concourse.bass as bass
import concourse.mybir as mybir
from concourse.bass_utils import run_bass_kernel_spmd

F32 = mybir.dt.float32
BF16 = mybir.dt.bfloat16
I32 = mybir.dt.int32
OP = mybir.AluOpType
AF = mybir.ActivationFunctionType
AX = mybir.AxisListType

NCORES = 8
T = 16384
TC = T // NCORES
D = 1024
E2 = 2048
LC = E2 // NCORES
NST = 16
DTR = 64
A4 = 4096
H2 = 2048
NEXP = 6
NTERMS = 20
RG = [list(range(NCORES))]
NEG = -1e30

TAPS = ()


class Pipe:

    def __init__(self, kb, name, engines, leads=None):
        self.kb = kb
        self.name = name
        self.engines = list(engines)
        self.leads = leads or {}
        self.steps = []

    def step(self, **eng_fns):
        for e in eng_fns:
            assert e in self.engines, (self.name, e)
        self.steps.append(eng_fns)

    def emit(self, block, eng_objs):
        kb = self.kb
        nsteps = len(self.steps)
        done = {}
        for f in self.engines:
            c, arr = 0, []
            for s in range(nsteps):
                if f in self.steps[s]:
                    c += 1
                arr.append(c)
            done[f] = arr
        sems = {f: kb.sem(f"{self.name}_{f}") for f in self.engines}

        def body_for(e):
            def body(eng):
                my = 0
                for s in range(nsteps):
                    fn = self.steps[s].get(e)
                    if fn is None:
                        continue
                    for f in self.engines:
                        if f == e:
                            continue
                        j = s - 1 - self.leads.get((e, f), 0)
                        tgt = done[f][j] if j >= 0 else 0
                        if tgt > 0:
                            eng.wait_ge(sems[f], tgt)
                    last = fn(eng)
                    my += 1
                    if e in ("v", "s"):
                        eng.drain().then_inc(sems[e], 1)
                    elif last is not None:
                        last.then_inc(sems[e], 1)
                    else:
                        eng.sem_inc(sems[e], 1)
            return body

        for e in self.engines:
            getattr(block, {"t": "tensor", "v": "vector", "s": "scalar",
                            "y": "sync", "g": "gpsimd"}[e])(body_for(e))


class KB:
    def __init__(self, nc, ctx):
        self.nc = nc
        self.ctx = ctx
        self.sems = {}

    def sem(self, name):
        if name not in self.sems:
            self.sems[name] = self.ctx.enter_context(self.nc.semaphore(name))
        return self.sems[name]


def dma(eng, out, in_, kb, tag):
    sem = kb.sem(tag)
    n = kb.__dict__.setdefault("_dmacnt", {})
    n[tag] = n.get(tag, 0) + 16
    eng.dma_start(out=out, in_=in_).then_inc(sem, 16)
    return n[tag]


class DmaBatch:

    def __init__(self, kb, eng, tag):
        self.kb, self.eng, self.tag = kb, eng, tag

    def go(self, pairs):
        tgt = None
        for out, in_ in pairs:
            tgt = dma(self.eng, out, in_, self.kb, self.tag)
        self.eng.wait_ge(self.kb.sem(self.tag), tgt)
        return None


def build(taps=()):
    nc = bass.Bass("TRN2", target_bir_lowering=False)
    ctx = ExitStack()
    kb = KB(nc, ctx)
    g = {}

    def inp(name, shape, dtype=F32):
        g[name] = nc.dram_tensor(name, shape, dtype, kind="ExternalInput")

    inp("residT", [D, TC]); inp("norm_w_t", [128, 8]); inp("ln_g_t", [128, 8])
    inp("ln_b_t", [128, 8]); inp("gate_w_t", [128, 8 * NEXP]); inp("gate_b_t", [NEXP, 1])
    inp("in_wT", [D, LC]); inp("skip_wT", [D, LC]); inp("wd1T", [LC, DTR])
    inp("wbT", [LC, NST]); inp("wcT", [LC, NST]); inp("wd2T", [DTR, LC])
    inp("a_log_t", [128, 2 * NST]); inp("conv_w_t", [128, 2 * 4])
    inp("conv_b_t", [128, 2]); inp("wd2_b_t", [128, 2]); inp("w_d_t", [128, 2])
    inp("out_wT", [E2, D]); inp("aw1", [D, A4]); inp("ab1_t", [128, 32])
    inp("aw2", [A4, D]); inp("ab2_t", [128, 8]); inp("ew1", [NEXP, D, H2])
    inp("eb1_t", [128, NEXP * 16]); inp("ew2", [NEXP, H2, D]); inp("eb2_n", [NEXP, D])
    inp("gens_n", [D, D]); inp("gcw_t", [128, 16]); inp("gc_b_t", [1, 1])
    inp("ident", [128, 128], BF16); inp("id8f", [8, 8], F32)
    inp("oh16", [NST, NST * 128], BF16); inp("oh8", [8, 8 * 128], BF16)
    inp("ones1", [128, 1], BF16); inp("ones1r", [1, 128], BF16)
    inp("tok0", [1, 1], I32); inp("cpool", [128, 32], F32)

    g["outT"] = nc.dram_tensor("outT", [D, TC], F32, kind="ExternalOutput")

    def intern(name, shape, dtype):
        g[name] = nc.dram_tensor(name, shape, dtype, kind="Internal")

    intern("cc_xn2_in", [D * TC], BF16); intern("cc_xn2_out", [NCORES * D * TC], BF16)
    intern("cc_bcd_in", [96 * T], F32); intern("cc_bcd_out", [96 * T], F32)
    intern("cc_gy_in", [LC * T], BF16); intern("cc_gy_out", [NCORES * LC * T], BF16)
    intern("cc_t_in", [D * D], BF16); intern("cc_t_out", [NCORES * D * D], BF16)
    intern("base_dr", [D, TC], F32); intern("xnf_dr", [D, TC], BF16)
    intern("gyw_dr", [E2, TC], BF16)
    intern("bt_dr", [NST, T], BF16); intern("ct_dr", [NST, T], BF16)
    intern("d1b_dr", [DTR, T], BF16)
    intern("x_dr", [LC, T], F32); intern("sskip_dr", [LC, T], F32)
    intern("abel_dr", [D, TC], BF16)

    taps = tuple(taps)
    tap_handles = {}

    def tap(name, shape, dtype=F32):
        if name in taps:
            tap_handles[name] = nc.dram_tensor("tap_" + name, shape, dtype,
                                               kind="ExternalOutput")
            return tap_handles[name]
        return None

    with ctx:
        E = ctx.enter_context
        sb = {}

        def sbuf(name, shape, dtype=F32):
            sb[name] = E(nc.sbuf_tensor(name, shape, dtype))
            return sb[name]

        sbuf("ident_s", [128, 128], BF16); sbuf("id8f_s", [8, 8], F32)
        sbuf("ones1_s", [128, 1], BF16); sbuf("ones1r_s", [1, 128], BF16)
        sbuf("oh8_s", [8, 8 * 128], BF16)
        sbuf("nrm_s", [128, 8]); sbuf("lng_s", [128, 8]); sbuf("lnb_s", [128, 8])
        sbuf("gw_s", [128, 8 * NEXP]); sbuf("gb_s", [NEXP, 1])
        sbuf("w6_s", [8, TC], BF16)
        sbuf("alog_s", [128, 2 * NST]); sbuf("acoef_s", [128, 2 * NST])
        sbuf("cw_s", [128, 8]); sbuf("cb_s", [128, 2]); sbuf("wd2b_s", [128, 2])
        sbuf("wdt_s", [128, 2]); sbuf("tok0_s", [1, 1], I32)
        sbuf("cp_s", [128, 32]); sbuf("gcb_s", [1, 1])
        sbuf("eb1_s", [128, NEXP * 16]); sbuf("ab1_s", [128, 32]); sbuf("ab2_s", [128, 8])

        def CP(col, p=128):
            return sb["cp_s"][0:p, col:col + 1]
        g["CP"] = CP
        psum = E(nc.psum_tensor("psum", [128, 4096], F32))

        def PS(bank, n=512, p=128):
            return psum[0:p, 512 * bank: 512 * bank + n]

        g["PS"] = PS
        g["psum"] = psum
        g["sb"] = sb
        g["tap"] = tap

        with nc.Block() as blk:
            @blk.sync
            def _(sy):
                d = DmaBatch(kb, sy, "c0")
                d.go([(sb["ident_s"][:], g["ident"][:]), (sb["id8f_s"][:], g["id8f"][:]),
                      (sb["ones1_s"][:], g["ones1"][:]), (sb["ones1r_s"][:], g["ones1r"][:]),
                      (sb["oh8_s"][:], g["oh8"][:]),
                      (sb["nrm_s"][:], g["norm_w_t"][:]), (sb["lng_s"][:], g["ln_g_t"][:]),
                      (sb["lnb_s"][:], g["ln_b_t"][:]), (sb["gw_s"][:], g["gate_w_t"][:]),
                      (sb["gb_s"][:], g["gate_b_t"][:]), (sb["alog_s"][:], g["a_log_t"][:]),
                      (sb["cw_s"][:], g["conv_w_t"][:]), (sb["cb_s"][:], g["conv_b_t"][:]),
                      (sb["wd2b_s"][:], g["wd2_b_t"][:]), (sb["wdt_s"][:], g["w_d_t"][:]),
                      (sb["tok0_s"][:], g["tok0"][:]),
                      (sb["cp_s"][:], g["cpool"][:]),
                      (sb["gcb_s"][:], g["gc_b_t"][:]),
                      (sb["eb1_s"][:], g["eb1_t"][:]), (sb["ab1_s"][:], g["ab1_t"][:]),
                      (sb["ab2_s"][:], g["ab2_t"][:])])

            @blk.scalar
            def _(sc):
                sc.wait_ge(kb.sem("c0"), 17 * 16)
                sc.activation(out=sb["acoef_s"][:], in_=sb["alog_s"][:], func=AF.Exp)
                sc.activation(out=sb["acoef_s"][:], in_=sb["acoef_s"][:],
                              func=AF.Copy, scale=sb["cp_s"][:, 2:3])

        phase0(kb, g)
        phase_expm(kb, g)
        phase2(kb, g)
        phase3(kb, g)
        phase4(kb, g)
        phase_ffn(kb, g)

    return nc, tap_handles


def phase0(kb, g):
    nc = kb.nc
    sb = g["sb"]
    PS = g["PS"]
    tap = g["tap"]
    SP = 512
    NCH = TC // SP
    CP = g["CP"]
    t_xn = tap("xn", [D, TC]); t_base = tap("base", [D, TC])
    t_xn2 = tap("xn2", [D, TC], BF16); t_xnf = tap("xnf", [D, TC], BF16)
    t_logits = tap("logits", [8, TC]); t_w6 = tap("w6", [8, TC], BF16)

    with ExitStack() as ctx:
        E = ctx.enter_context
        rt = [E(nc.sbuf_tensor(f"p0rt{k}", [128, SP], F32)) for k in range(8)]
        xn = [E(nc.sbuf_tensor(f"p0xn{k}", [128, SP], F32)) for k in range(8)]
        xb = [E(nc.sbuf_tensor(f"p0xb{k}", [128, SP], BF16)) for k in range(8)]
        sq = [E(nc.sbuf_tensor(f"p0sq{k}", [128, SP], BF16)) for k in range(8)]
        x2b = [E(nc.sbuf_tensor(f"p0x2{k}", [128, SP], BF16)) for k in range(8)]
        xfb = [E(nc.sbuf_tensor(f"p0xf{k}", [128, SP], BF16)) for k in range(8)]
        xf32 = [E(nc.sbuf_tensor(f"p0xf32{k}", [128, SP], F32)) for k in range(8)]
        row = E(nc.sbuf_tensor("p0row", [1, 10 * SP], F32))
        rowb = E(nc.sbuf_tensor("p0rowb", [1, 10 * SP], BF16))
        lg6 = E(nc.sbuf_tensor("p0lg6", [8, SP], F32))
        ms8 = E(nc.sbuf_tensor("p0ms8", [128, 8], F32))
        mr1 = E(nc.sbuf_tensor("p0mr1", [128, 8], F32))
        mr2 = E(nc.sbuf_tensor("p0mr2", [128, 8], F32))
        lt0 = E(nc.sbuf_tensor("p0lt0", [128, 8], F32))
        lt1 = E(nc.sbuf_tensor("p0lt1", [128, 8], F32))
        lt2 = E(nc.sbuf_tensor("p0lt2", [128, 8], F32))
        mk1 = E(nc.sbuf_tensor("p0mk1", [128, 8], F32))
        wdd = E(nc.sbuf_tensor("p0wd", [128, 8], F32))
        mks = E(nc.sbuf_tensor("p0mks", [128, 32], F32))
        wds = E(nc.sbuf_tensor("p0wds", [128, 32], F32))
        qq = E(nc.sbuf_tensor("p0q", [128, 4], F32))
        wtok = E(nc.sbuf_tensor("p0wtok", [128, 32], F32))
        wtokb = E(nc.sbuf_tensor("p0wtokb", [128, 32], BF16))

        def R(slot, n=SP):
            return row[0:1, slot * SP: slot * SP + n]

        def Rb(slot, n=SP):
            return rowb[0:1, slot * SP: slot * SP + n]

        pipe = Pipe(kb, "p0", ["y", "t", "s", "v"])
        for i in range(NCH):
            c0, c1 = SP * i, SP * (i + 1)

            def ld(sy, i=i, c0=c0, c1=c1):
                return DmaBatch(kb, sy, "p0ld").go(
                    [(rt[k][:], g["residT"][128 * k:128 * (k + 1), c0:c1])
                     for k in range(8)])
            pipe.step(y=ld)
            def casts(sc):
                for k in range(8):
                    sc.activation(out=xb[k][:], in_=rt[k][:], func=AF.Copy)
                for k in range(7):
                    sc.activation(out=sq[k][:], in_=rt[k][:], func=AF.Square)
                return sc.activation(out=sq[7][:], in_=rt[7][:], func=AF.Square)
            pipe.step(s=casts)

            def sums(tn):
                for k in range(8):
                    tn.matmul(PS(0, SP, 1), sb["ones1_s"][:], xb[k][:],
                              start=(k == 0), stop=(k == 7))
                for k in range(8):
                    m = tn.matmul(PS(1, SP, 1), sb["ones1_s"][:], sq[k][:],
                                  start=(k == 0), stop=(k == 7))
                return m
            pipe.step(t=sums)

            def stat1(sc):
                sc.activation(out=R(0), in_=PS(0, SP, 1), func=AF.Copy,
                              scale=CP(1, 1))
                sc.activation(out=R(6), in_=PS(1, SP, 1), func=AF.Copy,
                              scale=CP(1, 1))
                return sc.activation(out=R(5), in_=R(0), func=AF.Square)
            pipe.step(s=stat1)

            def var1(ve):
                return ve.tensor_sub(R(5), R(6), R(5))
            pipe.step(v=var1)

            def stat2(sc):
                sc.activation(out=R(9), in_=R(5), func=AF.Sqrt, bias=CP(0, 1))
                return sc.activation(out=Rb(0), in_=R(0), func=AF.Copy)
            pipe.step(s=stat2)

            def lo1(ve):
                ve.reciprocal(out=R(1), in_=R(9))
                return ve.tensor_sub(R(5), R(0), Rb(0))
            pipe.step(v=lo1)

            def lo1c(sc):
                sc.activation(out=Rb(5), in_=R(5), func=AF.Copy)
                return sc.activation(out=Rb(1), in_=R(1), func=AF.Copy)
            pipe.step(s=lo1c)

            def lo1b(ve):
                return ve.tensor_sub(R(6), R(1), Rb(1))
            pipe.step(v=lo1b)

            def lo1d(sc):
                return sc.activation(out=Rb(6), in_=R(6), func=AF.Copy)
            pipe.step(s=lo1d)

            def bc1(tn):
                tn.matmul(PS(2, SP), sb["ones1r_s"][:], Rb(0), start=True, stop=False)
                tn.matmul(PS(2, SP), sb["ones1r_s"][:], Rb(5), start=False, stop=True)
                tn.matmul(PS(3, SP), sb["ones1r_s"][:], Rb(1), start=True, stop=False)
                return tn.matmul(PS(3, SP), sb["ones1r_s"][:], Rb(6), start=False, stop=True)
            pipe.step(t=bc1)

            def mkxn(ve):
                for k in range(8):
                    ve.tensor_sub(xn[k][:], rt[k][:], PS(2, SP))
                for k in range(8):
                    ve.tensor_mul(xn[k][:], xn[k][:], PS(3, SP))
                m = None
                for k in range(8):
                    m = ve.scalar_tensor_tensor(out=rt[k][:], in0=xn[k][:], scalar=2.0,
                                                in1=rt[k][:], op0=OP.mult, op1=OP.add)
                return m
            pipe.step(v=mkxn)

            def stbase(sy, i=i, c0=c0, c1=c1):
                pairs = [(g["base_dr"][128 * k:128 * (k + 1), c0:c1], rt[k][:])
                         for k in range(8)]
                if t_base is not None:
                    pairs += [(t_base[128 * k:128 * (k + 1), c0:c1], rt[k][:])
                              for k in range(8)]
                if t_xn is not None:
                    pairs += [(t_xn[128 * k:128 * (k + 1), c0:c1], xn[k][:])
                              for k in range(8)]
                return DmaBatch(kb, sy, "p0st").go(pairs)
            pipe.step(y=stbase)

            def casts2(sc):
                for k in range(8):
                    sc.activation(out=xb[k][:], in_=xn[k][:], func=AF.Copy)
                for k in range(8):
                    m = sc.activation(out=sq[k][:], in_=xn[k][:], func=AF.Square)
                return m
            pipe.step(s=casts2)
            pipe.step(t=sums)

            def stat3(sc):
                sc.activation(out=R(2), in_=PS(1, SP, 1), func=AF.Sqrt,
                              scale=CP(1, 1), bias=CP(0, 1))
                sc.activation(out=R(3), in_=PS(0, SP, 1), func=AF.Copy,
                              scale=CP(1, 1))
                sc.activation(out=R(6), in_=PS(1, SP, 1), func=AF.Copy,
                              scale=CP(1, 1))
                return sc.activation(out=R(5), in_=R(3), func=AF.Square)
            pipe.step(s=stat3)

            def var2(ve):
                ve.reciprocal(out=R(2), in_=R(2))
                return ve.tensor_sub(R(5), R(6), R(5))
            pipe.step(v=var2)

            def stat4(sc):
                sc.activation(out=R(9), in_=R(5), func=AF.Sqrt, bias=CP(0, 1))
                sc.activation(out=Rb(2), in_=R(2), func=AF.Copy)
                return sc.activation(out=Rb(3), in_=R(3), func=AF.Copy)
            pipe.step(s=stat4)

            def lo2(ve):
                ve.reciprocal(out=R(4), in_=R(9))
                ve.tensor_sub(R(5), R(2), Rb(2))
                return ve.tensor_sub(R(6), R(3), Rb(3))
            pipe.step(v=lo2)

            def lo2c(sc):
                sc.activation(out=Rb(5), in_=R(5), func=AF.Copy)
                sc.activation(out=Rb(6), in_=R(6), func=AF.Copy)
                return sc.activation(out=Rb(4), in_=R(4), func=AF.Copy)
            pipe.step(s=lo2c)

            def lo2b(ve):
                return ve.tensor_sub(R(7), R(4), Rb(4))
            pipe.step(v=lo2b)

            def lo2d(sc):
                return sc.activation(out=Rb(7), in_=R(7), func=AF.Copy)
            pipe.step(s=lo2d)

            def bc2(tn):
                tn.matmul(PS(2, SP), sb["ones1r_s"][:], Rb(2), start=True, stop=False)
                tn.matmul(PS(2, SP), sb["ones1r_s"][:], Rb(5), start=False, stop=True)
                tn.matmul(PS(3, SP), sb["ones1r_s"][:], Rb(3), start=True, stop=False)
                tn.matmul(PS(3, SP), sb["ones1r_s"][:], Rb(6), start=False, stop=True)
                tn.matmul(PS(4, SP), sb["ones1r_s"][:], Rb(4), start=True, stop=False)
                return tn.matmul(PS(4, SP), sb["ones1r_s"][:], Rb(7), start=False, stop=True)
            pipe.step(t=bc2)

            def mkx2f(ve):
                for k in range(8):
                    ve.scalar_tensor_tensor(out=x2b[k][:], in0=xn[k][:],
                                            scalar=sb["nrm_s"][:, k:k + 1],
                                            in1=PS(2, SP), op0=OP.mult, op1=OP.mult)
                for k in range(8):
                    ve.tensor_sub(xf32[k][:], xn[k][:], PS(3, SP))
                for k in range(8):
                    ve.tensor_mul(xf32[k][:], xf32[k][:], PS(4, SP))
                m = None
                for k in range(8):
                    ve.tensor_scalar(out=xfb[k][:], in0=xf32[k][:],
                                     scalar1=sb["lng_s"][:, k:k + 1],
                                     scalar2=sb["lnb_s"][:, k:k + 1],
                                     op0=OP.mult, op1=OP.add)
                    m = ve.tensor_scalar(out=xf32[k][:], in0=xf32[k][:],
                                         scalar1=sb["lng_s"][:, k:k + 1],
                                         scalar2=sb["lnb_s"][:, k:k + 1],
                                         op0=OP.mult, op1=OP.add)
                return m
            pipe.step(v=mkx2f)

            def stx2f(sy, i=i, c0=c0, c1=c1):
                pairs = []
                for k in range(8):
                    pairs.append((g["cc_xn2_in"].ap().rearrange(
                        "(d t) -> d t", d=D)[128 * k:128 * (k + 1), c0:c1], x2b[k][:]))
                    pairs.append((g["xnf_dr"][128 * k:128 * (k + 1), c0:c1], xfb[k][:]))
                if t_xn2 is not None:
                    pairs += [(t_xn2[128 * k:128 * (k + 1), c0:c1], x2b[k][:]) for k in range(8)]
                if t_xnf is not None:
                    pairs += [(t_xnf[128 * k:128 * (k + 1), c0:c1], xfb[k][:]) for k in range(8)]
                return DmaBatch(kb, sy, "p0st").go(pairs)
            pipe.step(y=stx2f)

            def glog(tn):
                for k in range(8):
                    m = tn.matmul(PS(6, SP, NEXP), sb["gw_s"][:, 6 * k:6 * (k + 1)],
                                  xf32[k][:], start=(k == 0), stop=(k == 7))
                return m
            pipe.step(t=glog)

            def gevac(ve):
                ve.memset(lg6[:], NEG)
                return ve.tensor_scalar(out=lg6[0:NEXP, :], in0=PS(6, SP, NEXP),
                                        scalar1=sb["gb_s"][:], scalar2=None, op0=OP.add)
            pipe.step(v=gevac)

            def gtr(tn):
                m = None
                for j in range(SP // 128):
                    m = tn.matmul(PS(7, 128, 128)[:, 8 * j:8 * (j + 1)][0:128, 0:8],
                                  lg6[:, 128 * j:128 * (j + 1)], sb["id8f_s"][:],
                                  start=True, stop=True)
                return m
            pipe.step(t=gtr)

            def top2b(ve):
                m = None
                for j in range(SP // 128):
                    pcol = PS(7, 128, 128)[:, 8 * j:8 * (j + 1)]
                    ve.tensor_copy(out=lt0[:], in_=pcol)
                    ve.drain()
                    ve.max(out=ms8[:], in_=lt0[:])
                    ve.memset(mr1[:], NEG)
                    ve.memset(mr2[:], NEG)
                    ve.drain()
                    ve.tensor_copy(out=mr1[:, 0:1], in_=ms8[:, 0:1])
                    ve.tensor_copy(out=mr2[:, 0:1], in_=ms8[:, 1:2])
                    ve.drain()
                    ve.match_replace(out=lt1[:], in_to_replace=mr1[:], in_values=lt0[:],
                                     imm_value=NEG)
                    ve.drain()
                    ve.match_replace(out=lt2[:], in_to_replace=mr2[:], in_values=lt1[:],
                                     imm_value=NEG)
                    ve.drain()
                    ve.tensor_tensor(out=mk1[:], in0=lt0[:], in1=lt1[:], op=OP.not_equal)
                    ve.tensor_tensor(out=wdd[:], in0=lt1[:], in1=lt2[:], op=OP.not_equal)
                    ve.drain()
                    ve.tensor_sub(wdd[:], wdd[:], mk1[:])
                    ve.drain()
                    ve.tensor_copy(out=mks[:, 8 * j:8 * (j + 1)], in_=mk1[:])
                    ve.tensor_copy(out=wds[:, 8 * j:8 * (j + 1)], in_=wdd[:])
                    m = ve.tensor_sub(qq[:, j:j + 1], ms8[:, 1:2], ms8[:, 0:1])
                return m
            pipe.step(v=top2b)

            def sigq(sc):
                return sc.activation(out=qq[:], in_=qq[:], func=AF.Sigmoid)
            pipe.step(s=sigq)

            def wcomb(ve):
                m = None
                for j in range(SP // 128):
                    m = ve.scalar_tensor_tensor(
                        out=wtok[:, 8 * j:8 * (j + 1)],
                        in0=wds[:, 8 * j:8 * (j + 1)],
                        scalar=qq[:, j:j + 1],
                        in1=mks[:, 8 * j:8 * (j + 1)],
                        op0=OP.mult, op1=OP.add)
                return m
            pipe.step(v=wcomb)

            def wcast(ve):
                return ve.tensor_copy(out=wtokb[:], in_=wtok[:])
            pipe.step(v=wcast)

            def wtr(tn):
                m = None
                for j in range(SP // 128):
                    m = tn.matmul(PS(6, 512)[0:8, 128 * j:128 * (j + 1)],
                                  wtokb[:, 8 * j:8 * (j + 1)], sb["ident_s"][:],
                                  start=True, stop=True)
                return m
            pipe.step(t=wtr)

            def wev(ve, i=i, c0=c0, c1=c1):
                return ve.tensor_copy(out=sb["w6_s"][:, c0:c1], in_=PS(6, SP, 8))
            pipe.step(v=wev)

            if t_logits is not None:
                def lgst(sy, i=i, c0=c0, c1=c1):
                    return DmaBatch(kb, sy, "p0st").go([(t_logits[:, c0:c1], lg6[:])])
                pipe.step(y=lgst)

        if t_w6 is not None:
            def w6st(sy):
                return DmaBatch(kb, sy, "p0st").go([(t_w6[:], sb["w6_s"][:])])
            pipe.step(y=w6st)

        with nc.Block() as blk:
            pipe.emit(blk, None)


def phase_expm(kb, g):
    nc = kb.nc
    sb = g["sb"]
    PS = g["PS"]
    CP = g["CP"]
    tap = g["tap"]
    t_tg = tap("tg", [D, D], BF16)

    with ExitStack() as ctx:
        E = ctx.enter_context
        gsf = [E(nc.sbuf_tensor(f"exgf{r}", [128, D], F32)) for r in range(8)]
        gsb = [E(nc.sbuf_tensor(f"exgb{r}", [128, D], BF16)) for r in range(8)]
        pb = [[E(nc.sbuf_tensor(f"expP{p}_{r}", [128, D], BF16)) for r in range(8)]
              for p in range(2)]

        leads = {(e, "g"): 100000 for e in "ytsv"}
        pipe = Pipe(kb, "pex", ["y", "t", "s", "v", "g"], leads=leads)

        def agx(gp):
            gp.collective_compute(
                "AllGather", OP.bypass, replica_groups=RG,
                ins=[g["cc_xn2_in"][:]], outs=[g["cc_xn2_out"][:]]).then_inc(
                kb.sem("agx"), 1)
            gp.wait_ge(kb.sem("agx"), 1)
            return None
        pipe.step(g=agx)

        def ldg(sy):
            return DmaBatch(kb, sy, "exld").go(
                [(gsf[r][:], g["gens_n"][128 * r:128 * (r + 1), :]) for r in range(8)])
        pipe.step(y=ldg)

        def cst(sc):
            m = None
            for r in range(8):
                m = sc.activation(out=gsb[r][:], in_=gsf[r][:], func=AF.Copy)
            return m
        pipe.step(s=cst)

        def initp(ve):
            par = NTERMS % 2
            for r in range(8):
                ve.memset(pb[par][r][:], 0.0)
            ve.drain()
            m = None
            for r in range(8):
                m = ve.tensor_copy(out=pb[par][r][:, 128 * r:128 * (r + 1)],
                                   in_=sb["ident_s"][:])
            return m
        pipe.step(v=initp)

        for k in range(NTERMS, 0, -1):
            src, dst = k % 2, 1 - (k % 2)
            for h in range(2):
                lo = 512 * h

                def mstep(tn, src=src, lo=lo):
                    m = None
                    for mo in range(8):
                        for r in range(8):
                            m = tn.matmul(PS(mo),
                                          gsb[r][:, 128 * mo:128 * (mo + 1)],
                                          pb[src][r][:, lo:lo + 512],
                                          start=(r == 0), stop=(r == 7))
                    return m
                pipe.step(t=mstep)

                def estep(sc, k=k, dst=dst, lo=lo):
                    m = None
                    for mo in range(8):
                        m = sc.activation(out=pb[dst][mo][:, lo:lo + 512],
                                          in_=PS(mo), func=AF.Copy, scale=CP(3 + k))
                    return m
                pipe.step(s=estep)

                def astep(ve, dst=dst, h=h):
                    m = None
                    for mo in range(4 * h, 4 * h + 4):
                        m = ve.tensor_tensor(
                            out=pb[dst][mo][:, 128 * mo:128 * (mo + 1)],
                            in0=pb[dst][mo][:, 128 * mo:128 * (mo + 1)],
                            in1=sb["ident_s"][:], op=OP.add)
                    return m
                pipe.step(v=astep)

        def stT(gp):
            pairs = [(g["cc_t_in"].ap().rearrange("(d n) -> d n", d=D)
                      [128 * r:128 * (r + 1), :], pb[0][r][:]) for r in range(8)]
            if t_tg is not None:
                pairs += [(t_tg[128 * r:128 * (r + 1), :], pb[0][r][:]) for r in range(8)]
            DmaBatch(kb, gp, "exst").go(pairs)
            gp.collective_compute(
                "AllGather", OP.bypass, replica_groups=RG,
                ins=[g["cc_t_in"][:]], outs=[g["cc_t_out"][:]]).then_inc(
                kb.sem("agt"), 1)
            gp.wait_ge(kb.sem("agt"), 1)
            return None
        pipe.step(g=stT)

        with nc.Block() as blk:
            pipe.emit(blk, None)


def phase2(kb, g):
    nc = kb.nc
    sb = g["sb"]
    PS = g["PS"]
    tap = g["tap"]
    E0 = kb.ctx.enter_context
    sb["rtot_s"] = E0(nc.sbuf_tensor("rtot_s", [128, 8 * D], BF16))
    t_rtot = tap("rtot", [128, 8 * D], BF16)
    t_x = tap("x_", [LC, T])
    t_ss = tap("sskip", [LC, T])

    ccx = g["cc_xn2_out"].ap().rearrange("(s d t) -> s d t", s=NCORES, d=D)
    cct = g["cc_t_out"].ap().rearrange("(s d n) -> s d n", s=NCORES, d=D)

    with ExitStack() as ctx:
        E = ctx.enter_context
        tz = [E(nc.sbuf_tensor(f"p2tz{r}", [128, D], BF16)) for r in range(8)]
        zb = [[E(nc.sbuf_tensor(f"p2z{p}_{r}", [128, D], BF16)) for r in range(8)]
              for p in range(2)]
        wstg = [E(nc.sbuf_tensor(f"p2wstg{r}", [128, LC], F32)) for r in range(8)]
        inwb = [E(nc.sbuf_tensor(f"p2inw{r}", [128, LC], BF16)) for r in range(8)]
        skwb = [E(nc.sbuf_tensor(f"p2skw{r}", [128, LC], BF16)) for r in range(8)]
        wd1b = [E(nc.sbuf_tensor(f"p2wd1{m}", [128, DTR], BF16)) for m in range(2)]
        wbb = [E(nc.sbuf_tensor(f"p2wbb{m}", [128, NST], BF16)) for m in range(2)]
        wcb = [E(nc.sbuf_tensor(f"p2wcb{m}", [128, NST], BF16)) for m in range(2)]
        xg = [E(nc.sbuf_tensor(f"p2xg{r}", [128, 512], BF16)) for r in range(8)]
        zext = [E(nc.sbuf_tensor(f"p2ze{m}", [128, 515], F32)) for m in range(2)]
        xc = [E(nc.sbuf_tensor(f"p2xc{m}", [128, 512], F32)) for m in range(2)]
        xf = [E(nc.sbuf_tensor(f"p2xf{m}", [128, 512], F32)) for m in range(2)]
        xbf = [E(nc.sbuf_tensor(f"p2xb{m}", [128, 512], BF16)) for m in range(2)]
        ssk = [E(nc.sbuf_tensor(f"p2ss{m}", [128, 512], F32)) for m in range(2)]
        pd1 = E(nc.sbuf_tensor("p2pd1", [DTR, 512], F32))
        pB = E(nc.sbuf_tensor("p2pB", [NST, 512], F32))
        pC = E(nc.sbuf_tensor("p2pC", [NST, 512], F32))

        pipe = Pipe(kb, "p2", ["y", "t", "s", "v"])

        def ldT(sy, blkid=3):
            return DmaBatch(kb, sy, "p2ld").go(
                [(tz[r][:], cct[blkid, 128 * r:128 * (r + 1), :]) for r in range(8)])
        pipe.step(y=ldT)

        for h in range(2):
            def ztr(tn, h=h):
                m = None
                for mo in range(8):
                    for j in range(4):
                        nc_ = 4 * h + j
                        m = tn.matmul(PS(mo)[:, 128 * j:128 * (j + 1)],
                                      tz[nc_][:, 128 * mo:128 * (mo + 1)],
                                      sb["ident_s"][:], start=True, stop=True)
                return m
            pipe.step(t=ztr)

            def zev(sc, h=h):
                m = None
                for mo in range(8):
                    m = sc.activation(out=zb[0][mo][:, 512 * h:512 * h + 512],
                                      in_=PS(mo), func=AF.Copy)
                return m
            pipe.step(s=zev)

        par = 0
        for stage, blkid in enumerate([2, 1, 0]):
            def ldTs(sy, blkid=blkid):
                return DmaBatch(kb, sy, "p2ld").go(
                    [(tz[r][:], cct[blkid, 128 * r:128 * (r + 1), :]) for r in range(8)])
            pipe.step(y=ldTs)
            last = stage == 2
            for h in range(2):
                def zmm(tn, par=par, h=h):
                    m = None
                    for mo in range(8):
                        for r in range(8):
                            m = tn.matmul(PS(mo),
                                          tz[r][:, 128 * mo:128 * (mo + 1)],
                                          zb[par][r][:, 512 * h:512 * h + 512],
                                          start=(r == 0), stop=(r == 7))
                    return m
                pipe.step(t=zmm)

                def zev2(sc, par=par, h=h, last=last):
                    m = None
                    for mo in range(8):
                        if last:
                            dst = sb["rtot_s"][:, D * mo + 512 * h: D * mo + 512 * h + 512]
                        else:
                            dst = zb[1 - par][mo][:, 512 * h:512 * h + 512]
                        m = sc.activation(out=dst, in_=PS(mo), func=AF.Copy)
                    return m
                pipe.step(s=zev2)
            par = 1 - par

        def ldw1(sy):
            return DmaBatch(kb, sy, "p2ld").go(
                [(wstg[r][:], g["in_wT"][128 * r:128 * (r + 1), :]) for r in range(8)])
        pipe.step(y=ldw1)

        def cw1(sc):
            m = None
            for r in range(8):
                m = sc.activation(out=inwb[r][:], in_=wstg[r][:], func=AF.Copy)
            return m
        pipe.step(s=cw1)

        def ldw2(sy):
            return DmaBatch(kb, sy, "p2ld").go(
                [(wstg[r][:], g["skip_wT"][128 * r:128 * (r + 1), :]) for r in range(8)])
        pipe.step(y=ldw2)

        def cw2(sc):
            m = None
            for r in range(8):
                m = sc.activation(out=skwb[r][:], in_=wstg[r][:], func=AF.Copy)
            return m
        pipe.step(s=cw2)

        def ldw3(sy):
            return DmaBatch(kb, sy, "p2ld").go(
                [(wstg[0][:, 0:DTR], g["wd1T"][0:128, :]),
                 (wstg[1][:, 0:DTR], g["wd1T"][128:256, :]),
                 (wstg[2][:, 0:NST], g["wbT"][0:128, :]),
                 (wstg[3][:, 0:NST], g["wbT"][128:256, :]),
                 (wstg[4][:, 0:NST], g["wcT"][0:128, :]),
                 (wstg[5][:, 0:NST], g["wcT"][128:256, :])])
        pipe.step(y=ldw3)

        def cw3(sc):
            for m in range(2):
                sc.activation(out=wd1b[m][:], in_=wstg[m][:, 0:DTR], func=AF.Copy)
                sc.activation(out=wbb[m][:], in_=wstg[2 + m][:, 0:NST], func=AF.Copy)
                sc.activation(out=wcb[m][:], in_=wstg[4 + m][:, 0:NST], func=AF.Copy)
            return sc.activation(out=wcb[1][:], in_=wstg[5][:, 0:NST], func=AF.Copy)
        pipe.step(s=cw3)

        def zinit(ve):
            ve.memset(zext[0][:, 0:3], 0.0)
            return ve.memset(zext[1][:, 0:3], 0.0)
        pipe.step(v=zinit)

        NSEG = T // 512
        for i in range(NSEG):
            src, col = (512 * i) // TC, (512 * i) % TC

            def ldx(sy, src=src, col=col):
                return DmaBatch(kb, sy, "p2ld").go(
                    [(xg[r][:], ccx[src, 128 * r:128 * (r + 1), col:col + 512])
                     for r in range(8)])
            pipe.step(y=ldx)

            def zmm2(tn):
                m = None
                for mo in range(2):
                    for r in range(8):
                        tn.matmul(PS(mo), inwb[r][:, 128 * mo:128 * (mo + 1)],
                                  xg[r][:], start=(r == 0), stop=(r == 7))
                for mo in range(2):
                    for r in range(8):
                        m = tn.matmul(PS(2 + mo), skwb[r][:, 128 * mo:128 * (mo + 1)],
                                      xg[r][:], start=(r == 0), stop=(r == 7))
                return m
            pipe.step(t=zmm2)

            def zevac(sc):
                for m in range(2):
                    sc.activation(out=zext[m][:, 3:515], in_=PS(m), func=AF.Copy)
                sc.activation(out=ssk[0][:], in_=PS(2), func=AF.Silu)
                return sc.activation(out=ssk[1][:], in_=PS(3), func=AF.Silu)
            pipe.step(s=zevac)

            def conv(ve):
                m = None
                for mm in range(2):
                    ve.tensor_scalar(out=xc[mm][:], in0=zext[mm][:, 3:515],
                                     scalar1=sb["cw_s"][:, 4 * mm + 3:4 * mm + 4],
                                     scalar2=sb["cb_s"][:, mm:mm + 1],
                                     op0=OP.mult, op1=OP.add)
                    for kk in range(3):
                        ve.scalar_tensor_tensor(
                            out=xc[mm][:], in0=zext[mm][:, kk:kk + 512],
                            scalar=sb["cw_s"][:, 4 * mm + kk:4 * mm + kk + 1],
                            in1=xc[mm][:], op0=OP.mult, op1=OP.add)
                    ve.drain()
                    m = ve.tensor_copy(out=zext[mm][:, 0:3], in_=zext[mm][:, 512:515])
                return m
            pipe.step(v=conv)

            def silux(sc):
                for m in range(2):
                    sc.activation(out=xf[m][:], in_=xc[m][:], func=AF.Silu)
                for m in range(2):
                    sc.activation(out=xbf[m][:], in_=xc[m][:], func=AF.Silu)
                return sc.activation(out=xbf[1][:], in_=xc[1][:], func=AF.Silu)
            pipe.step(s=silux)

            def pmm(tn):
                for m in range(2):
                    tn.matmul(PS(4, 512, DTR), wd1b[m][:], xbf[m][:],
                              start=(m == 0), stop=(m == 1))
                for m in range(2):
                    tn.matmul(PS(5, 512, NST), wbb[m][:], xbf[m][:],
                              start=(m == 0), stop=(m == 1))
                m2 = None
                for m in range(2):
                    m2 = tn.matmul(PS(6, 512, NST), wcb[m][:], xbf[m][:],
                                   start=(m == 0), stop=(m == 1))
                return m2
            pipe.step(t=pmm)

            def pev(ve):
                ve.tensor_copy(out=pd1[:], in_=PS(4, 512, DTR))
                ve.tensor_copy(out=pB[:], in_=PS(5, 512, NST))
                return ve.tensor_copy(out=pC[:], in_=PS(6, 512, NST))
            pipe.step(v=pev)

            def stx(sy, i=i):
                c0 = 512 * i
                bcd = g["cc_bcd_in"].ap().rearrange("(r t) -> r t", r=96)
                pairs = [(g["x_dr"][128 * m:128 * (m + 1), c0:c0 + 512], xf[m][:])
                         for m in range(2)]
                pairs += [(g["sskip_dr"][128 * m:128 * (m + 1), c0:c0 + 512], ssk[m][:])
                          for m in range(2)]
                pairs += [(bcd[0:DTR, c0:c0 + 512], pd1[:]),
                          (bcd[DTR:DTR + NST, c0:c0 + 512], pB[:]),
                          (bcd[DTR + NST:96, c0:c0 + 512], pC[:])]
                if t_x is not None:
                    pairs += [(t_x[128 * m:128 * (m + 1), c0:c0 + 512], xf[m][:])
                              for m in range(2)]
                if t_ss is not None:
                    pairs += [(t_ss[128 * m:128 * (m + 1), c0:c0 + 512], ssk[m][:])
                              for m in range(2)]
                return DmaBatch(kb, sy, "p2st").go(pairs)
            pipe.step(y=stx)

        if t_rtot is not None:
            def strt(sy):
                return DmaBatch(kb, sy, "p2st").go([(t_rtot[:], sb["rtot_s"][:])])
            pipe.step(y=strt)

        with nc.Block() as blk:
            pipe.emit(blk, None)


def phase3(kb, g):
    nc = kb.nc
    tap = g["tap"]
    t_bcd = tap("bcd", [96, T])
    with nc.Block() as blk:
        @blk.gpsimd
        def _(gp):
            gp.collective_compute(
                "AllReduce", OP.add, replica_groups=RG,
                ins=[g["cc_bcd_in"][:]], outs=[g["cc_bcd_out"][:]]).then_inc(
                kb.sem("p3g"), 1)
            gp.wait_ge(kb.sem("p3g"), 1)
            if t_bcd is not None:
                tgt = dma(gp, t_bcd[:],
                          g["cc_bcd_out"].ap().rearrange("(r t) -> r t", r=96),
                          kb, "p3st")
                gp.wait_ge(kb.sem("p3st"), tgt)


def phase4(kb, g):
    nc = kb.nc
    sb = g["sb"]
    PS = g["PS"]
    tap = g["tap"]
    t_y = tap("yscan", [LC, T])
    t_gy = tap("gy", [LC, T], BF16)
    t_dl = tap("delta", [LC, T])
    t_d2 = tap("d2raw", [LC, T])
    t_hb = tap("hb", [128, NST * 2 * 512])

    bcd = g["cc_bcd_out"].ap().rearrange("(r t) -> r t", r=96)
    ggy = g["cc_gy_in"].ap().rearrange("(l t) -> l t", l=LC)
    S = 512
    NSEG = T // S

    with ExitStack() as ctx:
        E = ctx.enter_context
        oh16_s = E(nc.sbuf_tensor("p4oh16", [NST, NST * 128], BF16))
        wd2f = E(nc.sbuf_tensor("p4wd2f", [DTR, LC], F32))
        wd2b = E(nc.sbuf_tensor("p4wd2b", [DTR, LC], BF16))
        d1f = E(nc.sbuf_tensor("p4d1f", [DTR, S], F32))
        d1b = E(nc.sbuf_tensor("p4d1b", [DTR, S], BF16))
        btf = E(nc.sbuf_tensor("p4btf", [NST, S], F32))
        btb = E(nc.sbuf_tensor("p4btb", [NST, S], BF16))
        ctf = E(nc.sbuf_tensor("p4ctf", [NST, S], F32))
        ctb = E(nc.sbuf_tensor("p4ctb", [NST, S], BF16))
        dlt = [E(nc.sbuf_tensor(f"p4dl{m}", [128, S], F32)) for m in range(2)]
        sr1 = [E(nc.sbuf_tensor(f"p4sr{m}", [128, S], F32)) for m in range(2)]
        d2r = [E(nc.sbuf_tensor(f"p4d2r{m}", [128, S], F32)) for m in range(2)]
        sa1 = [E(nc.sbuf_tensor(f"p4sa{m}", [128, S], F32)) for m in range(2)]
        dx = [E(nc.sbuf_tensor(f"p4dx{m}", [128, S], F32)) for m in range(2)]
        xw = [E(nc.sbuf_tensor(f"p4xw{m}", [128, S], F32)) for m in range(2)]
        ssk = [E(nc.sbuf_tensor(f"p4ss{m}", [128, S], F32)) for m in range(2)]
        aa = E(nc.sbuf_tensor("p4aa", [128, 8 * S], F32))
        bb = [E(nc.sbuf_tensor(f"p4bb{m}", [128, S], F32)) for m in range(2)]
        hbuf = E(nc.sbuf_tensor("p4hb", [128, NST * 2 * S], F32))
        state = E(nc.sbuf_tensor("p4st", [128, 32], F32))
        yy = [E(nc.sbuf_tensor(f"p4yy{m}", [128, S], F32)) for m in range(2)]
        gyb = [E(nc.sbuf_tensor(f"p4gy{m}", [128, S], BF16)) for m in range(2)]

        hb3 = hbuf.ap().rearrange("p (n q) -> p n q", q=2 * S)
        hb3r = hbuf.ap().rearrange("p (n q) -> p q n", q=2 * S)

        pipe = Pipe(kb, "p4", ["y", "t", "s", "v"])

        def ldw(sy):
            return DmaBatch(kb, sy, "p4ld").go([(wd2f[:], g["wd2T"][:]),
                                                (oh16_s[:], g["oh16"][:])])
        pipe.step(y=ldw)

        def cwz(sc):
            return sc.activation(out=wd2b[:], in_=wd2f[:], func=AF.Copy)
        pipe.step(s=cwz)

        def sinit(ve):
            return ve.memset(state[:], 0.0)
        pipe.step(v=sinit)

        for i in range(NSEG):
            c0 = S * i

            def ld(sy, c0=c0):
                return DmaBatch(kb, sy, "p4ld").go(
                    [(d1f[:], bcd[0:DTR, c0:c0 + S]),
                     (btf[:], bcd[DTR:DTR + NST, c0:c0 + S]),
                     (ctf[:], bcd[DTR + NST:96, c0:c0 + S]),
                     (xw[0][:], g["x_dr"][0:128, c0:c0 + S]),
                     (xw[1][:], g["x_dr"][128:256, c0:c0 + S]),
                     (ssk[0][:], g["sskip_dr"][0:128, c0:c0 + S]),
                     (ssk[1][:], g["sskip_dr"][128:256, c0:c0 + S])])
            pipe.step(y=ld)

            def cst(sc):
                sc.activation(out=d1b[:], in_=d1f[:], func=AF.Copy)
                sc.activation(out=btb[:], in_=btf[:], func=AF.Copy)
                return sc.activation(out=ctb[:], in_=ctf[:], func=AF.Copy)
            pipe.step(s=cst)

            def d2mm(tn):
                m = None
                for et in range(2):
                    m = tn.matmul(PS(et), wd2b[:, 128 * et:128 * (et + 1)],
                                  d1b[:], start=True, stop=True)
                return m
            pipe.step(t=d2mm)

            def spl(sc):
                m = None
                for et in range(2):
                    if t_d2 is not None:
                        sc.activation(out=d2r[et][:], in_=PS(et), func=AF.Copy)
                    sc.activation(out=sr1[et][:], in_=PS(et), func=AF.Relu,
                                  bias=sb["wd2b_s"][:, et:et + 1])
                    sc.activation(out=sa1[et][:], in_=PS(et), func=AF.Abs,
                                  bias=sb["wd2b_s"][:, et:et + 1])
                    sc.activation(out=sa1[et][:], in_=sa1[et][:], func=AF.Exp,
                                  scale=CP(2, 1) if False else sb["cp_s"][:, 2:3])
                    m = sc.activation(out=sa1[et][:], in_=sa1[et][:], func=AF.Ln,
                                      bias=sb["cp_s"][:, 3:4])
                return m
            pipe.step(s=spl)

            def mdx(ve):
                ve.tensor_add(dlt[0][:], sr1[0][:], sa1[0][:])
                ve.tensor_add(dlt[1][:], sr1[1][:], sa1[1][:])
                ve.drain()
                ve.tensor_mul(dx[0][:], dlt[0][:], xw[0][:])
                return ve.tensor_mul(dx[1][:], dlt[1][:], xw[1][:])
            pipe.step(v=mdx)

            for ng in range(4):
                def reps(tn, ng=ng):
                    m = None
                    for j in range(4):
                        n = 4 * ng + j
                        m = tn.matmul(PS(2 * j), oh16_s[:, 128 * n:128 * (n + 1)],
                                      btb[:], start=True, stop=True)
                        m = tn.matmul(PS(2 * j + 1), oh16_s[:, 128 * n:128 * (n + 1)],
                                      ctb[:], start=True, stop=True)
                    return m
                pipe.step(t=reps)

                def aexp(sc, ng=ng):
                    m = None
                    for j in range(4):
                        n = 4 * ng + j
                        for et in range(2):
                            m = sc.activation(
                                out=aa[:, (2 * j + et) * S:(2 * j + et + 1) * S],
                                in_=dlt[et][:], func=AF.Exp,
                                scale=sb["acoef_s"][:, 16 * et + n:16 * et + n + 1])
                    return m
                pipe.step(s=aexp)

                def scans(ve, ng=ng):
                    for j in range(4):
                        n = 4 * ng + j
                        for et in range(2):
                            hcol = n * 2 * S + et * S
                            ve.tensor_mul(bb[et][:], dx[et][:], PS(2 * j))
                            ve.tensor_tensor_scan(
                                out=hbuf[:, hcol:hcol + S],
                                data0=aa[:, (2 * j + et) * S:(2 * j + et + 1) * S],
                                data1=bb[et][:],
                                initial=state[:, 16 * et + n:16 * et + n + 1],
                                op0=OP.mult, op1=OP.add)
                    ve.drain()
                    for et in range(2):
                        ve.tensor_copy(
                            out=state[:, 16 * et + 4 * ng:16 * et + 4 * ng + 4],
                            in_=hb3[:, 4 * ng:4 * ng + 4, et * S + S - 1:et * S + S])
                    ve.drain()
                    m = None
                    for j in range(4):
                        n = 4 * ng + j
                        for et in range(2):
                            hcol = n * 2 * S + et * S
                            m = ve.tensor_tensor(
                                out=hbuf[:, hcol:hcol + S],
                                in0=hbuf[:, hcol:hcol + S],
                                in1=PS(2 * j + 1), op=OP.mult)
                    return m
                pipe.step(v=scans)


            def redu(ve):
                m = None
                for et in range(2):
                    m = ve.reduce_sum(out=yy[et][:],
                                      in_=hb3r[:, et * S:(et + 1) * S, :],
                                      axis=AX.X)
                return m
            pipe.step(v=redu)

            def gate(ve):
                m = None
                for et in range(2):
                    ve.scalar_tensor_tensor(out=yy[et][:], in0=xw[et][:],
                                            scalar=sb["wdt_s"][:, et:et + 1],
                                            in1=yy[et][:], op0=OP.mult, op1=OP.add)
                    m = ve.tensor_mul(gyb[et][:], yy[et][:], ssk[et][:])
                return m
            pipe.step(v=gate)

            def stg(sy, i=i, c0=c0):
                pairs = [(ggy[128 * et:128 * (et + 1), c0:c0 + S], gyb[et][:])
                         for et in range(2)]
                if t_dl is not None:
                    pairs += [(t_dl[128 * et:128 * (et + 1), c0:c0 + S], dlt[et][:])
                              for et in range(2)]
                if t_d2 is not None:
                    pairs += [(t_d2[128 * et:128 * (et + 1), c0:c0 + S], d2r[et][:])
                              for et in range(2)]
                if t_hb is not None and i == 0:
                    pairs += [(t_hb[:], hbuf[:])]
                if t_y is not None:
                    pairs += [(t_y[128 * et:128 * (et + 1), c0:c0 + S], yy[et][:])
                              for et in range(2)]
                if t_gy is not None:
                    pairs += [(t_gy[128 * et:128 * (et + 1), c0:c0 + S], gyb[et][:])
                              for et in range(2)]
                return DmaBatch(kb, sy, "p4st").go(pairs)
            pipe.step(y=stg)

        with nc.Block() as blk:
            pipe.emit(blk, None)

        with nc.Block() as blk2:
            @blk2.gpsimd
            def _(gp):
                gp.collective_compute(
                    "AllGather", OP.bypass, replica_groups=RG,
                    ins=[g["cc_gy_in"][:]], outs=[g["cc_gy_out"][:]]).then_inc(
                    kb.sem("aggy"), 1)
                gp.wait_ge(kb.sem("aggy"), 1)
                v = gp.value_load(sb["tok0_s"][0:1, 0:1])
                gg = g["cc_gy_out"].ap().rearrange("(r t) -> r t", r=NCORES * LC)
                tgt = None
                for r in range(16):
                    tgt = dma(gp, g["gyw_dr"][128 * r:128 * (r + 1), :],
                              gg[128 * r:128 * (r + 1), bass.ds(v, TC)], kb, "gyw")
                gp.wait_ge(kb.sem("gyw"), tgt)


def phase_ffn(kb, g):
    nc = kb.nc
    sb = g["sb"]
    PS = g["PS"]
    tap = g["tap"]
    t_abel = tap("abel", [D, TC], BF16)
    t_nona = tap("nona", [D, TC], BF16)
    t_moe2 = tap("moe2", [D, TC], BF16)
    t_rt2 = tap("rtot2", [128, 8 * D], BF16)

    with ExitStack() as ctx:
        E = ctx.enter_context
        xnfb = [E(nc.sbuf_tensor(f"pfxn{r}", [128, 1024], BF16)) for r in range(8)]
        wkf = [E(nc.sbuf_tensor(f"pfwf{p}", [128, 2048], F32)) for p in range(2)]
        wkb = [E(nc.sbuf_tensor(f"pfwb{p}", [128, 2048], BF16)) for p in range(2)]
        he = [E(nc.sbuf_tensor(f"pfhe{r}", [128, 1024], BF16)) for r in range(16)]
        moe = [E(nc.sbuf_tensor(f"pfmo{r}", [128, 1024], F32)) for r in range(8)]
        nonab = [E(nc.sbuf_tensor(f"pfnb{r}", [128, 1024], BF16)) for r in range(8)]
        abelb = [E(nc.sbuf_tensor(f"pfab{r}", [128, 1024], BF16)) for r in range(8)]
        basef = [E(nc.sbuf_tensor(f"pfbs{r}", [128, 1024], F32)) for r in range(2)]
        wrepb = E(nc.sbuf_tensor("pfwr", [128, 1024], BF16))
        eb2b = E(nc.sbuf_tensor("pfeb2", [NEXP, D], BF16))
        gcwb = E(nc.sbuf_tensor("pfgcw", [128, 16], BF16))
        grow = E(nc.sbuf_tensor("pfgrow", [1, 1024], F32))
        ghi = E(nc.sbuf_tensor("pfghi", [1, 1024], BF16))
        glo = E(nc.sbuf_tensor("pfglo", [1, 1024], F32))
        glob = E(nc.sbuf_tensor("pfglob", [1, 1024], BF16))
        gyr = abelb + nonab

        pipe = Pipe(kb, "pf", ["y", "t", "s", "v"])
        step = pipe.step

        def ldsm(sy):
            return DmaBatch(kb, sy, "pfld").go([(basef[0][0:NEXP, :], g["eb2_n"][:]),
                                                (basef[1][0:128, 0:16], g["gcw_t"][:])])
        step(y=ldsm)

        def cssm(sc):
            sc.activation(out=eb2b[:], in_=basef[0][0:NEXP, :], func=AF.Copy)
            return sc.activation(out=gcwb[:], in_=basef[1][:, 0:16], func=AF.Copy)
        step(s=cssm)

        for H in range(2):
            hc0 = 1024 * H

            def ldxn(sy, hc0=hc0):
                return DmaBatch(kb, sy, "pfld").go(
                    [(xnfb[r][:], g["xnf_dr"][128 * r:128 * (r + 1), hc0:hc0 + 1024])
                     for r in range(8)])
            step(y=ldxn)

            units = []
            raw = []
            for e in range(NEXP):
                raw.append(("wrep", e))
                for mt in range(16):
                    raw.append(("w1", e, mt))
                raw.append(("wsc", e))
                for mo in range(8):
                    raw.append(("w2", e, mo))
            raw.append(("csmoe",))
            for mo in range(8):
                raw.append(("na", mo))
            raw.append(("tapmoe",))
            for q in range(2):
                for mt in range(32):
                    raw.append(("a1", q, mt))
                for mo in range(8):
                    raw.append(("a2a", q, mo))
                    raw.append(("a2b", q, mo))
            prev = None
            for u in raw:
                grp = u[0] if u[0] not in ("a2a", "a2b") else "a2"
                if prev is not None and grp != prev:
                    units.append(("pad",))
                    units.append(("pad",))
                units.append(u)
                prev = grp
            units.append(("pad",))
            units.append(("pad",))

            ew1r = g["ew1"].ap().rearrange("e (r p) h -> e p r h", p=128)
            ew2r = g["ew2"].ap().rearrange("e (r p) d -> e p r d", p=128)
            aw1r = g["aw1"].ap().rearrange("(r p) h -> p r h", p=128)
            aw2r = g["aw2"].ap().rearrange("(r p) d -> p r d", p=128)

            def u_load(sy, u, p, hc0=hc0):
                kind = u[0]
                d = DmaBatch(kb, sy, "pfld")
                wk3 = wkf[p].ap().rearrange("q (r c) -> q r c", c=128)
                if kind == "w1":
                    _, e, mt = u
                    return d.go([(wk3[:, 4 * j:4 * (j + 1), :],
                                  ew1r[e, :, 4 * j:4 * (j + 1), 128 * mt:128 * (mt + 1)])
                                 for j in range(2)])
                if kind == "w2":
                    _, e, mo = u
                    return d.go([(wk3[:, 8 * j:8 * (j + 1), :],
                                  ew2r[e, :, 8 * j:8 * (j + 1), 128 * mo:128 * (mo + 1)])
                                 for j in range(2)])
                if kind == "a1":
                    _, q_, mt = u
                    return d.go([(wk3[:, 4 * j:4 * (j + 1), :],
                                  aw1r[:, 4 * j:4 * (j + 1), 128 * mt:128 * (mt + 1)])
                                 for j in range(2)])
                if kind == "a2a":
                    _, q_, mo = u
                    return d.go([(wk3[:, 8 * j:8 * (j + 1), :],
                                  aw2r[:, 8 * j:8 * (j + 1), 128 * mo:128 * (mo + 1)])
                                 for j in range(2)])
                if kind == "a2b":
                    _, q_, mo = u
                    return d.go([(wk3[:, 8 * j:8 * (j + 1), :],
                                  aw2r[:, 16 + 8 * j:16 + 8 * (j + 1), 128 * mo:128 * (mo + 1)])
                                 for j in range(2)])
                return None

            def u_cast(sc, u, p):
                kind = u[0]
                if kind in ("w1", "a1"):
                    return sc.activation(out=wkb[p][:, 0:1024], in_=wkf[p][:, 0:1024],
                                         func=AF.Copy)
                if kind in ("w2", "a2a", "a2b"):
                    return sc.activation(out=wkb[p][:], in_=wkf[p][:], func=AF.Copy)
                return None

            def u_mm(tn, u, p, hc0=hc0):
                kind = u[0]
                m = None
                if kind == "wrep":
                    e = u[1]
                    for tt in range(2):
                        m = tn.matmul(PS(4 + tt), sb["oh8_s"][:, 128 * e:128 * (e + 1)],
                                      sb["w6_s"][:, hc0 + 512 * tt:hc0 + 512 * (tt + 1)],
                                      start=True, stop=True)
                elif kind == "w1":
                    bk = 2 * (p % 2)
                    for r in range(8):
                        for tt in range(2):
                            m = tn.matmul(PS(bk + tt), wkb[p][:, 128 * r:128 * (r + 1)],
                                          xnfb[r][:, 512 * tt:512 * (tt + 1)],
                                          start=(r == 0), stop=(r == 7))
                elif kind == "w2":
                    _, e, mo = u
                    bk = 4 + 2 * (p % 2)
                    for r in range(16):
                        for tt in range(2):
                            m = tn.matmul(PS(bk + tt), wkb[p][:, 128 * r:128 * (r + 1)],
                                          he[r][:, 512 * tt:512 * (tt + 1)],
                                          start=(r == 0),
                                          stop=(r == 15 and e != 0))
                    if e == 0:
                        for tt in range(2):
                            m = tn.matmul(
                                PS(bk + tt), eb2b[0:NEXP, 128 * mo:128 * (mo + 1)],
                                sb["w6_s"][0:NEXP, hc0 + 512 * tt:hc0 + 512 * (tt + 1)],
                                start=False, stop=True)
                elif kind == "na":
                    mo = u[1]
                    bk = 4 + 2 * (mo % 2)
                    for tt in range(2):
                        for r in range(8):
                            m = tn.matmul(PS(bk + tt),
                                          sb["rtot_s"][:, D * r + 128 * mo:D * r + 128 * (mo + 1)],
                                          gyr[r][:, 512 * tt:512 * (tt + 1)],
                                          start=(r == 0), stop=(r == 7))
                elif kind == "a1":
                    _, q, mt = u
                    bk = 2 * (p % 2)
                    for r in range(8):
                        m = tn.matmul(PS(bk), wkb[p][:, 128 * r:128 * (r + 1)],
                                      xnfb[r][:, 512 * q:512 * (q + 1)],
                                      start=(r == 0), stop=(r == 7))
                elif kind == "a2a":
                    _, q, mo = u
                    bk = 4 + 2 * (mo % 2)
                    for r in range(16):
                        m = tn.matmul(PS(bk), wkb[p][:, 128 * r:128 * (r + 1)],
                                      he[r % 16][:, 512 * (r // 16):512 * (r // 16) + 512],
                                      start=(r == 0), stop=False)
                elif kind == "a2b":
                    _, q, mo = u
                    bk = 4 + 2 * (mo % 2)
                    for r in range(16):
                        m = tn.matmul(PS(bk), wkb[p][:, 128 * r:128 * (r + 1)],
                                      he[r % 16][:, 512 * (1):512 * (1) + 512]
                                      if False else he[(16 + r) % 16][:, 512:1024],
                                      start=False, stop=(r == 15))
                return m

            def u_post_s(sc, u, p, hc0=hc0):
                kind = u[0]
                m = None
                if kind == "wrep":
                    sc.activation(out=wrepb[:, 0:512], in_=PS(4), func=AF.Copy)
                    m = sc.activation(out=wrepb[:, 512:1024], in_=PS(5), func=AF.Copy)
                elif kind == "w1":
                    _, e, mt = u
                    bk = 2 * (p % 2)
                    for tt in range(2):
                        m = sc.activation(out=he[mt][:, 512 * tt:512 * (tt + 1)],
                                          in_=PS(bk + tt), func=AF.Gelu,
                                          bias=sb["eb1_s"][:, 16 * e + mt:16 * e + mt + 1])
                elif kind == "na":
                    mo = u[1]
                    bk = 4 + 2 * (mo % 2)
                    sc.activation(out=nonab[mo][:, 0:512], in_=PS(bk), func=AF.Copy)
                    m = sc.activation(out=nonab[mo][:, 512:1024], in_=PS(bk + 1), func=AF.Copy)
                elif kind == "a1":
                    _, q, mt = u
                    bk = 2 * (p % 2)
                    h1q = he[mt % 16][:, 512 * (mt // 16):512 * (mt // 16) + 512]
                    m = sc.activation(out=h1q, in_=PS(bk), func=AF.Gelu,
                                      bias=sb["ab1_s"][:, mt:mt + 1])
                elif kind == "a2b":
                    _, q, mo = u
                    bk = 4 + 2 * (mo % 2)
                    m = sc.activation(out=abelb[mo][:, 512 * q:512 * q + 512], in_=PS(bk),
                                      func=AF.Identity, bias=sb["ab2_s"][:, mo:mo + 1])
                elif kind == "csmoe":
                    for mo in range(8):
                        m = sc.activation(out=gyr[mo][:], in_=moe[mo][:], func=AF.Copy)
                return m

            def u_post_v(ve, u, p):
                kind = u[0]
                m = None
                if kind == "wsc":
                    for mt in range(16):
                        m = ve.tensor_mul(he[mt][:], he[mt][:], wrepb[:])
                elif kind == "w2":
                    _, e, mo = u
                    bk = 4 + 2 * (p % 2)
                    for tt in range(2):
                        if e == 0:
                            ve.tensor_copy(out=moe[mo][:, 512 * tt:512 * (tt + 1)],
                                           in_=PS(bk + tt))
                        else:
                            ve.tensor_add(moe[mo][:, 512 * tt:512 * (tt + 1)],
                                          moe[mo][:, 512 * tt:512 * (tt + 1)],
                                          PS(bk + tt))
                    m = None
                return m

            NU = len(units)
            for k in range(NU + 3):
                fns = {}
                if k < NU:
                    u = units[k]
                    if u[0] in ("w1", "w2", "a1", "a2a", "a2b"):
                        fns["y"] = (lambda sy, u=u, p=k % 2, f=u_load: f(sy, u, p))
                    elif u[0] == "tapmoe" and t_moe2 is not None:
                        fns["y"] = (lambda sy, hc0=hc0: DmaBatch(kb, sy, "pfst").go(
                            [(t_moe2[128 * mo:128 * (mo + 1), hc0:hc0 + 1024], gyr[mo][:])
                             for mo in range(8)]))
                uc = units[k - 1] if 1 <= k <= NU else None
                um = units[k - 2] if 2 <= k <= NU + 1 else None
                up = units[k - 3] if 3 <= k <= NU + 2 else None

                def s_slot(sc, uc=uc, up=up, k=k, f1=u_cast, f2=u_post_s):
                    m = None
                    if uc is not None:
                        m = f1(sc, uc, (k - 1) % 2) or m
                    if up is not None:
                        m = f2(sc, up, (k - 3) % 2) or m
                    return m
                if (uc is not None and uc[0] in ("w1", "w2", "a1", "a2a", "a2b")) or \
                   (up is not None and up[0] in ("wrep", "w1", "na", "a1", "a2b", "csmoe")):
                    fns["s"] = s_slot
                if um is not None and um[0] in ("wrep", "w1", "w2", "na", "a1", "a2a", "a2b"):
                    fns["t"] = (lambda tn, um=um, k=k, f=u_mm: f(tn, um, (k - 2) % 2))
                if up is not None and up[0] in ("wsc", "w2"):
                    fns["v"] = (lambda ve, up=up, k=k, f=u_post_v: f(ve, up, (k - 3) % 2))
                if fns:
                    step(**fns)

            def gcm(tn, hc0=hc0):
                m = None
                for tt in range(2):
                    for mo in range(8):
                        tn.matmul(PS(tt, 512, 1), gcwb[:, mo:mo + 1],
                                  abelb[mo][:, 512 * tt:512 * (tt + 1)],
                                  start=(mo == 0), stop=False)
                    for mo in range(8):
                        m = tn.matmul(PS(tt, 512, 1), gcwb[:, 8 + mo:9 + mo],
                                      nonab[mo][:, 512 * tt:512 * (tt + 1)],
                                      start=False, stop=(mo == 7))
                return m
            step(t=gcm)

            def gsig(sc):
                sc.activation(out=grow[0:1, 0:512], in_=PS(0, 512, 1), func=AF.Sigmoid,
                              bias=sb_gcb(sb))
                sc.activation(out=grow[0:1, 512:1024], in_=PS(1, 512, 1), func=AF.Sigmoid,
                              bias=sb_gcb(sb))
                return sc.activation(out=ghi[:], in_=grow[:], func=AF.Copy)
            step(s=gsig)

            def gl(ve):
                return ve.tensor_sub(glo[:], grow[:], ghi[:])
            step(v=gl)

            def glc(sc):
                return sc.activation(out=glob[:], in_=glo[:], func=AF.Copy)
            step(s=glc)

            def gbc(tn):
                m = None
                for tt in range(2):
                    tn.matmul(PS(4 + tt), sb["ones1r_s"][:], ghi[0:1, 512 * tt:512 * (tt + 1)],
                              start=True, stop=False)
                    m = tn.matmul(PS(4 + tt), sb["ones1r_s"][:], glob[0:1, 512 * tt:512 * (tt + 1)],
                                  start=False, stop=True)
                return m
            step(t=gbc)

            def mixv(ve):
                m = None
                for mo in range(8):
                    ve.tensor_copy(out=moe[mo][:], in_=abelb[mo][:])
                    ve.tensor_copy(out=basef[0][:], in_=nonab[mo][:])
                    ve.drain()
                    ve.tensor_sub(basef[0][:], basef[0][:], moe[mo][:])
                    for tt in range(2):
                        ve.tensor_mul(basef[0][:, 512 * tt:512 * (tt + 1)],
                                      basef[0][:, 512 * tt:512 * (tt + 1)], PS(4 + tt))
                    ve.drain()
                    m = ve.tensor_add(moe[mo][:], moe[mo][:], basef[0][:])
                return m
            step(v=mixv)

            if t_rt2 is not None:
                def strt2(sy):
                    return DmaBatch(kb, sy, "pfst").go([(t_rt2[:], sb["rtot_s"][:])])
                step(y=strt2)
            if t_abel is not None:
                def stab(sy, hc0=hc0):
                    return DmaBatch(kb, sy, "pfst").go(
                        [(t_abel[128 * mo:128 * (mo + 1), hc0:hc0 + 1024], abelb[mo][:])
                         for mo in range(8)])
                step(y=stab)
            if t_nona is not None:
                def stna(sy, hc0=hc0):
                    return DmaBatch(kb, sy, "pfst").go(
                        [(t_nona[128 * mo:128 * (mo + 1), hc0:hc0 + 1024], nonab[mo][:])
                         for mo in range(8)])
                step(y=stna)

            for r in range(17):
                fns = {}
                if r < 16:
                    fns["y"] = (lambda sy, r=r: DmaBatch(kb, sy, "pfld").go(
                        [(wkf[r % 2][:, 0:1024], g["out_wT"][128 * r:128 * (r + 1), :])]))
                if r >= 1:
                    fns["s"] = (lambda sc, r=r: sc.activation(
                        out=he[r - 1][:], in_=wkf[(r - 1) % 2][:, 0:1024], func=AF.Copy))
                step(**fns)

            def ldgy(sy, hc0=hc0):
                pairs = [(gyr[r][:],
                          g["gyw_dr"][128 * r:128 * (r + 1), hc0:hc0 + 1024])
                         for r in range(16)]
                return DmaBatch(kb, sy, "pfld").go(pairs)
            step(y=ldgy)

            for mo in range(0, 8, 2):
                def ldbs(sy, mo=mo, hc0=hc0):
                    return DmaBatch(kb, sy, "pfld").go(
                        [(basef[j][:], g["base_dr"][128 * (mo + j):128 * (mo + j + 1),
                                                    hc0:hc0 + 1024]) for j in range(2)])
                step(y=ldbs)

                def mmat(tn, mo=mo):
                    m = None
                    for j in range(2):
                        for r in range(16):
                            for tt in range(2):
                                m = tn.matmul(PS(2 * j + tt), he[r][:, 128 * (mo + j):128 * (mo + j + 1)],
                                              gyr[r][:, 512 * tt:512 * (tt + 1)],
                                              start=(r == 0), stop=(r == 15))
                    return m
                step(t=mmat)

                def fsum(ve, mo=mo):
                    m = None
                    for j in range(2):
                        for tt in range(2):
                            ve.tensor_add(basef[j][:, 512 * tt:512 * (tt + 1)],
                                          basef[j][:, 512 * tt:512 * (tt + 1)],
                                          PS(2 * j + tt))
                        ve.drain()
                        m = ve.tensor_add(basef[j][:], basef[j][:], moe[mo + j][:])
                    return m
                step(v=fsum)

                def stout(sy, mo=mo, hc0=hc0):
                    pairs = [(g["outT"][128 * (mo + j):128 * (mo + j + 1), hc0:hc0 + 1024],
                              basef[j][:]) for j in range(2)]
                    return DmaBatch(kb, sy, "pfst").go(pairs)
                step(y=stout)

        with nc.Block() as blk:
            pipe.emit(blk, None)


def sb_gcb(sb):
    return sb["gcb_s"][0:1, 0:1]


def _stage_inputs(i):
    import ml_dtypes
    bf = ml_dtypes.bfloat16
    resid = i["L0_attn_resid"][0]
    cores = []
    eye128 = np.eye(128, dtype=bf)
    id8f = np.eye(8, dtype=np.float32)
    oh16 = np.zeros((NST, NST * 128), dtype=bf)
    for n in range(NST):
        oh16[n, 128 * n:128 * (n + 1)] = 1
    oh8 = np.zeros((8, 8 * 128), dtype=bf)
    for n in range(8):
        oh8[n, 128 * n:128 * (n + 1)] = 1
    ones1 = np.ones((128, 1), dtype=bf)
    ones1r = np.ones((1, 128), dtype=bf)
    cvals = [1e-5, 1.0 / D, -1.0, 1.0] + [1.0 / (k + 1) for k in range(20)] + [0.0] * 8
    cpool = np.tile(np.array(cvals, np.float32)[None, :], (128, 1))

    def pack_cols(v, ncol):
        return np.ascontiguousarray(v.reshape(ncol, 128).T)

    aT = lambda x: np.ascontiguousarray(np.asarray(x).T)
    C = np.ascontiguousarray
    for c in range(NCORES):
        tsl = slice(TC * c, TC * (c + 1))
        lsl = slice(LC * c, LC * (c + 1))
        m = {
            "residT": aT(resid[tsl]),
            "norm_w_t": pack_cols(i["L0_attn_norm_w"], 8),
            "ln_g_t": pack_cols(i["L0_ffn_ln_g"], 8),
            "ln_b_t": pack_cols(i["L0_ffn_ln_b"], 8),
            "gate_w_t": C(i["L0_ffn_gate_w"].reshape(8, 128, NEXP).transpose(1, 0, 2)
                          .reshape(128, 8 * NEXP)),
            "gate_b_t": C(i["L0_ffn_gate_b"].reshape(NEXP, 1)),
            "in_wT": aT(i["L0_attn_in_w"][lsl]),
            "skip_wT": aT(i["L0_attn_skip_w"][lsl]),
            "wd1T": aT(i["L0_attn_wd1"][:, lsl]),
            "wbT": aT(i["L0_attn_wb"][:, lsl]),
            "wcT": aT(i["L0_attn_wc"][:, lsl]),
            "wd2T": aT(i["L0_attn_wd2"][lsl]),
            "a_log_t": C(i["L0_attn_A_log"][lsl].reshape(2, 128, NST)
                         .transpose(1, 0, 2).reshape(128, 2 * NST)),
            "conv_w_t": C(i["L0_attn_conv_w"][lsl].reshape(2, 128, 4)
                          .transpose(1, 0, 2).reshape(128, 8)),
            "conv_b_t": pack_cols(i["L0_attn_conv_b"][lsl], 2),
            "wd2_b_t": pack_cols(i["L0_attn_wd2_b"][lsl], 2),
            "w_d_t": pack_cols(i["L0_attn_W_D"][lsl], 2),
            "out_wT": aT(i["L0_attn_out_w"]),
            "aw1": C(i["L0_ffn_aw1"]),
            "ab1_t": pack_cols(i["L0_ffn_ab1"], 32),
            "aw2": C(i["L0_ffn_aw2"]),
            "ab2_t": pack_cols(i["L0_ffn_ab2"], 8),
            "ew1": C(i["L0_ffn_ew1"]),
            "eb1_t": C(i["L0_ffn_eb1"].reshape(NEXP, 16, 128)
                       .transpose(2, 0, 1).reshape(128, NEXP * 16)),
            "ew2": C(i["L0_ffn_ew2"]),
            "eb2_n": C(i["L0_ffn_eb2"]),
            "gens_n": C(i["L0_ffn_gens"][c % 4]),
            "gcw_t": pack_cols(i["L0_ffn_gc_w"][:, 0], 16),
            "gc_b_t": C(i["L0_ffn_gc_b"].reshape(1, 1)),
            "ident": eye128, "id8f": id8f, "oh16": oh16, "oh8": oh8,
            "ones1": ones1, "ones1r": ones1r,
            "tok0": np.array([[TC * c]], np.int32),
            "cpool": cpool,
        }
        cores.append(m)
    return cores


_BUILT = {}


def kernel(**inputs):
    taps = tuple(TAPS)
    nc, tap_handles = build(taps)
    in_maps = _stage_inputs(inputs)
    res = run_bass_kernel_spmd(nc, in_maps, core_ids=list(range(NCORES)))
    outs = [np.asarray(r["outT"]) for r in res.results]
    full = np.concatenate([o.T for o in outs], axis=0)[None]
    if taps:
        kernel.tap_results = [{k: np.asarray(r["tap_" + k]) for k in taps}
                              for r in res.results]
    return np.ascontiguousarray(full, dtype=np.float32)


# version 25
# speedup vs baseline: 1.6010x; 1.0797x over previous
import numpy as np
from contextlib import ExitStack

import concourse.bass as bass
import concourse.mybir as mybir
from concourse.bass_utils import run_bass_kernel_spmd

F32 = mybir.dt.float32
BF16 = mybir.dt.bfloat16
I32 = mybir.dt.int32
OP = mybir.AluOpType
AF = mybir.ActivationFunctionType
AX = mybir.AxisListType

NCORES = 8
T = 16384
TC = T // NCORES
D = 1024
E2 = 2048
LC = E2 // NCORES
NST = 16
DTR = 64
A4 = 4096
H2 = 2048
NEXP = 6
NTERMS = 20
RG = [list(range(NCORES))]
NEG = -1e30

TAPS = ()


class Pipe:

    def __init__(self, kb, name, engines, leads=None):
        self.kb = kb
        self.name = name
        self.engines = list(engines)
        self.leads = leads or {}
        self.steps = []

    def step(self, **eng_fns):
        for e in eng_fns:
            assert e in self.engines, (self.name, e)
        self.steps.append(eng_fns)

    def emit(self, block, eng_objs):
        kb = self.kb
        nsteps = len(self.steps)
        done = {}
        for f in self.engines:
            c, arr = 0, []
            for s in range(nsteps):
                if f in self.steps[s]:
                    c += 1
                arr.append(c)
            done[f] = arr
        sems = {f: kb.sem(f"{self.name}_{f}") for f in self.engines}

        def body_for(e):
            def body(eng):
                my = 0
                for s in range(nsteps):
                    fn = self.steps[s].get(e)
                    if fn is None:
                        continue
                    for f in self.engines:
                        if f == e:
                            continue
                        j = s - 1 - self.leads.get((e, f), 0)
                        tgt = done[f][j] if j >= 0 else 0
                        if tgt > 0:
                            eng.wait_ge(sems[f], tgt)
                    last = fn(eng)
                    my += 1
                    if e in ("v", "s"):
                        eng.drain().then_inc(sems[e], 1)
                    elif last is not None:
                        last.then_inc(sems[e], 1)
                    else:
                        eng.sem_inc(sems[e], 1)
            return body

        for e in self.engines:
            getattr(block, {"t": "tensor", "v": "vector", "s": "scalar",
                            "y": "sync", "g": "gpsimd"}[e])(body_for(e))


class KB:
    def __init__(self, nc, ctx):
        self.nc = nc
        self.ctx = ctx
        self.sems = {}

    def sem(self, name):
        if name not in self.sems:
            self.sems[name] = self.ctx.enter_context(self.nc.semaphore(name))
        return self.sems[name]


def dma(eng, out, in_, kb, tag):
    sem = kb.sem(tag)
    n = kb.__dict__.setdefault("_dmacnt", {})
    n[tag] = n.get(tag, 0) + 16
    eng.dma_start(out=out, in_=in_).then_inc(sem, 16)
    return n[tag]


class DmaBatch:

    def __init__(self, kb, eng, tag):
        self.kb, self.eng, self.tag = kb, eng, tag

    def go(self, pairs):
        tgt = None
        for out, in_ in pairs:
            tgt = dma(self.eng, out, in_, self.kb, self.tag)
        self.eng.wait_ge(self.kb.sem(self.tag), tgt)
        return None


def build(taps=()):
    nc = bass.Bass("TRN2", target_bir_lowering=False)
    ctx = ExitStack()
    kb = KB(nc, ctx)
    g = {}

    def inp(name, shape, dtype=F32):
        g[name] = nc.dram_tensor(name, shape, dtype, kind="ExternalInput")

    inp("residT", [D, TC]); inp("norm_w_t", [128, 8]); inp("ln_g_t", [128, 8])
    inp("ln_b_t", [128, 8]); inp("gate_w_t", [128, 8 * NEXP]); inp("gate_b_t", [NEXP, 1])
    inp("in_wT", [D, LC]); inp("skip_wT", [D, LC]); inp("wd1T", [LC, DTR])
    inp("wbT", [LC, NST]); inp("wcT", [LC, NST]); inp("wd2T", [DTR, LC])
    inp("a_log_t", [128, 2 * NST]); inp("conv_w_t", [128, 2 * 4])
    inp("conv_b_t", [128, 2]); inp("wd2_b_t", [128, 2]); inp("w_d_t", [128, 2])
    inp("out_wT", [E2, D]); inp("aw1", [D, A4]); inp("ab1_t", [128, 32])
    inp("aw2", [A4, D]); inp("ab2_t", [128, 8]); inp("ew1", [NEXP, D, H2])
    inp("eb1_t", [128, NEXP * 16]); inp("ew2", [NEXP, H2, D]); inp("eb2_n", [NEXP, D])
    inp("gens_n", [D, D]); inp("gcw_t", [128, 16]); inp("gc_b_t", [1, 1])
    inp("ident", [128, 128], BF16); inp("id8f", [8, 8], F32)
    inp("oh16", [NST, NST * 128], BF16); inp("oh8", [8, 8 * 128], BF16)
    inp("ones1", [128, 1], BF16); inp("ones1r", [1, 128], BF16)
    inp("tok0", [1, 1], I32); inp("cpool", [128, 32], F32)

    g["outT"] = nc.dram_tensor("outT", [D, TC], F32, kind="ExternalOutput")

    def intern(name, shape, dtype):
        g[name] = nc.dram_tensor(name, shape, dtype, kind="Internal")

    intern("cc_xn2_in", [D * TC], BF16); intern("cc_xn2_out", [NCORES * D * TC], BF16)
    intern("cc_bcd_in", [96 * T], F32); intern("cc_bcd_out", [96 * T], F32)
    intern("cc_gy_in", [LC * T], BF16); intern("cc_gy_out", [NCORES * LC * T], BF16)
    intern("cc_t_in", [D * D], BF16); intern("cc_t_out", [NCORES * D * D], BF16)
    intern("base_dr", [D, TC], F32); intern("xnf_dr", [D, TC], BF16)
    intern("gyw_dr", [E2, TC], BF16)
    intern("bt_dr", [NST, T], BF16); intern("ct_dr", [NST, T], BF16)
    intern("d1b_dr", [DTR, T], BF16)
    intern("x_dr", [LC, T], F32); intern("sskip_dr", [LC, T], F32)
    intern("abel_dr", [D, TC], BF16)

    taps = tuple(taps)
    tap_handles = {}

    def tap(name, shape, dtype=F32):
        if name in taps:
            tap_handles[name] = nc.dram_tensor("tap_" + name, shape, dtype,
                                               kind="ExternalOutput")
            return tap_handles[name]
        return None

    with ctx:
        E = ctx.enter_context
        sb = {}

        def sbuf(name, shape, dtype=F32):
            sb[name] = E(nc.sbuf_tensor(name, shape, dtype))
            return sb[name]

        sbuf("ident_s", [128, 128], BF16); sbuf("id8f_s", [8, 8], F32)
        sbuf("ones1_s", [128, 1], BF16); sbuf("ones1r_s", [1, 128], BF16)
        sbuf("oh8_s", [8, 8 * 128], BF16)
        sbuf("nrm_s", [128, 8]); sbuf("lng_s", [128, 8]); sbuf("lnb_s", [128, 8])
        sbuf("gw_s", [128, 8 * NEXP]); sbuf("gb_s", [NEXP, 1])
        sbuf("w6_s", [8, TC], BF16)
        sbuf("alog_s", [128, 2 * NST]); sbuf("acoef_s", [128, 2 * NST])
        sbuf("cw_s", [128, 8]); sbuf("cb_s", [128, 2]); sbuf("wd2b_s", [128, 2])
        sbuf("wdt_s", [128, 2]); sbuf("tok0_s", [1, 1], I32)
        sbuf("cp_s", [128, 32]); sbuf("gcb_s", [1, 1])
        sbuf("eb1_s", [128, NEXP * 16]); sbuf("ab1_s", [128, 32]); sbuf("ab2_s", [128, 8])

        def CP(col, p=128):
            return sb["cp_s"][0:p, col:col + 1]
        g["CP"] = CP
        psum = E(nc.psum_tensor("psum", [128, 4096], F32))

        def PS(bank, n=512, p=128):
            return psum[0:p, 512 * bank: 512 * bank + n]

        g["PS"] = PS
        g["psum"] = psum
        g["sb"] = sb
        g["tap"] = tap

        with nc.Block() as blk:
            @blk.sync
            def _(sy):
                d = DmaBatch(kb, sy, "c0")
                d.go([(sb["ident_s"][:], g["ident"][:]), (sb["id8f_s"][:], g["id8f"][:]),
                      (sb["ones1_s"][:], g["ones1"][:]), (sb["ones1r_s"][:], g["ones1r"][:]),
                      (sb["oh8_s"][:], g["oh8"][:]),
                      (sb["nrm_s"][:], g["norm_w_t"][:]), (sb["lng_s"][:], g["ln_g_t"][:]),
                      (sb["lnb_s"][:], g["ln_b_t"][:]), (sb["gw_s"][:], g["gate_w_t"][:]),
                      (sb["gb_s"][:], g["gate_b_t"][:]), (sb["alog_s"][:], g["a_log_t"][:]),
                      (sb["cw_s"][:], g["conv_w_t"][:]), (sb["cb_s"][:], g["conv_b_t"][:]),
                      (sb["wd2b_s"][:], g["wd2_b_t"][:]), (sb["wdt_s"][:], g["w_d_t"][:]),
                      (sb["tok0_s"][:], g["tok0"][:]),
                      (sb["cp_s"][:], g["cpool"][:]),
                      (sb["gcb_s"][:], g["gc_b_t"][:]),
                      (sb["eb1_s"][:], g["eb1_t"][:]), (sb["ab1_s"][:], g["ab1_t"][:]),
                      (sb["ab2_s"][:], g["ab2_t"][:])])

            @blk.scalar
            def _(sc):
                sc.wait_ge(kb.sem("c0"), 17 * 16)
                sc.activation(out=sb["acoef_s"][:], in_=sb["alog_s"][:], func=AF.Exp)
                sc.activation(out=sb["acoef_s"][:], in_=sb["acoef_s"][:],
                              func=AF.Copy, scale=sb["cp_s"][:, 2:3])

        phase0(kb, g)
        phase_expm(kb, g)
        phase2(kb, g)
        phase3(kb, g)
        phase4(kb, g)
        phase_ffn(kb, g)

    return nc, tap_handles


def phase0(kb, g):
    nc = kb.nc
    sb = g["sb"]
    PS = g["PS"]
    tap = g["tap"]
    SP = 512
    NCH = TC // SP
    CP = g["CP"]
    t_xn = tap("xn", [D, TC]); t_base = tap("base", [D, TC])
    t_xn2 = tap("xn2", [D, TC], BF16); t_xnf = tap("xnf", [D, TC], BF16)
    t_logits = tap("logits", [8, TC]); t_w6 = tap("w6", [8, TC], BF16)

    with ExitStack() as ctx:
        E = ctx.enter_context
        rt = [E(nc.sbuf_tensor(f"p0rt{k}", [128, SP], F32)) for k in range(8)]
        xn = [E(nc.sbuf_tensor(f"p0xn{k}", [128, SP], F32)) for k in range(8)]
        xb = [E(nc.sbuf_tensor(f"p0xb{k}", [128, SP], BF16)) for k in range(8)]
        sq = [E(nc.sbuf_tensor(f"p0sq{k}", [128, SP], BF16)) for k in range(8)]
        x2b = [E(nc.sbuf_tensor(f"p0x2{k}", [128, SP], BF16)) for k in range(8)]
        xfb = [E(nc.sbuf_tensor(f"p0xf{k}", [128, SP], BF16)) for k in range(8)]
        xf32 = [E(nc.sbuf_tensor(f"p0xf32{k}", [128, SP], F32)) for k in range(8)]
        row = E(nc.sbuf_tensor("p0row", [1, 10 * SP], F32))
        rowb = E(nc.sbuf_tensor("p0rowb", [1, 10 * SP], BF16))
        lg6 = E(nc.sbuf_tensor("p0lg6", [8, SP], F32))
        ms8 = E(nc.sbuf_tensor("p0ms8", [128, 8], F32))
        mr1 = E(nc.sbuf_tensor("p0mr1", [128, 8], F32))
        mr2 = E(nc.sbuf_tensor("p0mr2", [128, 8], F32))
        lt0 = E(nc.sbuf_tensor("p0lt0", [128, 8], F32))
        lt1 = E(nc.sbuf_tensor("p0lt1", [128, 8], F32))
        lt2 = E(nc.sbuf_tensor("p0lt2", [128, 8], F32))
        mk1 = E(nc.sbuf_tensor("p0mk1", [128, 8], F32))
        wdd = E(nc.sbuf_tensor("p0wd", [128, 8], F32))
        mks = E(nc.sbuf_tensor("p0mks", [128, 32], F32))
        wds = E(nc.sbuf_tensor("p0wds", [128, 32], F32))
        qq = E(nc.sbuf_tensor("p0q", [128, 4], F32))
        wtok = E(nc.sbuf_tensor("p0wtok", [128, 32], F32))
        wtokb = E(nc.sbuf_tensor("p0wtokb", [128, 32], BF16))

        def R(slot, n=SP):
            return row[0:1, slot * SP: slot * SP + n]

        def Rb(slot, n=SP):
            return rowb[0:1, slot * SP: slot * SP + n]

        pipe = Pipe(kb, "p0", ["y", "t", "s", "v"])
        for i in range(NCH):
            c0, c1 = SP * i, SP * (i + 1)

            def ld(sy, i=i, c0=c0, c1=c1):
                return DmaBatch(kb, sy, "p0ld").go(
                    [(rt[k][:], g["residT"][128 * k:128 * (k + 1), c0:c1])
                     for k in range(8)])
            pipe.step(y=ld)
            def casts(sc):
                for k in range(8):
                    sc.activation(out=xb[k][:], in_=rt[k][:], func=AF.Copy)
                for k in range(7):
                    sc.activation(out=sq[k][:], in_=rt[k][:], func=AF.Square)
                return sc.activation(out=sq[7][:], in_=rt[7][:], func=AF.Square)
            pipe.step(s=casts)

            def sums(tn):
                for k in range(8):
                    tn.matmul(PS(0, SP, 1), sb["ones1_s"][:], xb[k][:],
                              start=(k == 0), stop=(k == 7))
                for k in range(8):
                    m = tn.matmul(PS(1, SP, 1), sb["ones1_s"][:], sq[k][:],
                                  start=(k == 0), stop=(k == 7))
                return m
            pipe.step(t=sums)

            def stat1(sc):
                sc.activation(out=R(0), in_=PS(0, SP, 1), func=AF.Copy,
                              scale=CP(1, 1))
                sc.activation(out=R(6), in_=PS(1, SP, 1), func=AF.Copy,
                              scale=CP(1, 1))
                return sc.activation(out=R(5), in_=R(0), func=AF.Square)
            pipe.step(s=stat1)

            def var1(ve):
                return ve.tensor_sub(R(5), R(6), R(5))
            pipe.step(v=var1)

            def stat2(sc):
                sc.activation(out=R(9), in_=R(5), func=AF.Sqrt, bias=CP(0, 1))
                return sc.activation(out=Rb(0), in_=R(0), func=AF.Copy)
            pipe.step(s=stat2)

            def lo1(ve):
                ve.reciprocal(out=R(1), in_=R(9))
                return ve.tensor_sub(R(5), R(0), Rb(0))
            pipe.step(v=lo1)

            def lo1c(sc):
                sc.activation(out=Rb(5), in_=R(5), func=AF.Copy)
                return sc.activation(out=Rb(1), in_=R(1), func=AF.Copy)
            pipe.step(s=lo1c)

            def lo1b(ve):
                return ve.tensor_sub(R(6), R(1), Rb(1))
            pipe.step(v=lo1b)

            def lo1d(sc):
                return sc.activation(out=Rb(6), in_=R(6), func=AF.Copy)
            pipe.step(s=lo1d)

            def bc1(tn):
                tn.matmul(PS(2, SP), sb["ones1r_s"][:], Rb(0), start=True, stop=False)
                tn.matmul(PS(2, SP), sb["ones1r_s"][:], Rb(5), start=False, stop=True)
                tn.matmul(PS(3, SP), sb["ones1r_s"][:], Rb(1), start=True, stop=False)
                return tn.matmul(PS(3, SP), sb["ones1r_s"][:], Rb(6), start=False, stop=True)
            pipe.step(t=bc1)

            def mkxn(ve):
                for k in range(8):
                    ve.tensor_sub(xn[k][:], rt[k][:], PS(2, SP))
                for k in range(8):
                    ve.tensor_mul(xn[k][:], xn[k][:], PS(3, SP))
                m = None
                for k in range(8):
                    m = ve.scalar_tensor_tensor(out=rt[k][:], in0=xn[k][:], scalar=2.0,
                                                in1=rt[k][:], op0=OP.mult, op1=OP.add)
                return m
            pipe.step(v=mkxn)

            def stbase(sy, i=i, c0=c0, c1=c1):
                pairs = [(g["base_dr"][128 * k:128 * (k + 1), c0:c1], rt[k][:])
                         for k in range(8)]
                if t_base is not None:
                    pairs += [(t_base[128 * k:128 * (k + 1), c0:c1], rt[k][:])
                              for k in range(8)]
                if t_xn is not None:
                    pairs += [(t_xn[128 * k:128 * (k + 1), c0:c1], xn[k][:])
                              for k in range(8)]
                return DmaBatch(kb, sy, "p0st").go(pairs)
            pipe.step(y=stbase)

            def casts2(sc):
                for k in range(8):
                    sc.activation(out=xb[k][:], in_=xn[k][:], func=AF.Copy)
                for k in range(8):
                    m = sc.activation(out=sq[k][:], in_=xn[k][:], func=AF.Square)
                return m
            pipe.step(s=casts2)
            pipe.step(t=sums)

            def stat3(sc):
                sc.activation(out=R(2), in_=PS(1, SP, 1), func=AF.Sqrt,
                              scale=CP(1, 1), bias=CP(0, 1))
                sc.activation(out=R(3), in_=PS(0, SP, 1), func=AF.Copy,
                              scale=CP(1, 1))
                sc.activation(out=R(6), in_=PS(1, SP, 1), func=AF.Copy,
                              scale=CP(1, 1))
                return sc.activation(out=R(5), in_=R(3), func=AF.Square)
            pipe.step(s=stat3)

            def var2(ve):
                ve.reciprocal(out=R(2), in_=R(2))
                return ve.tensor_sub(R(5), R(6), R(5))
            pipe.step(v=var2)

            def stat4(sc):
                sc.activation(out=R(9), in_=R(5), func=AF.Sqrt, bias=CP(0, 1))
                sc.activation(out=Rb(2), in_=R(2), func=AF.Copy)
                return sc.activation(out=Rb(3), in_=R(3), func=AF.Copy)
            pipe.step(s=stat4)

            def lo2(ve):
                ve.reciprocal(out=R(4), in_=R(9))
                ve.tensor_sub(R(5), R(2), Rb(2))
                return ve.tensor_sub(R(6), R(3), Rb(3))
            pipe.step(v=lo2)

            def lo2c(sc):
                sc.activation(out=Rb(5), in_=R(5), func=AF.Copy)
                sc.activation(out=Rb(6), in_=R(6), func=AF.Copy)
                return sc.activation(out=Rb(4), in_=R(4), func=AF.Copy)
            pipe.step(s=lo2c)

            def lo2b(ve):
                return ve.tensor_sub(R(7), R(4), Rb(4))
            pipe.step(v=lo2b)

            def lo2d(sc):
                return sc.activation(out=Rb(7), in_=R(7), func=AF.Copy)
            pipe.step(s=lo2d)

            def bc2(tn):
                tn.matmul(PS(2, SP), sb["ones1r_s"][:], Rb(2), start=True, stop=False)
                tn.matmul(PS(2, SP), sb["ones1r_s"][:], Rb(5), start=False, stop=True)
                tn.matmul(PS(3, SP), sb["ones1r_s"][:], Rb(3), start=True, stop=False)
                tn.matmul(PS(3, SP), sb["ones1r_s"][:], Rb(6), start=False, stop=True)
                tn.matmul(PS(4, SP), sb["ones1r_s"][:], Rb(4), start=True, stop=False)
                return tn.matmul(PS(4, SP), sb["ones1r_s"][:], Rb(7), start=False, stop=True)
            pipe.step(t=bc2)

            def mkx2f(ve):
                for k in range(8):
                    ve.scalar_tensor_tensor(out=x2b[k][:], in0=xn[k][:],
                                            scalar=sb["nrm_s"][:, k:k + 1],
                                            in1=PS(2, SP), op0=OP.mult, op1=OP.mult)
                for k in range(8):
                    ve.tensor_sub(xf32[k][:], xn[k][:], PS(3, SP))
                for k in range(8):
                    ve.tensor_mul(xf32[k][:], xf32[k][:], PS(4, SP))
                m = None
                for k in range(8):
                    ve.tensor_scalar(out=xfb[k][:], in0=xf32[k][:],
                                     scalar1=sb["lng_s"][:, k:k + 1],
                                     scalar2=sb["lnb_s"][:, k:k + 1],
                                     op0=OP.mult, op1=OP.add)
                    m = ve.tensor_scalar(out=xf32[k][:], in0=xf32[k][:],
                                         scalar1=sb["lng_s"][:, k:k + 1],
                                         scalar2=sb["lnb_s"][:, k:k + 1],
                                         op0=OP.mult, op1=OP.add)
                return m
            pipe.step(v=mkx2f)

            def stx2f(sy, i=i, c0=c0, c1=c1):
                pairs = []
                for k in range(8):
                    pairs.append((g["cc_xn2_in"].ap().rearrange(
                        "(d t) -> d t", d=D)[128 * k:128 * (k + 1), c0:c1], x2b[k][:]))
                    pairs.append((g["xnf_dr"][128 * k:128 * (k + 1), c0:c1], xfb[k][:]))
                if t_xn2 is not None:
                    pairs += [(t_xn2[128 * k:128 * (k + 1), c0:c1], x2b[k][:]) for k in range(8)]
                if t_xnf is not None:
                    pairs += [(t_xnf[128 * k:128 * (k + 1), c0:c1], xfb[k][:]) for k in range(8)]
                return DmaBatch(kb, sy, "p0st").go(pairs)
            pipe.step(y=stx2f)

            def glog(tn):
                for k in range(8):
                    m = tn.matmul(PS(6, SP, NEXP), sb["gw_s"][:, 6 * k:6 * (k + 1)],
                                  xf32[k][:], start=(k == 0), stop=(k == 7))
                return m
            pipe.step(t=glog)

            def gevac(ve):
                ve.memset(lg6[:], NEG)
                return ve.tensor_scalar(out=lg6[0:NEXP, :], in0=PS(6, SP, NEXP),
                                        scalar1=sb["gb_s"][:], scalar2=None, op0=OP.add)
            pipe.step(v=gevac)

            def gtr(tn):
                m = None
                for j in range(SP // 128):
                    m = tn.matmul(PS(7, 128, 128)[:, 8 * j:8 * (j + 1)][0:128, 0:8],
                                  lg6[:, 128 * j:128 * (j + 1)], sb["id8f_s"][:],
                                  start=True, stop=True)
                return m
            pipe.step(t=gtr)

            def top2b(ve):
                m = None
                for j in range(SP // 128):
                    pcol = PS(7, 128, 128)[:, 8 * j:8 * (j + 1)]
                    ve.tensor_copy(out=lt0[:], in_=pcol)
                    ve.drain()
                    ve.max(out=ms8[:], in_=lt0[:])
                    ve.memset(mr1[:], NEG)
                    ve.memset(mr2[:], NEG)
                    ve.drain()
                    ve.tensor_copy(out=mr1[:, 0:1], in_=ms8[:, 0:1])
                    ve.tensor_copy(out=mr2[:, 0:1], in_=ms8[:, 1:2])
                    ve.drain()
                    ve.match_replace(out=lt1[:], in_to_replace=mr1[:], in_values=lt0[:],
                                     imm_value=NEG)
                    ve.drain()
                    ve.match_replace(out=lt2[:], in_to_replace=mr2[:], in_values=lt1[:],
                                     imm_value=NEG)
                    ve.drain()
                    ve.tensor_tensor(out=mk1[:], in0=lt0[:], in1=lt1[:], op=OP.not_equal)
                    ve.tensor_tensor(out=wdd[:], in0=lt1[:], in1=lt2[:], op=OP.not_equal)
                    ve.drain()
                    ve.tensor_sub(wdd[:], wdd[:], mk1[:])
                    ve.drain()
                    ve.tensor_copy(out=mks[:, 8 * j:8 * (j + 1)], in_=mk1[:])
                    ve.tensor_copy(out=wds[:, 8 * j:8 * (j + 1)], in_=wdd[:])
                    m = ve.tensor_sub(qq[:, j:j + 1], ms8[:, 1:2], ms8[:, 0:1])
                return m
            pipe.step(v=top2b)

            def sigq(sc):
                return sc.activation(out=qq[:], in_=qq[:], func=AF.Sigmoid)
            pipe.step(s=sigq)

            def wcomb(ve):
                m = None
                for j in range(SP // 128):
                    m = ve.scalar_tensor_tensor(
                        out=wtok[:, 8 * j:8 * (j + 1)],
                        in0=wds[:, 8 * j:8 * (j + 1)],
                        scalar=qq[:, j:j + 1],
                        in1=mks[:, 8 * j:8 * (j + 1)],
                        op0=OP.mult, op1=OP.add)
                return m
            pipe.step(v=wcomb)

            def wcast(ve):
                return ve.tensor_copy(out=wtokb[:], in_=wtok[:])
            pipe.step(v=wcast)

            def wtr(tn):
                m = None
                for j in range(SP // 128):
                    m = tn.matmul(PS(6, 512)[0:8, 128 * j:128 * (j + 1)],
                                  wtokb[:, 8 * j:8 * (j + 1)], sb["ident_s"][:],
                                  start=True, stop=True)
                return m
            pipe.step(t=wtr)

            def wev(ve, i=i, c0=c0, c1=c1):
                return ve.tensor_copy(out=sb["w6_s"][:, c0:c1], in_=PS(6, SP, 8))
            pipe.step(v=wev)

            if t_logits is not None:
                def lgst(sy, i=i, c0=c0, c1=c1):
                    return DmaBatch(kb, sy, "p0st").go([(t_logits[:, c0:c1], lg6[:])])
                pipe.step(y=lgst)

        if t_w6 is not None:
            def w6st(sy):
                return DmaBatch(kb, sy, "p0st").go([(t_w6[:], sb["w6_s"][:])])
            pipe.step(y=w6st)

        with nc.Block() as blk:
            pipe.emit(blk, None)


def phase_expm(kb, g):
    nc = kb.nc
    sb = g["sb"]
    PS = g["PS"]
    CP = g["CP"]
    tap = g["tap"]
    t_tg = tap("tg", [D, D], BF16)

    with ExitStack() as ctx:
        E = ctx.enter_context
        gsf = [E(nc.sbuf_tensor(f"exgf{r}", [128, D], F32)) for r in range(8)]
        gsb = [E(nc.sbuf_tensor(f"exgb{r}", [128, D], BF16)) for r in range(8)]
        pb = [[E(nc.sbuf_tensor(f"expP{p}_{r}", [128, D], BF16)) for r in range(8)]
              for p in range(2)]

        leads = {(e, "g"): 100000 for e in "ytsv"}
        pipe = Pipe(kb, "pex", ["y", "t", "s", "v", "g"], leads=leads)

        def agx(gp):
            gp.collective_compute(
                "AllGather", OP.bypass, replica_groups=RG,
                ins=[g["cc_xn2_in"][:]], outs=[g["cc_xn2_out"][:]]).then_inc(
                kb.sem("agx"), 1)
            gp.wait_ge(kb.sem("agx"), 1)
            return None
        pipe.step(g=agx)

        def ldg(sy):
            return DmaBatch(kb, sy, "exld").go(
                [(gsf[r][:], g["gens_n"][128 * r:128 * (r + 1), :]) for r in range(8)])
        pipe.step(y=ldg)

        def cst(sc):
            m = None
            for r in range(8):
                m = sc.activation(out=gsb[r][:], in_=gsf[r][:], func=AF.Copy)
            return m
        pipe.step(s=cst)

        def initp(ve):
            par = NTERMS % 2
            for r in range(8):
                ve.memset(pb[par][r][:], 0.0)
            ve.drain()
            m = None
            for r in range(8):
                m = ve.tensor_copy(out=pb[par][r][:, 128 * r:128 * (r + 1)],
                                   in_=sb["ident_s"][:])
            return m
        pipe.step(v=initp)

        for k in range(NTERMS, 0, -1):
            src, dst = k % 2, 1 - (k % 2)
            for h in range(2):
                lo = 512 * h

                def mstep(tn, src=src, lo=lo):
                    m = None
                    for mo in range(8):
                        for r in range(8):
                            m = tn.matmul(PS(mo),
                                          gsb[r][:, 128 * mo:128 * (mo + 1)],
                                          pb[src][r][:, lo:lo + 512],
                                          start=(r == 0), stop=(r == 7))
                    return m
                pipe.step(t=mstep)

                def estep(sc, k=k, dst=dst, lo=lo):
                    m = None
                    for mo in range(8):
                        m = sc.activation(out=pb[dst][mo][:, lo:lo + 512],
                                          in_=PS(mo), func=AF.Copy, scale=CP(3 + k))
                    return m
                pipe.step(s=estep)

                def astep(ve, dst=dst, h=h):
                    m = None
                    for mo in range(4 * h, 4 * h + 4):
                        m = ve.tensor_tensor(
                            out=pb[dst][mo][:, 128 * mo:128 * (mo + 1)],
                            in0=pb[dst][mo][:, 128 * mo:128 * (mo + 1)],
                            in1=sb["ident_s"][:], op=OP.add)
                    return m
                pipe.step(v=astep)

        def stT(gp):
            pairs = [(g["cc_t_in"].ap().rearrange("(d n) -> d n", d=D)
                      [128 * r:128 * (r + 1), :], pb[0][r][:]) for r in range(8)]
            if t_tg is not None:
                pairs += [(t_tg[128 * r:128 * (r + 1), :], pb[0][r][:]) for r in range(8)]
            DmaBatch(kb, gp, "exst").go(pairs)
            gp.collective_compute(
                "AllGather", OP.bypass, replica_groups=RG,
                ins=[g["cc_t_in"][:]], outs=[g["cc_t_out"][:]]).then_inc(
                kb.sem("agt"), 1)
            gp.wait_ge(kb.sem("agt"), 1)
            return None
        pipe.step(g=stT)

        with nc.Block() as blk:
            pipe.emit(blk, None)


def phase2(kb, g):
    nc = kb.nc
    sb = g["sb"]
    PS = g["PS"]
    tap = g["tap"]
    E0 = kb.ctx.enter_context
    sb["rtot_s"] = E0(nc.sbuf_tensor("rtot_s", [128, 8 * D], BF16))
    t_rtot = tap("rtot", [128, 8 * D], BF16)
    t_x = tap("x_", [LC, T])
    t_ss = tap("sskip", [LC, T])

    ccx = g["cc_xn2_out"].ap().rearrange("(s d t) -> s d t", s=NCORES, d=D)
    cct = g["cc_t_out"].ap().rearrange("(s d n) -> s d n", s=NCORES, d=D)

    with ExitStack() as ctx:
        E = ctx.enter_context
        tz = [E(nc.sbuf_tensor(f"p2tz{r}", [128, D], BF16)) for r in range(8)]
        zb = [[E(nc.sbuf_tensor(f"p2z{p}_{r}", [128, D], BF16)) for r in range(8)]
              for p in range(2)]
        wstg = [E(nc.sbuf_tensor(f"p2wstg{r}", [128, LC], F32)) for r in range(8)]
        inwb = [E(nc.sbuf_tensor(f"p2inw{r}", [128, LC], BF16)) for r in range(8)]
        skwb = [E(nc.sbuf_tensor(f"p2skw{r}", [128, LC], BF16)) for r in range(8)]
        wd1b = [E(nc.sbuf_tensor(f"p2wd1{m}", [128, DTR], BF16)) for m in range(2)]
        wbb = [E(nc.sbuf_tensor(f"p2wbb{m}", [128, NST], BF16)) for m in range(2)]
        wcb = [E(nc.sbuf_tensor(f"p2wcb{m}", [128, NST], BF16)) for m in range(2)]
        xg = [E(nc.sbuf_tensor(f"p2xg{r}", [128, 512], BF16)) for r in range(8)]
        zext = [E(nc.sbuf_tensor(f"p2ze{m}", [128, 515], F32)) for m in range(2)]
        xc = [E(nc.sbuf_tensor(f"p2xc{m}", [128, 512], F32)) for m in range(2)]
        xf = [E(nc.sbuf_tensor(f"p2xf{m}", [128, 512], F32)) for m in range(2)]
        xbf = [E(nc.sbuf_tensor(f"p2xb{m}", [128, 512], BF16)) for m in range(2)]
        ssk = [E(nc.sbuf_tensor(f"p2ss{m}", [128, 512], F32)) for m in range(2)]
        pd1 = E(nc.sbuf_tensor("p2pd1", [DTR, 512], F32))
        pB = E(nc.sbuf_tensor("p2pB", [NST, 512], F32))
        pC = E(nc.sbuf_tensor("p2pC", [NST, 512], F32))

        pipe = Pipe(kb, "p2", ["y", "t", "s", "v"])

        def ldT(sy, blkid=3):
            return DmaBatch(kb, sy, "p2ld").go(
                [(tz[r][:], cct[blkid, 128 * r:128 * (r + 1), :]) for r in range(8)])
        pipe.step(y=ldT)

        for h in range(2):
            def ztr(tn, h=h):
                m = None
                for mo in range(8):
                    for j in range(4):
                        nc_ = 4 * h + j
                        m = tn.matmul(PS(mo)[:, 128 * j:128 * (j + 1)],
                                      tz[nc_][:, 128 * mo:128 * (mo + 1)],
                                      sb["ident_s"][:], start=True, stop=True)
                return m
            pipe.step(t=ztr)

            def zev(sc, h=h):
                m = None
                for mo in range(8):
                    m = sc.activation(out=zb[0][mo][:, 512 * h:512 * h + 512],
                                      in_=PS(mo), func=AF.Copy)
                return m
            pipe.step(s=zev)

        par = 0
        for stage, blkid in enumerate([2, 1, 0]):
            def ldTs(sy, blkid=blkid):
                return DmaBatch(kb, sy, "p2ld").go(
                    [(tz[r][:], cct[blkid, 128 * r:128 * (r + 1), :]) for r in range(8)])
            pipe.step(y=ldTs)
            last = stage == 2
            for h in range(2):
                def zmm(tn, par=par, h=h):
                    m = None
                    for mo in range(8):
                        for r in range(8):
                            m = tn.matmul(PS(mo),
                                          tz[r][:, 128 * mo:128 * (mo + 1)],
                                          zb[par][r][:, 512 * h:512 * h + 512],
                                          start=(r == 0), stop=(r == 7))
                    return m
                pipe.step(t=zmm)

                def zev2(sc, par=par, h=h, last=last):
                    m = None
                    for mo in range(8):
                        if last:
                            dst = sb["rtot_s"][:, D * mo + 512 * h: D * mo + 512 * h + 512]
                        else:
                            dst = zb[1 - par][mo][:, 512 * h:512 * h + 512]
                        m = sc.activation(out=dst, in_=PS(mo), func=AF.Copy)
                    return m
                pipe.step(s=zev2)
            par = 1 - par

        def ldw1(sy):
            return DmaBatch(kb, sy, "p2ld").go(
                [(wstg[r][:], g["in_wT"][128 * r:128 * (r + 1), :]) for r in range(8)])
        pipe.step(y=ldw1)

        def cw1(sc):
            m = None
            for r in range(8):
                m = sc.activation(out=inwb[r][:], in_=wstg[r][:], func=AF.Copy)
            return m
        pipe.step(s=cw1)

        def ldw2(sy):
            return DmaBatch(kb, sy, "p2ld").go(
                [(wstg[r][:], g["skip_wT"][128 * r:128 * (r + 1), :]) for r in range(8)])
        pipe.step(y=ldw2)

        def cw2(sc):
            m = None
            for r in range(8):
                m = sc.activation(out=skwb[r][:], in_=wstg[r][:], func=AF.Copy)
            return m
        pipe.step(s=cw2)

        def ldw3(sy):
            return DmaBatch(kb, sy, "p2ld").go(
                [(wstg[0][:, 0:DTR], g["wd1T"][0:128, :]),
                 (wstg[1][:, 0:DTR], g["wd1T"][128:256, :]),
                 (wstg[2][:, 0:NST], g["wbT"][0:128, :]),
                 (wstg[3][:, 0:NST], g["wbT"][128:256, :]),
                 (wstg[4][:, 0:NST], g["wcT"][0:128, :]),
                 (wstg[5][:, 0:NST], g["wcT"][128:256, :])])
        pipe.step(y=ldw3)

        def cw3(sc):
            for m in range(2):
                sc.activation(out=wd1b[m][:], in_=wstg[m][:, 0:DTR], func=AF.Copy)
                sc.activation(out=wbb[m][:], in_=wstg[2 + m][:, 0:NST], func=AF.Copy)
                sc.activation(out=wcb[m][:], in_=wstg[4 + m][:, 0:NST], func=AF.Copy)
            return sc.activation(out=wcb[1][:], in_=wstg[5][:, 0:NST], func=AF.Copy)
        pipe.step(s=cw3)

        def zinit(ve):
            ve.memset(zext[0][:, 0:3], 0.0)
            return ve.memset(zext[1][:, 0:3], 0.0)
        pipe.step(v=zinit)

        NSEG = T // 512
        for i in range(NSEG):
            src, col = (512 * i) // TC, (512 * i) % TC

            def ldx(sy, src=src, col=col):
                return DmaBatch(kb, sy, "p2ld").go(
                    [(xg[r][:], ccx[src, 128 * r:128 * (r + 1), col:col + 512])
                     for r in range(8)])
            pipe.step(y=ldx)

            def zmm2(tn):
                m = None
                for mo in range(2):
                    for r in range(8):
                        tn.matmul(PS(mo), inwb[r][:, 128 * mo:128 * (mo + 1)],
                                  xg[r][:], start=(r == 0), stop=(r == 7))
                for mo in range(2):
                    for r in range(8):
                        m = tn.matmul(PS(2 + mo), skwb[r][:, 128 * mo:128 * (mo + 1)],
                                      xg[r][:], start=(r == 0), stop=(r == 7))
                return m
            pipe.step(t=zmm2)

            def zevac(sc):
                for m in range(2):
                    sc.activation(out=zext[m][:, 3:515], in_=PS(m), func=AF.Copy)
                sc.activation(out=ssk[0][:], in_=PS(2), func=AF.Silu)
                return sc.activation(out=ssk[1][:], in_=PS(3), func=AF.Silu)
            pipe.step(s=zevac)

            def conv(ve):
                m = None
                for mm in range(2):
                    ve.tensor_scalar(out=xc[mm][:], in0=zext[mm][:, 3:515],
                                     scalar1=sb["cw_s"][:, 4 * mm + 3:4 * mm + 4],
                                     scalar2=sb["cb_s"][:, mm:mm + 1],
                                     op0=OP.mult, op1=OP.add)
                    for kk in range(3):
                        ve.scalar_tensor_tensor(
                            out=xc[mm][:], in0=zext[mm][:, kk:kk + 512],
                            scalar=sb["cw_s"][:, 4 * mm + kk:4 * mm + kk + 1],
                            in1=xc[mm][:], op0=OP.mult, op1=OP.add)
                    ve.drain()
                    m = ve.tensor_copy(out=zext[mm][:, 0:3], in_=zext[mm][:, 512:515])
                return m
            pipe.step(v=conv)

            def silux(sc):
                for m in range(2):
                    sc.activation(out=xf[m][:], in_=xc[m][:], func=AF.Silu)
                for m in range(2):
                    sc.activation(out=xbf[m][:], in_=xc[m][:], func=AF.Silu)
                return sc.activation(out=xbf[1][:], in_=xc[1][:], func=AF.Silu)
            pipe.step(s=silux)

            def pmm(tn):
                for m in range(2):
                    tn.matmul(PS(4, 512, DTR), wd1b[m][:], xbf[m][:],
                              start=(m == 0), stop=(m == 1))
                for m in range(2):
                    tn.matmul(PS(5, 512, NST), wbb[m][:], xbf[m][:],
                              start=(m == 0), stop=(m == 1))
                m2 = None
                for m in range(2):
                    m2 = tn.matmul(PS(6, 512, NST), wcb[m][:], xbf[m][:],
                                   start=(m == 0), stop=(m == 1))
                return m2
            pipe.step(t=pmm)

            def pev(ve):
                ve.tensor_copy(out=pd1[:], in_=PS(4, 512, DTR))
                ve.tensor_copy(out=pB[:], in_=PS(5, 512, NST))
                return ve.tensor_copy(out=pC[:], in_=PS(6, 512, NST))
            pipe.step(v=pev)

            def stx(sy, i=i):
                c0 = 512 * i
                bcd = g["cc_bcd_in"].ap().rearrange("(r t) -> r t", r=96)
                pairs = [(g["x_dr"][128 * m:128 * (m + 1), c0:c0 + 512], xf[m][:])
                         for m in range(2)]
                pairs += [(g["sskip_dr"][128 * m:128 * (m + 1), c0:c0 + 512], ssk[m][:])
                          for m in range(2)]
                pairs += [(bcd[0:DTR, c0:c0 + 512], pd1[:]),
                          (bcd[DTR:DTR + NST, c0:c0 + 512], pB[:]),
                          (bcd[DTR + NST:96, c0:c0 + 512], pC[:])]
                if t_x is not None:
                    pairs += [(t_x[128 * m:128 * (m + 1), c0:c0 + 512], xf[m][:])
                              for m in range(2)]
                if t_ss is not None:
                    pairs += [(t_ss[128 * m:128 * (m + 1), c0:c0 + 512], ssk[m][:])
                              for m in range(2)]
                return DmaBatch(kb, sy, "p2st").go(pairs)
            pipe.step(y=stx)

        if t_rtot is not None:
            def strt(sy):
                return DmaBatch(kb, sy, "p2st").go([(t_rtot[:], sb["rtot_s"][:])])
            pipe.step(y=strt)

        with nc.Block() as blk:
            pipe.emit(blk, None)


def phase3(kb, g):
    nc = kb.nc
    tap = g["tap"]
    t_bcd = tap("bcd", [96, T])
    with nc.Block() as blk:
        @blk.gpsimd
        def _(gp):
            gp.collective_compute(
                "AllReduce", OP.add, replica_groups=RG,
                ins=[g["cc_bcd_in"][:]], outs=[g["cc_bcd_out"][:]]).then_inc(
                kb.sem("p3g"), 1)
            gp.wait_ge(kb.sem("p3g"), 1)
            if t_bcd is not None:
                tgt = dma(gp, t_bcd[:],
                          g["cc_bcd_out"].ap().rearrange("(r t) -> r t", r=96),
                          kb, "p3st")
                gp.wait_ge(kb.sem("p3st"), tgt)


def phase4(kb, g):
    nc = kb.nc
    sb = g["sb"]
    PS = g["PS"]
    tap = g["tap"]
    t_y = tap("yscan", [LC, T])
    t_gy = tap("gy", [LC, T], BF16)
    t_dl = tap("delta", [LC, T])
    t_d2 = tap("d2raw", [LC, T])
    t_hb = tap("hb", [128, NST * 2 * 512])

    bcd = g["cc_bcd_out"].ap().rearrange("(r t) -> r t", r=96)
    ggy = g["cc_gy_in"].ap().rearrange("(l t) -> l t", l=LC)
    S = 512
    NSEG = T // S

    with ExitStack() as ctx:
        E = ctx.enter_context
        oh16_s = E(nc.sbuf_tensor("p4oh16", [NST, NST * 128], BF16))
        wd2f = E(nc.sbuf_tensor("p4wd2f", [DTR, LC], F32))
        wd2b = E(nc.sbuf_tensor("p4wd2b", [DTR, LC], BF16))
        d1f = E(nc.sbuf_tensor("p4d1f", [DTR, S], F32))
        d1b = E(nc.sbuf_tensor("p4d1b", [DTR, S], BF16))
        btf = E(nc.sbuf_tensor("p4btf", [NST, S], F32))
        btb = E(nc.sbuf_tensor("p4btb", [NST, S], BF16))
        ctf = E(nc.sbuf_tensor("p4ctf", [NST, S], F32))
        ctb = E(nc.sbuf_tensor("p4ctb", [NST, S], BF16))
        dlt = [E(nc.sbuf_tensor(f"p4dl{m}", [128, S], F32)) for m in range(2)]
        sr1 = [E(nc.sbuf_tensor(f"p4sr{m}", [128, S], F32)) for m in range(2)]
        d2r = [E(nc.sbuf_tensor(f"p4d2r{m}", [128, S], F32)) for m in range(2)]
        sa1 = [E(nc.sbuf_tensor(f"p4sa{m}", [128, S], F32)) for m in range(2)]
        dx = [E(nc.sbuf_tensor(f"p4dx{m}", [128, S], F32)) for m in range(2)]
        xw = [E(nc.sbuf_tensor(f"p4xw{m}", [128, S], F32)) for m in range(2)]
        ssk = [E(nc.sbuf_tensor(f"p4ss{m}", [128, S], F32)) for m in range(2)]
        aa = E(nc.sbuf_tensor("p4aa", [128, 16 * S], F32))
        bb = [E(nc.sbuf_tensor(f"p4bb{m}", [128, S], F32)) for m in range(2)]
        hbuf = E(nc.sbuf_tensor("p4hb", [128, NST * 2 * S], F32))
        state = E(nc.sbuf_tensor("p4st", [128, 32], F32))
        yy = [E(nc.sbuf_tensor(f"p4yy{m}", [128, S], F32)) for m in range(2)]
        gyb = [E(nc.sbuf_tensor(f"p4gy{m}", [128, S], BF16)) for m in range(2)]

        hb3 = hbuf.ap().rearrange("p (n q) -> p n q", q=2 * S)
        hb3r = hbuf.ap().rearrange("p (n q) -> p q n", q=2 * S)

        pipe = Pipe(kb, "p4", ["y", "t", "s", "v"])

        def ldw(sy):
            return DmaBatch(kb, sy, "p4ld").go([(wd2f[:], g["wd2T"][:]),
                                                (oh16_s[:], g["oh16"][:])])
        pipe.step(y=ldw)

        def cwz(sc):
            return sc.activation(out=wd2b[:], in_=wd2f[:], func=AF.Copy)
        pipe.step(s=cwz)

        def sinit(ve):
            return ve.memset(state[:], 0.0)
        pipe.step(v=sinit)

        for i in range(NSEG):
            c0 = S * i

            def ld(sy, c0=c0):
                return DmaBatch(kb, sy, "p4ld").go(
                    [(d1f[:], bcd[0:DTR, c0:c0 + S]),
                     (btf[:], bcd[DTR:DTR + NST, c0:c0 + S]),
                     (ctf[:], bcd[DTR + NST:96, c0:c0 + S]),
                     (xw[0][:], g["x_dr"][0:128, c0:c0 + S]),
                     (xw[1][:], g["x_dr"][128:256, c0:c0 + S]),
                     (ssk[0][:], g["sskip_dr"][0:128, c0:c0 + S]),
                     (ssk[1][:], g["sskip_dr"][128:256, c0:c0 + S])])
            pipe.step(y=ld)

            def cst(sc):
                sc.activation(out=d1b[:], in_=d1f[:], func=AF.Copy)
                sc.activation(out=btb[:], in_=btf[:], func=AF.Copy)
                return sc.activation(out=ctb[:], in_=ctf[:], func=AF.Copy)
            pipe.step(s=cst)

            def d2mm(tn):
                m = None
                for et in range(2):
                    m = tn.matmul(PS(et), wd2b[:, 128 * et:128 * (et + 1)],
                                  d1b[:], start=True, stop=True)
                return m
            pipe.step(t=d2mm)

            def spl(sc):
                m = None
                for et in range(2):
                    if t_d2 is not None:
                        sc.activation(out=d2r[et][:], in_=PS(et), func=AF.Copy)
                    sc.activation(out=sr1[et][:], in_=PS(et), func=AF.Relu,
                                  bias=sb["wd2b_s"][:, et:et + 1])
                    sc.activation(out=sa1[et][:], in_=PS(et), func=AF.Abs,
                                  bias=sb["wd2b_s"][:, et:et + 1])
                    sc.activation(out=sa1[et][:], in_=sa1[et][:], func=AF.Exp,
                                  scale=CP(2, 1) if False else sb["cp_s"][:, 2:3])
                    m = sc.activation(out=sa1[et][:], in_=sa1[et][:], func=AF.Ln,
                                      bias=sb["cp_s"][:, 3:4])
                return m
            pipe.step(s=spl)

            def mdx(ve):
                ve.tensor_add(dlt[0][:], sr1[0][:], sa1[0][:])
                ve.tensor_add(dlt[1][:], sr1[1][:], sa1[1][:])
                ve.drain()
                ve.tensor_mul(dx[0][:], dlt[0][:], xw[0][:])
                return ve.tensor_mul(dx[1][:], dlt[1][:], xw[1][:])
            pipe.step(v=mdx)

            def reps2(tn, i, gp):
                m = None
                b0 = 4 * ((gp + 1) % 2)
                for j2 in range(2):
                    n = 2 * gp + j2
                    m = tn.matmul(PS(b0 + 2 * j2), oh16_s[:, 128 * n:128 * (n + 1)],
                                  btb[:], start=True, stop=True)
                    m = tn.matmul(PS(b0 + 2 * j2 + 1), oh16_s[:, 128 * n:128 * (n + 1)],
                                  ctb[:], start=True, stop=True)
                return m

            def aexp2(sc, i, gp):
                m = None
                for j2 in range(2):
                    n = 2 * gp + j2
                    for et in range(2):
                        sl = (gp % 2) * 4 + 2 * j2 + et
                        m = sc.activation(out=aa[:, sl * S:(sl + 1) * S],
                                          in_=dlt[et][:], func=AF.Exp,
                                          scale=sb["acoef_s"][:, 16 * et + n:16 * et + n + 1])
                return m

            def scans2(ve, i, gp):
                b0 = 4 * ((gp + 1) % 2)
                for j2 in range(2):
                    n = 2 * gp + j2
                    for et in range(2):
                        hcol = n * 2 * S + et * S
                        sl = (gp % 2) * 4 + 2 * j2 + et
                        ve.tensor_mul(bb[et][:], dx[et][:], PS(b0 + 2 * j2))
                        ve.tensor_tensor_scan(
                            out=hbuf[:, hcol:hcol + S],
                            data0=aa[:, sl * S:(sl + 1) * S],
                            data1=bb[et][:],
                            initial=state[:, 16 * et + n:16 * et + n + 1],
                            op0=OP.mult, op1=OP.add)
                ve.drain()
                for et in range(2):
                    ve.tensor_copy(out=state[:, 16 * et + 2 * gp:16 * et + 2 * gp + 2],
                                   in_=hb3[:, 2 * gp:2 * gp + 2, et * S + S - 1:et * S + S])
                ve.drain()
                m = None
                for j2 in range(2):
                    n = 2 * gp + j2
                    for et in range(2):
                        hcol = n * 2 * S + et * S
                        m = ve.tensor_tensor(out=hbuf[:, hcol:hcol + S],
                                             in0=hbuf[:, hcol:hcol + S],
                                             in1=PS(b0 + 2 * j2 + 1), op=OP.mult)
                return m

            pipe.step(t=(lambda tn, i=i: reps2(tn, i, 0)),
                      s=(lambda sc, i=i: aexp2(sc, i, 0)))
            for gp in range(8):
                fns = {"v": (lambda ve, i=i, gp=gp: scans2(ve, i, gp))}
                if gp < 7:
                    fns["t"] = (lambda tn, i=i, gp=gp: reps2(tn, i, gp + 1))
                    fns["s"] = (lambda sc, i=i, gp=gp: aexp2(sc, i, gp + 1))
                pipe.step(**fns)

            def redu(ve):
                m = None
                for et in range(2):
                    m = ve.reduce_sum(out=yy[et][:],
                                      in_=hb3r[:, et * S:(et + 1) * S, :],
                                      axis=AX.X)
                return m
            pipe.step(v=redu)

            def gate(ve):
                m = None
                for et in range(2):
                    ve.scalar_tensor_tensor(out=yy[et][:], in0=xw[et][:],
                                            scalar=sb["wdt_s"][:, et:et + 1],
                                            in1=yy[et][:], op0=OP.mult, op1=OP.add)
                    m = ve.tensor_mul(gyb[et][:], yy[et][:], ssk[et][:])
                return m
            pipe.step(v=gate)

            def stg(sy, i=i, c0=c0):
                pairs = [(ggy[128 * et:128 * (et + 1), c0:c0 + S], gyb[et][:])
                         for et in range(2)]
                if t_dl is not None:
                    pairs += [(t_dl[128 * et:128 * (et + 1), c0:c0 + S], dlt[et][:])
                              for et in range(2)]
                if t_d2 is not None:
                    pairs += [(t_d2[128 * et:128 * (et + 1), c0:c0 + S], d2r[et][:])
                              for et in range(2)]
                if t_hb is not None and i == 0:
                    pairs += [(t_hb[:], hbuf[:])]
                if t_y is not None:
                    pairs += [(t_y[128 * et:128 * (et + 1), c0:c0 + S], yy[et][:])
                              for et in range(2)]
                if t_gy is not None:
                    pairs += [(t_gy[128 * et:128 * (et + 1), c0:c0 + S], gyb[et][:])
                              for et in range(2)]
                return DmaBatch(kb, sy, "p4st").go(pairs)
            pipe.step(y=stg)

        with nc.Block() as blk:
            pipe.emit(blk, None)

        with nc.Block() as blk2:
            @blk2.gpsimd
            def _(gp):
                gp.collective_compute(
                    "AllGather", OP.bypass, replica_groups=RG,
                    ins=[g["cc_gy_in"][:]], outs=[g["cc_gy_out"][:]]).then_inc(
                    kb.sem("aggy"), 1)
                gp.wait_ge(kb.sem("aggy"), 1)
                v = gp.value_load(sb["tok0_s"][0:1, 0:1])
                gg = g["cc_gy_out"].ap().rearrange("(r t) -> r t", r=NCORES * LC)
                tgt = None
                for r in range(16):
                    tgt = dma(gp, g["gyw_dr"][128 * r:128 * (r + 1), :],
                              gg[128 * r:128 * (r + 1), bass.ds(v, TC)], kb, "gyw")
                gp.wait_ge(kb.sem("gyw"), tgt)


def phase_ffn(kb, g):
    nc = kb.nc
    sb = g["sb"]
    PS = g["PS"]
    tap = g["tap"]
    t_abel = tap("abel", [D, TC], BF16)
    t_nona = tap("nona", [D, TC], BF16)
    t_moe2 = tap("moe2", [D, TC], BF16)
    t_rt2 = tap("rtot2", [128, 8 * D], BF16)

    with ExitStack() as ctx:
        E = ctx.enter_context
        xnfb = [E(nc.sbuf_tensor(f"pfxn{r}", [128, 1024], BF16)) for r in range(8)]
        wkf = [E(nc.sbuf_tensor(f"pfwf{p}", [128, 2048], F32)) for p in range(2)]
        wkb = [E(nc.sbuf_tensor(f"pfwb{p}", [128, 2048], BF16)) for p in range(2)]
        he = [E(nc.sbuf_tensor(f"pfhe{r}", [128, 1024], BF16)) for r in range(16)]
        moe = [E(nc.sbuf_tensor(f"pfmo{r}", [128, 1024], F32)) for r in range(8)]
        nonab = [E(nc.sbuf_tensor(f"pfnb{r}", [128, 1024], BF16)) for r in range(8)]
        abelb = [E(nc.sbuf_tensor(f"pfab{r}", [128, 1024], BF16)) for r in range(8)]
        basef = [E(nc.sbuf_tensor(f"pfbs{r}", [128, 1024], F32)) for r in range(2)]
        wrepb = E(nc.sbuf_tensor("pfwr", [128, 1024], BF16))
        eb2b = E(nc.sbuf_tensor("pfeb2", [NEXP, D], BF16))
        gcwb = E(nc.sbuf_tensor("pfgcw", [128, 16], BF16))
        grow = E(nc.sbuf_tensor("pfgrow", [1, 1024], F32))
        ghi = E(nc.sbuf_tensor("pfghi", [1, 1024], BF16))
        glo = E(nc.sbuf_tensor("pfglo", [1, 1024], F32))
        glob = E(nc.sbuf_tensor("pfglob", [1, 1024], BF16))
        gyr = abelb + nonab

        pipe = Pipe(kb, "pf", ["y", "t", "s", "v"])
        step = pipe.step

        def ldsm(sy):
            return DmaBatch(kb, sy, "pfld").go([(basef[0][0:NEXP, :], g["eb2_n"][:]),
                                                (basef[1][0:128, 0:16], g["gcw_t"][:])])
        step(y=ldsm)

        def cssm(sc):
            sc.activation(out=eb2b[:], in_=basef[0][0:NEXP, :], func=AF.Copy)
            return sc.activation(out=gcwb[:], in_=basef[1][:, 0:16], func=AF.Copy)
        step(s=cssm)

        for H in range(2):
            hc0 = 1024 * H

            def ldxn(sy, hc0=hc0):
                return DmaBatch(kb, sy, "pfld").go(
                    [(xnfb[r][:], g["xnf_dr"][128 * r:128 * (r + 1), hc0:hc0 + 1024])
                     for r in range(8)])
            step(y=ldxn)

            units = []
            raw = []
            for e in range(NEXP):
                raw.append(("wrep", e))
                for mt in range(16):
                    raw.append(("w1", e, mt))
                raw.append(("wsc", e))
                for mo in range(8):
                    raw.append(("w2", e, mo))
            raw.append(("csmoe",))
            for mo in range(8):
                raw.append(("na", mo))
            raw.append(("tapmoe",))
            for q in range(2):
                for mt in range(32):
                    raw.append(("a1", q, mt))
                for mo in range(8):
                    raw.append(("a2a", q, mo))
                    raw.append(("a2b", q, mo))
            prev = None
            for u in raw:
                grp = u[0] if u[0] not in ("a2a", "a2b") else "a2"
                if prev is not None and grp != prev:
                    units.append(("pad",))
                    units.append(("pad",))
                units.append(u)
                prev = grp
            units.append(("pad",))
            units.append(("pad",))

            ew1r = g["ew1"].ap().rearrange("e (r p) h -> e p r h", p=128)
            ew2r = g["ew2"].ap().rearrange("e (r p) d -> e p r d", p=128)
            aw1r = g["aw1"].ap().rearrange("(r p) h -> p r h", p=128)
            aw2r = g["aw2"].ap().rearrange("(r p) d -> p r d", p=128)

            def u_load(sy, u, p, hc0=hc0):
                kind = u[0]
                d = DmaBatch(kb, sy, "pfld")
                wk3 = wkf[p].ap().rearrange("q (r c) -> q r c", c=128)
                if kind == "w1":
                    _, e, mt = u
                    return d.go([(wk3[:, 4 * j:4 * (j + 1), :],
                                  ew1r[e, :, 4 * j:4 * (j + 1), 128 * mt:128 * (mt + 1)])
                                 for j in range(2)])
                if kind == "w2":
                    _, e, mo = u
                    return d.go([(wk3[:, 8 * j:8 * (j + 1), :],
                                  ew2r[e, :, 8 * j:8 * (j + 1), 128 * mo:128 * (mo + 1)])
                                 for j in range(2)])
                if kind == "a1":
                    _, q_, mt = u
                    return d.go([(wk3[:, 4 * j:4 * (j + 1), :],
                                  aw1r[:, 4 * j:4 * (j + 1), 128 * mt:128 * (mt + 1)])
                                 for j in range(2)])
                if kind == "a2a":
                    _, q_, mo = u
                    return d.go([(wk3[:, 8 * j:8 * (j + 1), :],
                                  aw2r[:, 8 * j:8 * (j + 1), 128 * mo:128 * (mo + 1)])
                                 for j in range(2)])
                if kind == "a2b":
                    _, q_, mo = u
                    return d.go([(wk3[:, 8 * j:8 * (j + 1), :],
                                  aw2r[:, 16 + 8 * j:16 + 8 * (j + 1), 128 * mo:128 * (mo + 1)])
                                 for j in range(2)])
                return None

            def u_cast(sc, u, p):
                kind = u[0]
                if kind in ("w1", "a1"):
                    return sc.activation(out=wkb[p][:, 0:1024], in_=wkf[p][:, 0:1024],
                                         func=AF.Copy)
                if kind in ("w2", "a2a", "a2b"):
                    return sc.activation(out=wkb[p][:], in_=wkf[p][:], func=AF.Copy)
                return None

            def u_mm(tn, u, p, hc0=hc0):
                kind = u[0]
                m = None
                if kind == "wrep":
                    e = u[1]
                    for tt in range(2):
                        m = tn.matmul(PS(4 + tt), sb["oh8_s"][:, 128 * e:128 * (e + 1)],
                                      sb["w6_s"][:, hc0 + 512 * tt:hc0 + 512 * (tt + 1)],
                                      start=True, stop=True)
                elif kind == "w1":
                    bk = 2 * (p % 2)
                    for r in range(8):
                        for tt in range(2):
                            m = tn.matmul(PS(bk + tt), wkb[p][:, 128 * r:128 * (r + 1)],
                                          xnfb[r][:, 512 * tt:512 * (tt + 1)],
                                          start=(r == 0), stop=(r == 7))
                elif kind == "w2":
                    _, e, mo = u
                    bk = 4 + 2 * (p % 2)
                    for r in range(16):
                        for tt in range(2):
                            m = tn.matmul(PS(bk + tt), wkb[p][:, 128 * r:128 * (r + 1)],
                                          he[r][:, 512 * tt:512 * (tt + 1)],
                                          start=(r == 0),
                                          stop=(r == 15 and e != 0))
                    if e == 0:
                        for tt in range(2):
                            m = tn.matmul(
                                PS(bk + tt), eb2b[0:NEXP, 128 * mo:128 * (mo + 1)],
                                sb["w6_s"][0:NEXP, hc0 + 512 * tt:hc0 + 512 * (tt + 1)],
                                start=False, stop=True)
                elif kind == "na":
                    mo = u[1]
                    bk = 4 + 2 * (mo % 2)
                    for tt in range(2):
                        for r in range(8):
                            m = tn.matmul(PS(bk + tt),
                                          sb["rtot_s"][:, D * r + 128 * mo:D * r + 128 * (mo + 1)],
                                          gyr[r][:, 512 * tt:512 * (tt + 1)],
                                          start=(r == 0), stop=(r == 7))
                elif kind == "a1":
                    _, q, mt = u
                    bk = 2 * (p % 2)
                    for r in range(8):
                        m = tn.matmul(PS(bk), wkb[p][:, 128 * r:128 * (r + 1)],
                                      xnfb[r][:, 512 * q:512 * (q + 1)],
                                      start=(r == 0), stop=(r == 7))
                elif kind == "a2a":
                    _, q, mo = u
                    bk = 4 + 2 * (mo % 2)
                    for r in range(16):
                        m = tn.matmul(PS(bk), wkb[p][:, 128 * r:128 * (r + 1)],
                                      he[r % 16][:, 512 * (r // 16):512 * (r // 16) + 512],
                                      start=(r == 0), stop=False)
                elif kind == "a2b":
                    _, q, mo = u
                    bk = 4 + 2 * (mo % 2)
                    for r in range(16):
                        m = tn.matmul(PS(bk), wkb[p][:, 128 * r:128 * (r + 1)],
                                      he[r % 16][:, 512 * (1):512 * (1) + 512]
                                      if False else he[(16 + r) % 16][:, 512:1024],
                                      start=False, stop=(r == 15))
                return m

            def u_post_s(sc, u, p, hc0=hc0):
                kind = u[0]
                m = None
                if kind == "wrep":
                    sc.activation(out=wrepb[:, 0:512], in_=PS(4), func=AF.Copy)
                    m = sc.activation(out=wrepb[:, 512:1024], in_=PS(5), func=AF.Copy)
                elif kind == "w1":
                    _, e, mt = u
                    bk = 2 * (p % 2)
                    for tt in range(2):
                        m = sc.activation(out=he[mt][:, 512 * tt:512 * (tt + 1)],
                                          in_=PS(bk + tt), func=AF.Gelu,
                                          bias=sb["eb1_s"][:, 16 * e + mt:16 * e + mt + 1])
                elif kind == "na":
                    mo = u[1]
                    bk = 4 + 2 * (mo % 2)
                    sc.activation(out=nonab[mo][:, 0:512], in_=PS(bk), func=AF.Copy)
                    m = sc.activation(out=nonab[mo][:, 512:1024], in_=PS(bk + 1), func=AF.Copy)
                elif kind == "a1":
                    _, q, mt = u
                    bk = 2 * (p % 2)
                    h1q = he[mt % 16][:, 512 * (mt // 16):512 * (mt // 16) + 512]
                    m = sc.activation(out=h1q, in_=PS(bk), func=AF.Gelu,
                                      bias=sb["ab1_s"][:, mt:mt + 1])
                elif kind == "a2b":
                    _, q, mo = u
                    bk = 4 + 2 * (mo % 2)
                    m = sc.activation(out=abelb[mo][:, 512 * q:512 * q + 512], in_=PS(bk),
                                      func=AF.Identity, bias=sb["ab2_s"][:, mo:mo + 1])
                elif kind == "csmoe":
                    for mo in range(8):
                        m = sc.activation(out=gyr[mo][:], in_=moe[mo][:], func=AF.Copy)
                return m

            def u_post_v(ve, u, p):
                kind = u[0]
                m = None
                if kind == "wsc":
                    for mt in range(16):
                        m = ve.tensor_mul(he[mt][:], he[mt][:], wrepb[:])
                elif kind == "w2":
                    _, e, mo = u
                    bk = 4 + 2 * (p % 2)
                    for tt in range(2):
                        if e == 0:
                            ve.tensor_copy(out=moe[mo][:, 512 * tt:512 * (tt + 1)],
                                           in_=PS(bk + tt))
                        else:
                            ve.tensor_add(moe[mo][:, 512 * tt:512 * (tt + 1)],
                                          moe[mo][:, 512 * tt:512 * (tt + 1)],
                                          PS(bk + tt))
                    m = None
                return m

            NU = len(units)
            for k in range(NU + 3):
                fns = {}
                if k < NU:
                    u = units[k]
                    if u[0] in ("w1", "w2", "a1", "a2a", "a2b"):
                        fns["y"] = (lambda sy, u=u, p=k % 2, f=u_load: f(sy, u, p))
                    elif u[0] == "tapmoe" and t_moe2 is not None:
                        fns["y"] = (lambda sy, hc0=hc0: DmaBatch(kb, sy, "pfst").go(
                            [(t_moe2[128 * mo:128 * (mo + 1), hc0:hc0 + 1024], gyr[mo][:])
                             for mo in range(8)]))
                uc = units[k - 1] if 1 <= k <= NU else None
                um = units[k - 2] if 2 <= k <= NU + 1 else None
                up = units[k - 3] if 3 <= k <= NU + 2 else None

                def s_slot(sc, uc=uc, up=up, k=k, f1=u_cast, f2=u_post_s):
                    m = None
                    if uc is not None:
                        m = f1(sc, uc, (k - 1) % 2) or m
                    if up is not None:
                        m = f2(sc, up, (k - 3) % 2) or m
                    return m
                if (uc is not None and uc[0] in ("w1", "w2", "a1", "a2a", "a2b")) or \
                   (up is not None and up[0] in ("wrep", "w1", "na", "a1", "a2b", "csmoe")):
                    fns["s"] = s_slot
                if um is not None and um[0] in ("wrep", "w1", "w2", "na", "a1", "a2a", "a2b"):
                    fns["t"] = (lambda tn, um=um, k=k, f=u_mm: f(tn, um, (k - 2) % 2))
                if up is not None and up[0] in ("wsc", "w2"):
                    fns["v"] = (lambda ve, up=up, k=k, f=u_post_v: f(ve, up, (k - 3) % 2))
                if fns:
                    step(**fns)

            def gcm(tn, hc0=hc0):
                m = None
                for tt in range(2):
                    for mo in range(8):
                        tn.matmul(PS(tt, 512, 1), gcwb[:, mo:mo + 1],
                                  abelb[mo][:, 512 * tt:512 * (tt + 1)],
                                  start=(mo == 0), stop=False)
                    for mo in range(8):
                        m = tn.matmul(PS(tt, 512, 1), gcwb[:, 8 + mo:9 + mo],
                                      nonab[mo][:, 512 * tt:512 * (tt + 1)],
                                      start=False, stop=(mo == 7))
                return m
            step(t=gcm)

            def gsig(sc):
                sc.activation(out=grow[0:1, 0:512], in_=PS(0, 512, 1), func=AF.Sigmoid,
                              bias=sb_gcb(sb))
                sc.activation(out=grow[0:1, 512:1024], in_=PS(1, 512, 1), func=AF.Sigmoid,
                              bias=sb_gcb(sb))
                return sc.activation(out=ghi[:], in_=grow[:], func=AF.Copy)
            step(s=gsig)

            def gl(ve):
                return ve.tensor_sub(glo[:], grow[:], ghi[:])
            step(v=gl)

            def glc(sc):
                return sc.activation(out=glob[:], in_=glo[:], func=AF.Copy)
            step(s=glc)

            def gbc(tn):
                m = None
                for tt in range(2):
                    tn.matmul(PS(4 + tt), sb["ones1r_s"][:], ghi[0:1, 512 * tt:512 * (tt + 1)],
                              start=True, stop=False)
                    m = tn.matmul(PS(4 + tt), sb["ones1r_s"][:], glob[0:1, 512 * tt:512 * (tt + 1)],
                                  start=False, stop=True)
                return m
            step(t=gbc)

            def mixv(ve):
                m = None
                for mo in range(8):
                    ve.tensor_copy(out=moe[mo][:], in_=abelb[mo][:])
                    ve.tensor_copy(out=basef[0][:], in_=nonab[mo][:])
                    ve.drain()
                    ve.tensor_sub(basef[0][:], basef[0][:], moe[mo][:])
                    for tt in range(2):
                        ve.tensor_mul(basef[0][:, 512 * tt:512 * (tt + 1)],
                                      basef[0][:, 512 * tt:512 * (tt + 1)], PS(4 + tt))
                    ve.drain()
                    m = ve.tensor_add(moe[mo][:], moe[mo][:], basef[0][:])
                return m
            step(v=mixv)

            if t_rt2 is not None:
                def strt2(sy):
                    return DmaBatch(kb, sy, "pfst").go([(t_rt2[:], sb["rtot_s"][:])])
                step(y=strt2)
            if t_abel is not None:
                def stab(sy, hc0=hc0):
                    return DmaBatch(kb, sy, "pfst").go(
                        [(t_abel[128 * mo:128 * (mo + 1), hc0:hc0 + 1024], abelb[mo][:])
                         for mo in range(8)])
                step(y=stab)
            if t_nona is not None:
                def stna(sy, hc0=hc0):
                    return DmaBatch(kb, sy, "pfst").go(
                        [(t_nona[128 * mo:128 * (mo + 1), hc0:hc0 + 1024], nonab[mo][:])
                         for mo in range(8)])
                step(y=stna)

            for r in range(17):
                fns = {}
                if r < 16:
                    fns["y"] = (lambda sy, r=r: DmaBatch(kb, sy, "pfld").go(
                        [(wkf[r % 2][:, 0:1024], g["out_wT"][128 * r:128 * (r + 1), :])]))
                if r >= 1:
                    fns["s"] = (lambda sc, r=r: sc.activation(
                        out=he[r - 1][:], in_=wkf[(r - 1) % 2][:, 0:1024], func=AF.Copy))
                step(**fns)

            def ldgy(sy, hc0=hc0):
                pairs = [(gyr[r][:],
                          g["gyw_dr"][128 * r:128 * (r + 1), hc0:hc0 + 1024])
                         for r in range(16)]
                return DmaBatch(kb, sy, "pfld").go(pairs)
            step(y=ldgy)

            for mo in range(0, 8, 2):
                def ldbs(sy, mo=mo, hc0=hc0):
                    return DmaBatch(kb, sy, "pfld").go(
                        [(basef[j][:], g["base_dr"][128 * (mo + j):128 * (mo + j + 1),
                                                    hc0:hc0 + 1024]) for j in range(2)])
                step(y=ldbs)

                def mmat(tn, mo=mo):
                    m = None
                    for j in range(2):
                        for r in range(16):
                            for tt in range(2):
                                m = tn.matmul(PS(2 * j + tt), he[r][:, 128 * (mo + j):128 * (mo + j + 1)],
                                              gyr[r][:, 512 * tt:512 * (tt + 1)],
                                              start=(r == 0), stop=(r == 15))
                    return m
                step(t=mmat)

                def fsum(ve, mo=mo):
                    m = None
                    for j in range(2):
                        for tt in range(2):
                            ve.tensor_add(basef[j][:, 512 * tt:512 * (tt + 1)],
                                          basef[j][:, 512 * tt:512 * (tt + 1)],
                                          PS(2 * j + tt))
                        ve.drain()
                        m = ve.tensor_add(basef[j][:], basef[j][:], moe[mo + j][:])
                    return m
                step(v=fsum)

                def stout(sy, mo=mo, hc0=hc0):
                    pairs = [(g["outT"][128 * (mo + j):128 * (mo + j + 1), hc0:hc0 + 1024],
                              basef[j][:]) for j in range(2)]
                    return DmaBatch(kb, sy, "pfst").go(pairs)
                step(y=stout)

        with nc.Block() as blk:
            pipe.emit(blk, None)


def sb_gcb(sb):
    return sb["gcb_s"][0:1, 0:1]


def _stage_inputs(i):
    import ml_dtypes
    bf = ml_dtypes.bfloat16
    resid = i["L0_attn_resid"][0]
    cores = []
    eye128 = np.eye(128, dtype=bf)
    id8f = np.eye(8, dtype=np.float32)
    oh16 = np.zeros((NST, NST * 128), dtype=bf)
    for n in range(NST):
        oh16[n, 128 * n:128 * (n + 1)] = 1
    oh8 = np.zeros((8, 8 * 128), dtype=bf)
    for n in range(8):
        oh8[n, 128 * n:128 * (n + 1)] = 1
    ones1 = np.ones((128, 1), dtype=bf)
    ones1r = np.ones((1, 128), dtype=bf)
    cvals = [1e-5, 1.0 / D, -1.0, 1.0] + [1.0 / (k + 1) for k in range(20)] + [0.0] * 8
    cpool = np.tile(np.array(cvals, np.float32)[None, :], (128, 1))

    def pack_cols(v, ncol):
        return np.ascontiguousarray(v.reshape(ncol, 128).T)

    aT = lambda x: np.ascontiguousarray(np.asarray(x).T)
    C = np.ascontiguousarray
    for c in range(NCORES):
        tsl = slice(TC * c, TC * (c + 1))
        lsl = slice(LC * c, LC * (c + 1))
        m = {
            "residT": aT(resid[tsl]),
            "norm_w_t": pack_cols(i["L0_attn_norm_w"], 8),
            "ln_g_t": pack_cols(i["L0_ffn_ln_g"], 8),
            "ln_b_t": pack_cols(i["L0_ffn_ln_b"], 8),
            "gate_w_t": C(i["L0_ffn_gate_w"].reshape(8, 128, NEXP).transpose(1, 0, 2)
                          .reshape(128, 8 * NEXP)),
            "gate_b_t": C(i["L0_ffn_gate_b"].reshape(NEXP, 1)),
            "in_wT": aT(i["L0_attn_in_w"][lsl]),
            "skip_wT": aT(i["L0_attn_skip_w"][lsl]),
            "wd1T": aT(i["L0_attn_wd1"][:, lsl]),
            "wbT": aT(i["L0_attn_wb"][:, lsl]),
            "wcT": aT(i["L0_attn_wc"][:, lsl]),
            "wd2T": aT(i["L0_attn_wd2"][lsl]),
            "a_log_t": C(i["L0_attn_A_log"][lsl].reshape(2, 128, NST)
                         .transpose(1, 0, 2).reshape(128, 2 * NST)),
            "conv_w_t": C(i["L0_attn_conv_w"][lsl].reshape(2, 128, 4)
                          .transpose(1, 0, 2).reshape(128, 8)),
            "conv_b_t": pack_cols(i["L0_attn_conv_b"][lsl], 2),
            "wd2_b_t": pack_cols(i["L0_attn_wd2_b"][lsl], 2),
            "w_d_t": pack_cols(i["L0_attn_W_D"][lsl], 2),
            "out_wT": aT(i["L0_attn_out_w"]),
            "aw1": C(i["L0_ffn_aw1"]),
            "ab1_t": pack_cols(i["L0_ffn_ab1"], 32),
            "aw2": C(i["L0_ffn_aw2"]),
            "ab2_t": pack_cols(i["L0_ffn_ab2"], 8),
            "ew1": C(i["L0_ffn_ew1"]),
            "eb1_t": C(i["L0_ffn_eb1"].reshape(NEXP, 16, 128)
                       .transpose(2, 0, 1).reshape(128, NEXP * 16)),
            "ew2": C(i["L0_ffn_ew2"]),
            "eb2_n": C(i["L0_ffn_eb2"]),
            "gens_n": C(i["L0_ffn_gens"][c % 4]),
            "gcw_t": pack_cols(i["L0_ffn_gc_w"][:, 0], 16),
            "gc_b_t": C(i["L0_ffn_gc_b"].reshape(1, 1)),
            "ident": eye128, "id8f": id8f, "oh16": oh16, "oh8": oh8,
            "ones1": ones1, "ones1r": ones1r,
            "tok0": np.array([[TC * c]], np.int32),
            "cpool": cpool,
        }
        cores.append(m)
    return cores


_BUILT = {}


def kernel(**inputs):
    taps = tuple(TAPS)
    nc, tap_handles = build(taps)
    in_maps = _stage_inputs(inputs)
    res = run_bass_kernel_spmd(nc, in_maps, core_ids=list(range(NCORES)))
    outs = [np.asarray(r["outT"]) for r in res.results]
    full = np.concatenate([o.T for o in outs], axis=0)[None]
    if taps:
        kernel.tap_results = [{k: np.asarray(r["tap_" + k]) for k in taps}
                              for r in res.results]
    return np.ascontiguousarray(full, dtype=np.float32)


# version 26 (final)
# speedup vs baseline: 1.6800x; 1.0493x over previous
import numpy as np
from contextlib import ExitStack

import concourse.bass as bass
import concourse.mybir as mybir
from concourse.bass_utils import run_bass_kernel_spmd

F32 = mybir.dt.float32
BF16 = mybir.dt.bfloat16
I32 = mybir.dt.int32
OP = mybir.AluOpType
AF = mybir.ActivationFunctionType
AX = mybir.AxisListType

NCORES = 8
T = 16384
TC = T // NCORES
D = 1024
E2 = 2048
LC = E2 // NCORES
NST = 16
DTR = 64
A4 = 4096
H2 = 2048
NEXP = 6
NTERMS = 20
RG = [list(range(NCORES))]
NEG = -1e30

TAPS = ()


class Pipe:

    def __init__(self, kb, name, engines, leads=None):
        self.kb = kb
        self.name = name
        self.engines = list(engines)
        self.leads = leads or {}
        self.steps = []

    def step(self, **eng_fns):
        for e in eng_fns:
            assert e in self.engines, (self.name, e)
        self.steps.append(eng_fns)

    def emit(self, block, eng_objs):
        kb = self.kb
        nsteps = len(self.steps)
        done = {}
        for f in self.engines:
            c, arr = 0, []
            for s in range(nsteps):
                if f in self.steps[s]:
                    c += 1
                arr.append(c)
            done[f] = arr
        sems = {f: kb.sem(f"{self.name}_{f}") for f in self.engines}

        def body_for(e):
            def body(eng):
                my = 0
                for s in range(nsteps):
                    fn = self.steps[s].get(e)
                    if fn is None:
                        continue
                    for f in self.engines:
                        if f == e:
                            continue
                        j = s - 1 - self.leads.get((e, f), 0)
                        tgt = done[f][j] if j >= 0 else 0
                        if tgt > 0:
                            eng.wait_ge(sems[f], tgt)
                    last = fn(eng)
                    my += 1
                    if e in ("v", "s"):
                        eng.drain().then_inc(sems[e], 1)
                    elif last is not None:
                        last.then_inc(sems[e], 1)
                    else:
                        eng.sem_inc(sems[e], 1)
            return body

        for e in self.engines:
            getattr(block, {"t": "tensor", "v": "vector", "s": "scalar",
                            "y": "sync", "g": "gpsimd"}[e])(body_for(e))


class KB:
    def __init__(self, nc, ctx):
        self.nc = nc
        self.ctx = ctx
        self.sems = {}

    def sem(self, name):
        if name not in self.sems:
            self.sems[name] = self.ctx.enter_context(self.nc.semaphore(name))
        return self.sems[name]


def dma(eng, out, in_, kb, tag):
    sem = kb.sem(tag)
    n = kb.__dict__.setdefault("_dmacnt", {})
    n[tag] = n.get(tag, 0) + 16
    eng.dma_start(out=out, in_=in_).then_inc(sem, 16)
    return n[tag]


class DmaBatch:

    def __init__(self, kb, eng, tag):
        self.kb, self.eng, self.tag = kb, eng, tag

    def go(self, pairs):
        tgt = None
        for out, in_ in pairs:
            tgt = dma(self.eng, out, in_, self.kb, self.tag)
        self.eng.wait_ge(self.kb.sem(self.tag), tgt)
        return None


def build(taps=()):
    nc = bass.Bass("TRN2", target_bir_lowering=False)
    ctx = ExitStack()
    kb = KB(nc, ctx)
    g = {}

    def inp(name, shape, dtype=F32):
        g[name] = nc.dram_tensor(name, shape, dtype, kind="ExternalInput")

    inp("residT", [D, TC]); inp("norm_w_t", [128, 8]); inp("ln_g_t", [128, 8])
    inp("ln_b_t", [128, 8]); inp("gate_w_t", [128, 8 * NEXP]); inp("gate_b_t", [NEXP, 1])
    inp("in_wT", [D, LC]); inp("skip_wT", [D, LC]); inp("wd1T", [LC, DTR])
    inp("wbT", [LC, NST]); inp("wcT", [LC, NST]); inp("wd2T", [DTR, LC])
    inp("a_log_t", [128, 2 * NST]); inp("conv_w_t", [128, 2 * 4])
    inp("conv_b_t", [128, 2]); inp("wd2_b_t", [128, 2]); inp("w_d_t", [128, 2])
    inp("out_wT", [E2, D]); inp("aw1", [D, A4]); inp("ab1_t", [128, 32])
    inp("aw2", [A4, D]); inp("ab2_t", [128, 8]); inp("ew1", [NEXP, D, H2])
    inp("eb1_t", [128, NEXP * 16]); inp("ew2", [NEXP, H2, D]); inp("eb2_n", [NEXP, D])
    inp("gens_n", [D, D]); inp("gcw_t", [128, 16]); inp("gc_b_t", [1, 1])
    inp("ident", [128, 128], BF16); inp("id8f", [8, 8], F32)
    inp("oh16", [NST, NST * 128], BF16); inp("oh8", [8, 8 * 128], BF16)
    inp("ones1", [128, 1], BF16); inp("ones1r", [1, 128], BF16)
    inp("tok0", [1, 1], I32); inp("cpool", [128, 32], F32)

    g["outT"] = nc.dram_tensor("outT", [D, TC], F32, kind="ExternalOutput")

    def intern(name, shape, dtype):
        g[name] = nc.dram_tensor(name, shape, dtype, kind="Internal")

    intern("cc_xn2_in", [D * TC], BF16); intern("cc_xn2_out", [NCORES * D * TC], BF16)
    intern("cc_bcd_in", [96 * T], F32); intern("cc_bcd_out", [96 * T], F32)
    intern("cc_gy_in", [LC * T], BF16); intern("cc_gy_out", [NCORES * LC * T], BF16)
    intern("cc_t_in", [D * D], BF16); intern("cc_t_out", [NCORES * D * D], BF16)
    intern("base_dr", [D, TC], F32); intern("xnf_dr", [D, TC], BF16)
    intern("gyw_dr", [E2, TC], BF16)
    intern("bt_dr", [NST, T], BF16); intern("ct_dr", [NST, T], BF16)
    intern("d1b_dr", [DTR, T], BF16)
    intern("x_dr", [LC, T], F32); intern("sskip_dr", [LC, T], F32)
    intern("abel_dr", [D, TC], BF16)

    taps = tuple(taps)
    tap_handles = {}

    def tap(name, shape, dtype=F32):
        if name in taps:
            tap_handles[name] = nc.dram_tensor("tap_" + name, shape, dtype,
                                               kind="ExternalOutput")
            return tap_handles[name]
        return None

    with ctx:
        E = ctx.enter_context
        sb = {}

        def sbuf(name, shape, dtype=F32):
            sb[name] = E(nc.sbuf_tensor(name, shape, dtype))
            return sb[name]

        sbuf("ident_s", [128, 128], BF16); sbuf("id8f_s", [8, 8], F32)
        sbuf("ones1_s", [128, 1], BF16); sbuf("ones1r_s", [1, 128], BF16)
        sbuf("oh8_s", [8, 8 * 128], BF16)
        sbuf("nrm_s", [128, 8]); sbuf("lng_s", [128, 8]); sbuf("lnb_s", [128, 8])
        sbuf("gw_s", [128, 8 * NEXP]); sbuf("gb_s", [NEXP, 1])
        sbuf("w6_s", [8, TC], BF16)
        sbuf("alog_s", [128, 2 * NST]); sbuf("acoef_s", [128, 2 * NST])
        sbuf("cw_s", [128, 8]); sbuf("cb_s", [128, 2]); sbuf("wd2b_s", [128, 2])
        sbuf("wdt_s", [128, 2]); sbuf("tok0_s", [1, 1], I32)
        sbuf("cp_s", [128, 32]); sbuf("gcb_s", [1, 1])
        sbuf("eb1_s", [128, NEXP * 16]); sbuf("ab1_s", [128, 32]); sbuf("ab2_s", [128, 8])

        def CP(col, p=128):
            return sb["cp_s"][0:p, col:col + 1]
        g["CP"] = CP
        psum = E(nc.psum_tensor("psum", [128, 4096], F32))

        def PS(bank, n=512, p=128):
            return psum[0:p, 512 * bank: 512 * bank + n]

        g["PS"] = PS
        g["psum"] = psum
        g["sb"] = sb
        g["tap"] = tap

        with nc.Block() as blk:
            @blk.sync
            def _(sy):
                d = DmaBatch(kb, sy, "c0")
                d.go([(sb["ident_s"][:], g["ident"][:]), (sb["id8f_s"][:], g["id8f"][:]),
                      (sb["ones1_s"][:], g["ones1"][:]), (sb["ones1r_s"][:], g["ones1r"][:]),
                      (sb["oh8_s"][:], g["oh8"][:]),
                      (sb["nrm_s"][:], g["norm_w_t"][:]), (sb["lng_s"][:], g["ln_g_t"][:]),
                      (sb["lnb_s"][:], g["ln_b_t"][:]), (sb["gw_s"][:], g["gate_w_t"][:]),
                      (sb["gb_s"][:], g["gate_b_t"][:]), (sb["alog_s"][:], g["a_log_t"][:]),
                      (sb["cw_s"][:], g["conv_w_t"][:]), (sb["cb_s"][:], g["conv_b_t"][:]),
                      (sb["wd2b_s"][:], g["wd2_b_t"][:]), (sb["wdt_s"][:], g["w_d_t"][:]),
                      (sb["tok0_s"][:], g["tok0"][:]),
                      (sb["cp_s"][:], g["cpool"][:]),
                      (sb["gcb_s"][:], g["gc_b_t"][:]),
                      (sb["eb1_s"][:], g["eb1_t"][:]), (sb["ab1_s"][:], g["ab1_t"][:]),
                      (sb["ab2_s"][:], g["ab2_t"][:])])

            @blk.scalar
            def _(sc):
                sc.wait_ge(kb.sem("c0"), 17 * 16)
                sc.activation(out=sb["acoef_s"][:], in_=sb["alog_s"][:], func=AF.Exp)
                sc.activation(out=sb["acoef_s"][:], in_=sb["acoef_s"][:],
                              func=AF.Copy, scale=sb["cp_s"][:, 2:3])

        phase0(kb, g)
        phase_expm(kb, g)
        phase2(kb, g)
        phase3(kb, g)
        phase4(kb, g)
        phase_ffn(kb, g)

    return nc, tap_handles


def phase0(kb, g):
    nc = kb.nc
    sb = g["sb"]
    PS = g["PS"]
    tap = g["tap"]
    SP = 512
    NCH = TC // SP
    CP = g["CP"]
    t_xn = tap("xn", [D, TC]); t_base = tap("base", [D, TC])
    t_xn2 = tap("xn2", [D, TC], BF16); t_xnf = tap("xnf", [D, TC], BF16)
    t_logits = tap("logits", [8, TC]); t_w6 = tap("w6", [8, TC], BF16)

    with ExitStack() as ctx:
        E = ctx.enter_context
        rt = [E(nc.sbuf_tensor(f"p0rt{k}", [128, SP], F32)) for k in range(8)]
        xn = [E(nc.sbuf_tensor(f"p0xn{k}", [128, SP], F32)) for k in range(8)]
        xb = [E(nc.sbuf_tensor(f"p0xb{k}", [128, SP], BF16)) for k in range(8)]
        sq = [E(nc.sbuf_tensor(f"p0sq{k}", [128, SP], BF16)) for k in range(8)]
        x2b = [E(nc.sbuf_tensor(f"p0x2{k}", [128, SP], BF16)) for k in range(8)]
        xfb = [E(nc.sbuf_tensor(f"p0xf{k}", [128, SP], BF16)) for k in range(8)]
        xf32 = [E(nc.sbuf_tensor(f"p0xf32{k}", [128, SP], F32)) for k in range(8)]
        row = E(nc.sbuf_tensor("p0row", [1, 10 * SP], F32))
        rowb = E(nc.sbuf_tensor("p0rowb", [1, 10 * SP], BF16))
        lg6 = E(nc.sbuf_tensor("p0lg6", [8, SP], F32))
        ms8 = E(nc.sbuf_tensor("p0ms8", [128, 8], F32))
        mr1 = E(nc.sbuf_tensor("p0mr1", [128, 8], F32))
        mr2 = E(nc.sbuf_tensor("p0mr2", [128, 8], F32))
        lt0 = E(nc.sbuf_tensor("p0lt0", [128, 8], F32))
        lt1 = E(nc.sbuf_tensor("p0lt1", [128, 8], F32))
        lt2 = E(nc.sbuf_tensor("p0lt2", [128, 8], F32))
        mk1 = E(nc.sbuf_tensor("p0mk1", [128, 8], F32))
        wdd = E(nc.sbuf_tensor("p0wd", [128, 8], F32))
        mks = E(nc.sbuf_tensor("p0mks", [128, 32], F32))
        wds = E(nc.sbuf_tensor("p0wds", [128, 32], F32))
        qq = E(nc.sbuf_tensor("p0q", [128, 4], F32))
        wtok = E(nc.sbuf_tensor("p0wtok", [128, 32], F32))
        wtokb = E(nc.sbuf_tensor("p0wtokb", [128, 32], BF16))

        def R(slot, n=SP):
            return row[0:1, slot * SP: slot * SP + n]

        def Rb(slot, n=SP):
            return rowb[0:1, slot * SP: slot * SP + n]

        pipe = Pipe(kb, "p0", ["y", "t", "s", "v"])
        for i in range(NCH):
            c0, c1 = SP * i, SP * (i + 1)

            def ld(sy, i=i, c0=c0, c1=c1):
                return DmaBatch(kb, sy, "p0ld").go(
                    [(rt[k][:], g["residT"][128 * k:128 * (k + 1), c0:c1])
                     for k in range(8)])
            pipe.step(y=ld)
            def casts(sc):
                for k in range(8):
                    sc.activation(out=xb[k][:], in_=rt[k][:], func=AF.Copy)
                for k in range(7):
                    sc.activation(out=sq[k][:], in_=rt[k][:], func=AF.Square)
                return sc.activation(out=sq[7][:], in_=rt[7][:], func=AF.Square)
            pipe.step(s=casts)

            def sums(tn):
                for k in range(8):
                    tn.matmul(PS(0, SP, 1), sb["ones1_s"][:], xb[k][:],
                              start=(k == 0), stop=(k == 7))
                for k in range(8):
                    m = tn.matmul(PS(1, SP, 1), sb["ones1_s"][:], sq[k][:],
                                  start=(k == 0), stop=(k == 7))
                return m
            pipe.step(t=sums)

            def stat1(sc):
                sc.activation(out=R(0), in_=PS(0, SP, 1), func=AF.Copy,
                              scale=CP(1, 1))
                sc.activation(out=R(6), in_=PS(1, SP, 1), func=AF.Copy,
                              scale=CP(1, 1))
                return sc.activation(out=R(5), in_=R(0), func=AF.Square)
            pipe.step(s=stat1)

            def var1(ve):
                return ve.tensor_sub(R(5), R(6), R(5))
            pipe.step(v=var1)

            def stat2(sc):
                sc.activation(out=R(9), in_=R(5), func=AF.Sqrt, bias=CP(0, 1))
                return sc.activation(out=Rb(0), in_=R(0), func=AF.Copy)
            pipe.step(s=stat2)

            def lo1(ve):
                ve.reciprocal(out=R(1), in_=R(9))
                return ve.tensor_sub(R(5), R(0), Rb(0))
            pipe.step(v=lo1)

            def lo1c(sc):
                sc.activation(out=Rb(5), in_=R(5), func=AF.Copy)
                return sc.activation(out=Rb(1), in_=R(1), func=AF.Copy)
            pipe.step(s=lo1c)

            def lo1b(ve):
                return ve.tensor_sub(R(6), R(1), Rb(1))
            pipe.step(v=lo1b)

            def lo1d(sc):
                return sc.activation(out=Rb(6), in_=R(6), func=AF.Copy)
            pipe.step(s=lo1d)

            def bc1(tn):
                tn.matmul(PS(2, SP), sb["ones1r_s"][:], Rb(0), start=True, stop=False)
                tn.matmul(PS(2, SP), sb["ones1r_s"][:], Rb(5), start=False, stop=True)
                tn.matmul(PS(3, SP), sb["ones1r_s"][:], Rb(1), start=True, stop=False)
                return tn.matmul(PS(3, SP), sb["ones1r_s"][:], Rb(6), start=False, stop=True)
            pipe.step(t=bc1)

            def mkxn(ve):
                for k in range(8):
                    ve.tensor_sub(xn[k][:], rt[k][:], PS(2, SP))
                for k in range(8):
                    ve.tensor_mul(xn[k][:], xn[k][:], PS(3, SP))
                m = None
                for k in range(8):
                    m = ve.scalar_tensor_tensor(out=rt[k][:], in0=xn[k][:], scalar=2.0,
                                                in1=rt[k][:], op0=OP.mult, op1=OP.add)
                return m
            pipe.step(v=mkxn)

            def stbase(sy, i=i, c0=c0, c1=c1):
                pairs = [(g["base_dr"][128 * k:128 * (k + 1), c0:c1], rt[k][:])
                         for k in range(8)]
                if t_base is not None:
                    pairs += [(t_base[128 * k:128 * (k + 1), c0:c1], rt[k][:])
                              for k in range(8)]
                if t_xn is not None:
                    pairs += [(t_xn[128 * k:128 * (k + 1), c0:c1], xn[k][:])
                              for k in range(8)]
                return DmaBatch(kb, sy, "p0st").go(pairs)
            pipe.step(y=stbase)

            def casts2(sc):
                for k in range(8):
                    sc.activation(out=xb[k][:], in_=xn[k][:], func=AF.Copy)
                for k in range(8):
                    m = sc.activation(out=sq[k][:], in_=xn[k][:], func=AF.Square)
                return m
            pipe.step(s=casts2)
            pipe.step(t=sums)

            def stat3(sc):
                sc.activation(out=R(2), in_=PS(1, SP, 1), func=AF.Sqrt,
                              scale=CP(1, 1), bias=CP(0, 1))
                sc.activation(out=R(3), in_=PS(0, SP, 1), func=AF.Copy,
                              scale=CP(1, 1))
                sc.activation(out=R(6), in_=PS(1, SP, 1), func=AF.Copy,
                              scale=CP(1, 1))
                return sc.activation(out=R(5), in_=R(3), func=AF.Square)
            pipe.step(s=stat3)

            def var2(ve):
                ve.reciprocal(out=R(2), in_=R(2))
                return ve.tensor_sub(R(5), R(6), R(5))
            pipe.step(v=var2)

            def stat4(sc):
                sc.activation(out=R(9), in_=R(5), func=AF.Sqrt, bias=CP(0, 1))
                sc.activation(out=Rb(2), in_=R(2), func=AF.Copy)
                return sc.activation(out=Rb(3), in_=R(3), func=AF.Copy)
            pipe.step(s=stat4)

            def lo2(ve):
                ve.reciprocal(out=R(4), in_=R(9))
                ve.tensor_sub(R(5), R(2), Rb(2))
                return ve.tensor_sub(R(6), R(3), Rb(3))
            pipe.step(v=lo2)

            def lo2c(sc):
                sc.activation(out=Rb(5), in_=R(5), func=AF.Copy)
                sc.activation(out=Rb(6), in_=R(6), func=AF.Copy)
                return sc.activation(out=Rb(4), in_=R(4), func=AF.Copy)
            pipe.step(s=lo2c)

            def lo2b(ve):
                return ve.tensor_sub(R(7), R(4), Rb(4))
            pipe.step(v=lo2b)

            def lo2d(sc):
                return sc.activation(out=Rb(7), in_=R(7), func=AF.Copy)
            pipe.step(s=lo2d)

            def bc2(tn):
                tn.matmul(PS(2, SP), sb["ones1r_s"][:], Rb(2), start=True, stop=False)
                tn.matmul(PS(2, SP), sb["ones1r_s"][:], Rb(5), start=False, stop=True)
                tn.matmul(PS(3, SP), sb["ones1r_s"][:], Rb(3), start=True, stop=False)
                tn.matmul(PS(3, SP), sb["ones1r_s"][:], Rb(6), start=False, stop=True)
                tn.matmul(PS(4, SP), sb["ones1r_s"][:], Rb(4), start=True, stop=False)
                return tn.matmul(PS(4, SP), sb["ones1r_s"][:], Rb(7), start=False, stop=True)
            pipe.step(t=bc2)

            def mkx2f(ve):
                for k in range(8):
                    ve.scalar_tensor_tensor(out=x2b[k][:], in0=xn[k][:],
                                            scalar=sb["nrm_s"][:, k:k + 1],
                                            in1=PS(2, SP), op0=OP.mult, op1=OP.mult)
                for k in range(8):
                    ve.tensor_sub(xf32[k][:], xn[k][:], PS(3, SP))
                for k in range(8):
                    ve.tensor_mul(xf32[k][:], xf32[k][:], PS(4, SP))
                m = None
                for k in range(8):
                    ve.tensor_scalar(out=xfb[k][:], in0=xf32[k][:],
                                     scalar1=sb["lng_s"][:, k:k + 1],
                                     scalar2=sb["lnb_s"][:, k:k + 1],
                                     op0=OP.mult, op1=OP.add)
                    m = ve.tensor_scalar(out=xf32[k][:], in0=xf32[k][:],
                                         scalar1=sb["lng_s"][:, k:k + 1],
                                         scalar2=sb["lnb_s"][:, k:k + 1],
                                         op0=OP.mult, op1=OP.add)
                return m
            pipe.step(v=mkx2f)

            def stx2f(sy, i=i, c0=c0, c1=c1):
                pairs = []
                for k in range(8):
                    pairs.append((g["cc_xn2_in"].ap().rearrange(
                        "(d t) -> d t", d=D)[128 * k:128 * (k + 1), c0:c1], x2b[k][:]))
                    pairs.append((g["xnf_dr"][128 * k:128 * (k + 1), c0:c1], xfb[k][:]))
                if t_xn2 is not None:
                    pairs += [(t_xn2[128 * k:128 * (k + 1), c0:c1], x2b[k][:]) for k in range(8)]
                if t_xnf is not None:
                    pairs += [(t_xnf[128 * k:128 * (k + 1), c0:c1], xfb[k][:]) for k in range(8)]
                return DmaBatch(kb, sy, "p0st").go(pairs)
            pipe.step(y=stx2f)

            def glog(tn):
                for k in range(8):
                    m = tn.matmul(PS(6, SP, NEXP), sb["gw_s"][:, 6 * k:6 * (k + 1)],
                                  xf32[k][:], start=(k == 0), stop=(k == 7))
                return m
            pipe.step(t=glog)

            def gevac(ve):
                ve.memset(lg6[:], NEG)
                return ve.tensor_scalar(out=lg6[0:NEXP, :], in0=PS(6, SP, NEXP),
                                        scalar1=sb["gb_s"][:], scalar2=None, op0=OP.add)
            pipe.step(v=gevac)

            def gtr(tn):
                m = None
                for j in range(SP // 128):
                    m = tn.matmul(PS(7, 128, 128)[:, 8 * j:8 * (j + 1)][0:128, 0:8],
                                  lg6[:, 128 * j:128 * (j + 1)], sb["id8f_s"][:],
                                  start=True, stop=True)
                return m
            pipe.step(t=gtr)

            def top2b(ve):
                m = None
                for j in range(SP // 128):
                    pcol = PS(7, 128, 128)[:, 8 * j:8 * (j + 1)]
                    ve.tensor_copy(out=lt0[:], in_=pcol)
                    ve.drain()
                    ve.max(out=ms8[:], in_=lt0[:])
                    ve.memset(mr1[:], NEG)
                    ve.memset(mr2[:], NEG)
                    ve.drain()
                    ve.tensor_copy(out=mr1[:, 0:1], in_=ms8[:, 0:1])
                    ve.tensor_copy(out=mr2[:, 0:1], in_=ms8[:, 1:2])
                    ve.drain()
                    ve.match_replace(out=lt1[:], in_to_replace=mr1[:], in_values=lt0[:],
                                     imm_value=NEG)
                    ve.drain()
                    ve.match_replace(out=lt2[:], in_to_replace=mr2[:], in_values=lt1[:],
                                     imm_value=NEG)
                    ve.drain()
                    ve.tensor_tensor(out=mk1[:], in0=lt0[:], in1=lt1[:], op=OP.not_equal)
                    ve.tensor_tensor(out=wdd[:], in0=lt1[:], in1=lt2[:], op=OP.not_equal)
                    ve.drain()
                    ve.tensor_sub(wdd[:], wdd[:], mk1[:])
                    ve.drain()
                    ve.tensor_copy(out=mks[:, 8 * j:8 * (j + 1)], in_=mk1[:])
                    ve.tensor_copy(out=wds[:, 8 * j:8 * (j + 1)], in_=wdd[:])
                    m = ve.tensor_sub(qq[:, j:j + 1], ms8[:, 1:2], ms8[:, 0:1])
                return m
            pipe.step(v=top2b)

            def sigq(sc):
                return sc.activation(out=qq[:], in_=qq[:], func=AF.Sigmoid)
            pipe.step(s=sigq)

            def wcomb(ve):
                m = None
                for j in range(SP // 128):
                    m = ve.scalar_tensor_tensor(
                        out=wtok[:, 8 * j:8 * (j + 1)],
                        in0=wds[:, 8 * j:8 * (j + 1)],
                        scalar=qq[:, j:j + 1],
                        in1=mks[:, 8 * j:8 * (j + 1)],
                        op0=OP.mult, op1=OP.add)
                return m
            pipe.step(v=wcomb)

            def wcast(ve):
                return ve.tensor_copy(out=wtokb[:], in_=wtok[:])
            pipe.step(v=wcast)

            def wtr(tn):
                m = None
                for j in range(SP // 128):
                    m = tn.matmul(PS(6, 512)[0:8, 128 * j:128 * (j + 1)],
                                  wtokb[:, 8 * j:8 * (j + 1)], sb["ident_s"][:],
                                  start=True, stop=True)
                return m
            pipe.step(t=wtr)

            def wev(ve, i=i, c0=c0, c1=c1):
                return ve.tensor_copy(out=sb["w6_s"][:, c0:c1], in_=PS(6, SP, 8))
            pipe.step(v=wev)

            if t_logits is not None:
                def lgst(sy, i=i, c0=c0, c1=c1):
                    return DmaBatch(kb, sy, "p0st").go([(t_logits[:, c0:c1], lg6[:])])
                pipe.step(y=lgst)

        if t_w6 is not None:
            def w6st(sy):
                return DmaBatch(kb, sy, "p0st").go([(t_w6[:], sb["w6_s"][:])])
            pipe.step(y=w6st)

        with nc.Block() as blk:
            pipe.emit(blk, None)


def phase_expm(kb, g):
    nc = kb.nc
    sb = g["sb"]
    PS = g["PS"]
    CP = g["CP"]
    tap = g["tap"]
    t_tg = tap("tg", [D, D], BF16)

    with ExitStack() as ctx:
        E = ctx.enter_context
        gsf = [E(nc.sbuf_tensor(f"exgf{r}", [128, D], F32)) for r in range(8)]
        gsb = [E(nc.sbuf_tensor(f"exgb{r}", [128, D], BF16)) for r in range(8)]
        pb = [[E(nc.sbuf_tensor(f"expP{p}_{r}", [128, D], BF16)) for r in range(8)]
              for p in range(2)]

        leads = {(e, "g"): 100000 for e in "ytsv"}
        pipe = Pipe(kb, "pex", ["y", "t", "s", "v", "g"], leads=leads)

        def agx(gp):
            gp.collective_compute(
                "AllGather", OP.bypass, replica_groups=RG,
                ins=[g["cc_xn2_in"][:]], outs=[g["cc_xn2_out"][:]]).then_inc(
                kb.sem("agx"), 1)
            gp.wait_ge(kb.sem("agx"), 1)
            return None
        pipe.step(g=agx)

        def ldg(sy):
            return DmaBatch(kb, sy, "exld").go(
                [(gsf[r][:], g["gens_n"][128 * r:128 * (r + 1), :]) for r in range(8)])
        pipe.step(y=ldg)

        def cst(sc):
            m = None
            for r in range(8):
                m = sc.activation(out=gsb[r][:], in_=gsf[r][:], func=AF.Copy)
            return m
        pipe.step(s=cst)

        def initp(ve):
            par = NTERMS % 2
            for r in range(8):
                ve.memset(pb[par][r][:], 0.0)
            ve.drain()
            m = None
            for r in range(8):
                m = ve.tensor_copy(out=pb[par][r][:, 128 * r:128 * (r + 1)],
                                   in_=sb["ident_s"][:])
            return m
        pipe.step(v=initp)

        def emm(tn, src, q):
            h, mo0 = q // 2, 4 * (q % 2)
            m = None
            for mo in range(mo0, mo0 + 4):
                for r in range(8):
                    m = tn.matmul(PS(mo),
                                  gsb[r][:, 128 * mo:128 * (mo + 1)],
                                  pb[src][r][:, 512 * h:512 * h + 512],
                                  start=(r == 0), stop=(r == 7))
            return m

        def eev(sc, k, dst, q):
            h, mo0 = q // 2, 4 * (q % 2)
            m = None
            for mo in range(mo0, mo0 + 4):
                m = sc.activation(out=pb[dst][mo][:, 512 * h:512 * h + 512],
                                  in_=PS(mo), func=AF.Copy, scale=CP(3 + k))
            return m

        def eadd(ve, dst, q):
            if q not in (0, 3):
                return None
            h, mo0 = q // 2, 4 * (q % 2)
            m = None
            for mo in range(mo0, mo0 + 4):
                m = ve.tensor_tensor(
                    out=pb[dst][mo][:, 128 * mo:128 * (mo + 1)],
                    in0=pb[dst][mo][:, 128 * mo:128 * (mo + 1)],
                    in1=sb["ident_s"][:], op=OP.add)
            return m

        for k in range(NTERMS, 0, -1):
            src, dst = k % 2, 1 - (k % 2)
            for st_ in range(6):
                fns = {}
                if st_ < 4:
                    fns["t"] = (lambda tn, src=src, q=st_: emm(tn, src, q))
                if 1 <= st_ < 5:
                    fns["s"] = (lambda sc, k=k, dst=dst, q=st_ - 1: eev(sc, k, dst, q))
                if st_ >= 2 and (st_ - 2) in (0, 3):
                    fns["v"] = (lambda ve, dst=dst, q=st_ - 2: eadd(ve, dst, q))
                if st_ == 5:
                    fns["v"] = (lambda ve, dst=dst: eadd(ve, dst, 3))
                if fns:
                    pipe.step(**fns)

        def stT(gp):
            pairs = [(g["cc_t_in"].ap().rearrange("(d n) -> d n", d=D)
                      [128 * r:128 * (r + 1), :], pb[0][r][:]) for r in range(8)]
            if t_tg is not None:
                pairs += [(t_tg[128 * r:128 * (r + 1), :], pb[0][r][:]) for r in range(8)]
            DmaBatch(kb, gp, "exst").go(pairs)
            gp.collective_compute(
                "AllGather", OP.bypass, replica_groups=RG,
                ins=[g["cc_t_in"][:]], outs=[g["cc_t_out"][:]]).then_inc(
                kb.sem("agt"), 1)
            gp.wait_ge(kb.sem("agt"), 1)
            return None
        pipe.step(g=stT)

        with nc.Block() as blk:
            pipe.emit(blk, None)


def phase2(kb, g):
    nc = kb.nc
    sb = g["sb"]
    PS = g["PS"]
    tap = g["tap"]
    E0 = kb.ctx.enter_context
    sb["rtot_s"] = E0(nc.sbuf_tensor("rtot_s", [128, 8 * D], BF16))
    t_rtot = tap("rtot", [128, 8 * D], BF16)
    t_x = tap("x_", [LC, T])
    t_ss = tap("sskip", [LC, T])

    ccx = g["cc_xn2_out"].ap().rearrange("(s d t) -> s d t", s=NCORES, d=D)
    cct = g["cc_t_out"].ap().rearrange("(s d n) -> s d n", s=NCORES, d=D)

    with ExitStack() as ctx:
        E = ctx.enter_context
        tz = [E(nc.sbuf_tensor(f"p2tz{r}", [128, D], BF16)) for r in range(8)]
        zb = [[E(nc.sbuf_tensor(f"p2z{p}_{r}", [128, D], BF16)) for r in range(8)]
              for p in range(2)]
        wstg = [E(nc.sbuf_tensor(f"p2wstg{r}", [128, LC], F32)) for r in range(8)]
        inwb = [E(nc.sbuf_tensor(f"p2inw{r}", [128, LC], BF16)) for r in range(8)]
        skwb = [E(nc.sbuf_tensor(f"p2skw{r}", [128, LC], BF16)) for r in range(8)]
        wd1b = [E(nc.sbuf_tensor(f"p2wd1{m}", [128, DTR], BF16)) for m in range(2)]
        wbb = [E(nc.sbuf_tensor(f"p2wbb{m}", [128, NST], BF16)) for m in range(2)]
        wcb = [E(nc.sbuf_tensor(f"p2wcb{m}", [128, NST], BF16)) for m in range(2)]
        xg = [E(nc.sbuf_tensor(f"p2xg{r}", [128, 512], BF16)) for r in range(8)]
        zext = [E(nc.sbuf_tensor(f"p2ze{m}", [128, 515], F32)) for m in range(2)]
        xc = [E(nc.sbuf_tensor(f"p2xc{m}", [128, 512], F32)) for m in range(2)]
        xf = [E(nc.sbuf_tensor(f"p2xf{m}", [128, 512], F32)) for m in range(2)]
        xbf = [E(nc.sbuf_tensor(f"p2xb{m}", [128, 512], BF16)) for m in range(2)]
        ssk = [E(nc.sbuf_tensor(f"p2ss{m}", [128, 512], F32)) for m in range(2)]
        pd1 = E(nc.sbuf_tensor("p2pd1", [DTR, 512], F32))
        pB = E(nc.sbuf_tensor("p2pB", [NST, 512], F32))
        pC = E(nc.sbuf_tensor("p2pC", [NST, 512], F32))

        pipe = Pipe(kb, "p2", ["y", "t", "s", "v"])

        def ldT(sy, blkid=3):
            return DmaBatch(kb, sy, "p2ld").go(
                [(tz[r][:], cct[blkid, 128 * r:128 * (r + 1), :]) for r in range(8)])
        pipe.step(y=ldT)

        for h in range(2):
            def ztr(tn, h=h):
                m = None
                for mo in range(8):
                    for j in range(4):
                        nc_ = 4 * h + j
                        m = tn.matmul(PS(mo)[:, 128 * j:128 * (j + 1)],
                                      tz[nc_][:, 128 * mo:128 * (mo + 1)],
                                      sb["ident_s"][:], start=True, stop=True)
                return m
            pipe.step(t=ztr)

            def zev(sc, h=h):
                m = None
                for mo in range(8):
                    m = sc.activation(out=zb[0][mo][:, 512 * h:512 * h + 512],
                                      in_=PS(mo), func=AF.Copy)
                return m
            pipe.step(s=zev)

        par = 0
        for stage, blkid in enumerate([2, 1, 0]):
            def ldTs(sy, blkid=blkid):
                return DmaBatch(kb, sy, "p2ld").go(
                    [(tz[r][:], cct[blkid, 128 * r:128 * (r + 1), :]) for r in range(8)])
            pipe.step(y=ldTs)
            last = stage == 2
            for h in range(2):
                def zmm(tn, par=par, h=h):
                    m = None
                    for mo in range(8):
                        for r in range(8):
                            m = tn.matmul(PS(mo),
                                          tz[r][:, 128 * mo:128 * (mo + 1)],
                                          zb[par][r][:, 512 * h:512 * h + 512],
                                          start=(r == 0), stop=(r == 7))
                    return m
                pipe.step(t=zmm)

                def zev2(sc, par=par, h=h, last=last):
                    m = None
                    for mo in range(8):
                        if last:
                            dst = sb["rtot_s"][:, D * mo + 512 * h: D * mo + 512 * h + 512]
                        else:
                            dst = zb[1 - par][mo][:, 512 * h:512 * h + 512]
                        m = sc.activation(out=dst, in_=PS(mo), func=AF.Copy)
                    return m
                pipe.step(s=zev2)
            par = 1 - par

        def ldw1(sy):
            return DmaBatch(kb, sy, "p2ld").go(
                [(wstg[r][:], g["in_wT"][128 * r:128 * (r + 1), :]) for r in range(8)])
        pipe.step(y=ldw1)

        def cw1(sc):
            m = None
            for r in range(8):
                m = sc.activation(out=inwb[r][:], in_=wstg[r][:], func=AF.Copy)
            return m
        pipe.step(s=cw1)

        def ldw2(sy):
            return DmaBatch(kb, sy, "p2ld").go(
                [(wstg[r][:], g["skip_wT"][128 * r:128 * (r + 1), :]) for r in range(8)])
        pipe.step(y=ldw2)

        def cw2(sc):
            m = None
            for r in range(8):
                m = sc.activation(out=skwb[r][:], in_=wstg[r][:], func=AF.Copy)
            return m
        pipe.step(s=cw2)

        def ldw3(sy):
            return DmaBatch(kb, sy, "p2ld").go(
                [(wstg[0][:, 0:DTR], g["wd1T"][0:128, :]),
                 (wstg[1][:, 0:DTR], g["wd1T"][128:256, :]),
                 (wstg[2][:, 0:NST], g["wbT"][0:128, :]),
                 (wstg[3][:, 0:NST], g["wbT"][128:256, :]),
                 (wstg[4][:, 0:NST], g["wcT"][0:128, :]),
                 (wstg[5][:, 0:NST], g["wcT"][128:256, :])])
        pipe.step(y=ldw3)

        def cw3(sc):
            for m in range(2):
                sc.activation(out=wd1b[m][:], in_=wstg[m][:, 0:DTR], func=AF.Copy)
                sc.activation(out=wbb[m][:], in_=wstg[2 + m][:, 0:NST], func=AF.Copy)
                sc.activation(out=wcb[m][:], in_=wstg[4 + m][:, 0:NST], func=AF.Copy)
            return sc.activation(out=wcb[1][:], in_=wstg[5][:, 0:NST], func=AF.Copy)
        pipe.step(s=cw3)

        def zinit(ve):
            ve.memset(zext[0][:, 0:3], 0.0)
            return ve.memset(zext[1][:, 0:3], 0.0)
        pipe.step(v=zinit)

        NSEG = T // 512
        ccx2 = ccx

        def p2_ldx(sy, i):
            src, col = (512 * i) // TC, (512 * i) % TC
            return DmaBatch(kb, sy, "p2ld").go(
                [(xg[r][:], ccx2[src, 128 * r:128 * (r + 1), col:col + 512])
                 for r in range(8)])

        def p2_zmm(tn):
            m = None
            for mo in range(2):
                for r in range(8):
                    tn.matmul(PS(mo), inwb[r][:, 128 * mo:128 * (mo + 1)],
                              xg[r][:], start=(r == 0), stop=(r == 7))
            for mo in range(2):
                for r in range(8):
                    m = tn.matmul(PS(2 + mo), skwb[r][:, 128 * mo:128 * (mo + 1)],
                                  xg[r][:], start=(r == 0), stop=(r == 7))
            return m

        def p2_zevac(sc):
            for m in range(2):
                sc.activation(out=zext[m][:, 3:515], in_=PS(m), func=AF.Copy)
            sc.activation(out=ssk[0][:], in_=PS(2), func=AF.Silu)
            return sc.activation(out=ssk[1][:], in_=PS(3), func=AF.Silu)

        def p2_conv(ve):
            m = None
            for mm in range(2):
                ve.tensor_scalar(out=xc[mm][:], in0=zext[mm][:, 3:515],
                                 scalar1=sb["cw_s"][:, 4 * mm + 3:4 * mm + 4],
                                 scalar2=sb["cb_s"][:, mm:mm + 1],
                                 op0=OP.mult, op1=OP.add)
                for kk in range(3):
                    ve.scalar_tensor_tensor(
                        out=xc[mm][:], in0=zext[mm][:, kk:kk + 512],
                        scalar=sb["cw_s"][:, 4 * mm + kk:4 * mm + kk + 1],
                        in1=xc[mm][:], op0=OP.mult, op1=OP.add)
                ve.drain()
                m = ve.tensor_copy(out=zext[mm][:, 0:3], in_=zext[mm][:, 512:515])
            return m

        def p2_silux(sc):
            for m in range(2):
                sc.activation(out=xf[m][:], in_=xc[m][:], func=AF.Silu)
            for m in range(2):
                sc.activation(out=xbf[m][:], in_=xc[m][:], func=AF.Silu)
            return sc.activation(out=xbf[1][:], in_=xc[1][:], func=AF.Silu)

        def p2_pmm(tn):
            for m in range(2):
                tn.matmul(PS(4, 512, DTR), wd1b[m][:], xbf[m][:],
                          start=(m == 0), stop=(m == 1))
            for m in range(2):
                tn.matmul(PS(5, 512, NST), wbb[m][:], xbf[m][:],
                          start=(m == 0), stop=(m == 1))
            m2 = None
            for m in range(2):
                m2 = tn.matmul(PS(6, 512, NST), wcb[m][:], xbf[m][:],
                               start=(m == 0), stop=(m == 1))
            return m2

        def p2_pev(ve):
            ve.tensor_copy(out=pd1[:], in_=PS(4, 512, DTR))
            ve.tensor_copy(out=pB[:], in_=PS(5, 512, NST))
            return ve.tensor_copy(out=pC[:], in_=PS(6, 512, NST))

        def p2_stx(sy, i):
            c0 = 512 * i
            bcd = g["cc_bcd_in"].ap().rearrange("(r t) -> r t", r=96)
            pairs = [(g["x_dr"][128 * m:128 * (m + 1), c0:c0 + 512], xf[m][:])
                     for m in range(2)]
            pairs += [(g["sskip_dr"][128 * m:128 * (m + 1), c0:c0 + 512], ssk[m][:])
                      for m in range(2)]
            pairs += [(bcd[0:DTR, c0:c0 + 512], pd1[:]),
                      (bcd[DTR:DTR + NST, c0:c0 + 512], pB[:]),
                      (bcd[DTR + NST:96, c0:c0 + 512], pC[:])]
            if t_x is not None:
                pairs += [(t_x[128 * m:128 * (m + 1), c0:c0 + 512], xf[m][:])
                          for m in range(2)]
            if t_ss is not None:
                pairs += [(t_ss[128 * m:128 * (m + 1), c0:c0 + 512], ssk[m][:])
                          for m in range(2)]
            return DmaBatch(kb, sy, "p2st").go(pairs)

        pipe.step(y=(lambda sy: p2_ldx(sy, 0)))
        for i in range(NSEG):
            fns = {"t": p2_zmm}
            if i >= 1:
                fns["y"] = (lambda sy, i=i: p2_stx(sy, i - 1))
            pipe.step(**fns)
            pipe.step(s=p2_zevac)
            fns = {"v": p2_conv}
            if i + 1 < NSEG:
                fns["y"] = (lambda sy, i=i: p2_ldx(sy, i + 1))
            pipe.step(**fns)
            pipe.step(s=p2_silux)
            pipe.step(t=p2_pmm)
            pipe.step(v=p2_pev)
        pipe.step(y=(lambda sy: p2_stx(sy, NSEG - 1)))

        if t_rtot is not None:
            def strt(sy):
                return DmaBatch(kb, sy, "p2st").go([(t_rtot[:], sb["rtot_s"][:])])
            pipe.step(y=strt)

        with nc.Block() as blk:
            pipe.emit(blk, None)


def phase3(kb, g):
    nc = kb.nc
    tap = g["tap"]
    t_bcd = tap("bcd", [96, T])
    with nc.Block() as blk:
        @blk.gpsimd
        def _(gp):
            gp.collective_compute(
                "AllReduce", OP.add, replica_groups=RG,
                ins=[g["cc_bcd_in"][:]], outs=[g["cc_bcd_out"][:]]).then_inc(
                kb.sem("p3g"), 1)
            gp.wait_ge(kb.sem("p3g"), 1)
            if t_bcd is not None:
                tgt = dma(gp, t_bcd[:],
                          g["cc_bcd_out"].ap().rearrange("(r t) -> r t", r=96),
                          kb, "p3st")
                gp.wait_ge(kb.sem("p3st"), tgt)


def phase4(kb, g):
    nc = kb.nc
    sb = g["sb"]
    PS = g["PS"]
    tap = g["tap"]
    t_y = tap("yscan", [LC, T])
    t_gy = tap("gy", [LC, T], BF16)
    t_dl = tap("delta", [LC, T])
    t_d2 = tap("d2raw", [LC, T])
    t_hb = tap("hb", [128, NST * 2 * 512])

    bcd = g["cc_bcd_out"].ap().rearrange("(r t) -> r t", r=96)
    ggy = g["cc_gy_in"].ap().rearrange("(l t) -> l t", l=LC)
    S = 512
    NSEG = T // S

    with ExitStack() as ctx:
        E = ctx.enter_context
        oh16_s = E(nc.sbuf_tensor("p4oh16", [NST, NST * 128], BF16))
        wd2f = E(nc.sbuf_tensor("p4wd2f", [DTR, LC], F32))
        wd2b = E(nc.sbuf_tensor("p4wd2b", [DTR, LC], BF16))
        d1f = E(nc.sbuf_tensor("p4d1f", [DTR, S], F32))
        d1b = E(nc.sbuf_tensor("p4d1b", [DTR, S], BF16))
        btf = E(nc.sbuf_tensor("p4btf", [NST, S], F32))
        btb = E(nc.sbuf_tensor("p4btb", [NST, S], BF16))
        ctf = E(nc.sbuf_tensor("p4ctf", [NST, S], F32))
        ctb = E(nc.sbuf_tensor("p4ctb", [NST, S], BF16))
        dlt = [E(nc.sbuf_tensor(f"p4dl{m}", [128, S], F32)) for m in range(2)]
        sr1 = [E(nc.sbuf_tensor(f"p4sr{m}", [128, S], F32)) for m in range(2)]
        d2r = [E(nc.sbuf_tensor(f"p4d2r{m}", [128, S], F32)) for m in range(2)]
        sa1 = [E(nc.sbuf_tensor(f"p4sa{m}", [128, S], F32)) for m in range(2)]
        dx = [E(nc.sbuf_tensor(f"p4dx{m}", [128, S], F32)) for m in range(2)]
        xw = [E(nc.sbuf_tensor(f"p4xw{m}", [128, S], F32)) for m in range(2)]
        ssk = [E(nc.sbuf_tensor(f"p4ss{m}", [128, S], F32)) for m in range(2)]
        aa = E(nc.sbuf_tensor("p4aa", [128, 16 * S], F32))
        bb = [E(nc.sbuf_tensor(f"p4bb{m}", [128, S], F32)) for m in range(2)]
        hbuf = E(nc.sbuf_tensor("p4hb", [128, NST * 2 * S], F32))
        state = E(nc.sbuf_tensor("p4st", [128, 32], F32))
        yy = [E(nc.sbuf_tensor(f"p4yy{m}", [128, S], F32)) for m in range(2)]
        gyb = [E(nc.sbuf_tensor(f"p4gy{m}", [128, S], BF16)) for m in range(2)]

        hb3 = hbuf.ap().rearrange("p (n q) -> p n q", q=2 * S)
        hb3r = hbuf.ap().rearrange("p (n q) -> p q n", q=2 * S)

        pipe = Pipe(kb, "p4", ["y", "t", "s", "v"])

        def ldw(sy):
            return DmaBatch(kb, sy, "p4ld").go([(wd2f[:], g["wd2T"][:]),
                                                (oh16_s[:], g["oh16"][:])])
        pipe.step(y=ldw)

        def cwz(sc):
            return sc.activation(out=wd2b[:], in_=wd2f[:], func=AF.Copy)
        pipe.step(s=cwz)

        def sinit(ve):
            return ve.memset(state[:], 0.0)
        pipe.step(v=sinit)

        for i in range(NSEG):
            c0 = S * i

            def ld(sy, c0=c0):
                return DmaBatch(kb, sy, "p4ld").go(
                    [(d1f[:], bcd[0:DTR, c0:c0 + S]),
                     (btf[:], bcd[DTR:DTR + NST, c0:c0 + S]),
                     (ctf[:], bcd[DTR + NST:96, c0:c0 + S]),
                     (xw[0][:], g["x_dr"][0:128, c0:c0 + S]),
                     (xw[1][:], g["x_dr"][128:256, c0:c0 + S]),
                     (ssk[0][:], g["sskip_dr"][0:128, c0:c0 + S]),
                     (ssk[1][:], g["sskip_dr"][128:256, c0:c0 + S])])
            pipe.step(y=ld)

            def cst(sc):
                sc.activation(out=d1b[:], in_=d1f[:], func=AF.Copy)
                sc.activation(out=btb[:], in_=btf[:], func=AF.Copy)
                return sc.activation(out=ctb[:], in_=ctf[:], func=AF.Copy)
            pipe.step(s=cst)

            def d2mm(tn):
                m = None
                for et in range(2):
                    m = tn.matmul(PS(et), wd2b[:, 128 * et:128 * (et + 1)],
                                  d1b[:], start=True, stop=True)
                return m
            pipe.step(t=d2mm)

            def spl(sc):
                m = None
                for et in range(2):
                    if t_d2 is not None:
                        sc.activation(out=d2r[et][:], in_=PS(et), func=AF.Copy)
                    sc.activation(out=sr1[et][:], in_=PS(et), func=AF.Relu,
                                  bias=sb["wd2b_s"][:, et:et + 1])
                    sc.activation(out=sa1[et][:], in_=PS(et), func=AF.Abs,
                                  bias=sb["wd2b_s"][:, et:et + 1])
                    sc.activation(out=sa1[et][:], in_=sa1[et][:], func=AF.Exp,
                                  scale=CP(2, 1) if False else sb["cp_s"][:, 2:3])
                    m = sc.activation(out=sa1[et][:], in_=sa1[et][:], func=AF.Ln,
                                      bias=sb["cp_s"][:, 3:4])
                return m
            pipe.step(s=spl)

            def mdx(ve):
                ve.tensor_add(dlt[0][:], sr1[0][:], sa1[0][:])
                ve.tensor_add(dlt[1][:], sr1[1][:], sa1[1][:])
                ve.drain()
                ve.tensor_mul(dx[0][:], dlt[0][:], xw[0][:])
                return ve.tensor_mul(dx[1][:], dlt[1][:], xw[1][:])
            pipe.step(v=mdx)

            def reps2(tn, i, gp):
                m = None
                b0 = 4 * ((gp + 1) % 2)
                for j2 in range(2):
                    n = 2 * gp + j2
                    m = tn.matmul(PS(b0 + 2 * j2), oh16_s[:, 128 * n:128 * (n + 1)],
                                  btb[:], start=True, stop=True)
                    m = tn.matmul(PS(b0 + 2 * j2 + 1), oh16_s[:, 128 * n:128 * (n + 1)],
                                  ctb[:], start=True, stop=True)
                return m

            def aexp2(sc, i, gp):
                m = None
                for j2 in range(2):
                    n = 2 * gp + j2
                    for et in range(2):
                        sl = (gp % 2) * 4 + 2 * j2 + et
                        m = sc.activation(out=aa[:, sl * S:(sl + 1) * S],
                                          in_=dlt[et][:], func=AF.Exp,
                                          scale=sb["acoef_s"][:, 16 * et + n:16 * et + n + 1])
                return m

            def scans2(ve, i, gp):
                b0 = 4 * ((gp + 1) % 2)
                for j2 in range(2):
                    n = 2 * gp + j2
                    for et in range(2):
                        hcol = n * 2 * S + et * S
                        sl = (gp % 2) * 4 + 2 * j2 + et
                        ve.tensor_mul(bb[et][:], dx[et][:], PS(b0 + 2 * j2))
                        ve.tensor_tensor_scan(
                            out=hbuf[:, hcol:hcol + S],
                            data0=aa[:, sl * S:(sl + 1) * S],
                            data1=bb[et][:],
                            initial=state[:, 16 * et + n:16 * et + n + 1],
                            op0=OP.mult, op1=OP.add)
                ve.drain()
                for et in range(2):
                    ve.tensor_copy(out=state[:, 16 * et + 2 * gp:16 * et + 2 * gp + 2],
                                   in_=hb3[:, 2 * gp:2 * gp + 2, et * S + S - 1:et * S + S])
                ve.drain()
                m = None
                for j2 in range(2):
                    n = 2 * gp + j2
                    for et in range(2):
                        hcol = n * 2 * S + et * S
                        m = ve.tensor_tensor(out=hbuf[:, hcol:hcol + S],
                                             in0=hbuf[:, hcol:hcol + S],
                                             in1=PS(b0 + 2 * j2 + 1), op=OP.mult)
                return m

            pipe.step(t=(lambda tn, i=i: reps2(tn, i, 0)),
                      s=(lambda sc, i=i: aexp2(sc, i, 0)))
            for gp in range(8):
                fns = {"v": (lambda ve, i=i, gp=gp: scans2(ve, i, gp))}
                if gp < 7:
                    fns["t"] = (lambda tn, i=i, gp=gp: reps2(tn, i, gp + 1))
                    fns["s"] = (lambda sc, i=i, gp=gp: aexp2(sc, i, gp + 1))
                pipe.step(**fns)

            def redu(ve):
                m = None
                for et in range(2):
                    m = ve.reduce_sum(out=yy[et][:],
                                      in_=hb3r[:, et * S:(et + 1) * S, :],
                                      axis=AX.X)
                return m
            pipe.step(v=redu)

            def gate(ve):
                m = None
                for et in range(2):
                    ve.scalar_tensor_tensor(out=yy[et][:], in0=xw[et][:],
                                            scalar=sb["wdt_s"][:, et:et + 1],
                                            in1=yy[et][:], op0=OP.mult, op1=OP.add)
                    m = ve.tensor_mul(gyb[et][:], yy[et][:], ssk[et][:])
                return m
            pipe.step(v=gate)

            def stg(sy, i=i, c0=c0):
                pairs = [(ggy[128 * et:128 * (et + 1), c0:c0 + S], gyb[et][:])
                         for et in range(2)]
                if t_dl is not None:
                    pairs += [(t_dl[128 * et:128 * (et + 1), c0:c0 + S], dlt[et][:])
                              for et in range(2)]
                if t_d2 is not None:
                    pairs += [(t_d2[128 * et:128 * (et + 1), c0:c0 + S], d2r[et][:])
                              for et in range(2)]
                if t_hb is not None and i == 0:
                    pairs += [(t_hb[:], hbuf[:])]
                if t_y is not None:
                    pairs += [(t_y[128 * et:128 * (et + 1), c0:c0 + S], yy[et][:])
                              for et in range(2)]
                if t_gy is not None:
                    pairs += [(t_gy[128 * et:128 * (et + 1), c0:c0 + S], gyb[et][:])
                              for et in range(2)]
                return DmaBatch(kb, sy, "p4st").go(pairs)
            pipe.step(y=stg)

        with nc.Block() as blk:
            pipe.emit(blk, None)

        with nc.Block() as blk2:
            @blk2.gpsimd
            def _(gp):
                gp.collective_compute(
                    "AllGather", OP.bypass, replica_groups=RG,
                    ins=[g["cc_gy_in"][:]], outs=[g["cc_gy_out"][:]]).then_inc(
                    kb.sem("aggy"), 1)
                gp.wait_ge(kb.sem("aggy"), 1)
                v = gp.value_load(sb["tok0_s"][0:1, 0:1])
                gg = g["cc_gy_out"].ap().rearrange("(r t) -> r t", r=NCORES * LC)
                tgt = None
                for r in range(16):
                    tgt = dma(gp, g["gyw_dr"][128 * r:128 * (r + 1), :],
                              gg[128 * r:128 * (r + 1), bass.ds(v, TC)], kb, "gyw")
                gp.wait_ge(kb.sem("gyw"), tgt)


def phase_ffn(kb, g):
    nc = kb.nc
    sb = g["sb"]
    PS = g["PS"]
    tap = g["tap"]
    t_abel = tap("abel", [D, TC], BF16)
    t_nona = tap("nona", [D, TC], BF16)
    t_moe2 = tap("moe2", [D, TC], BF16)
    t_rt2 = tap("rtot2", [128, 8 * D], BF16)

    with ExitStack() as ctx:
        E = ctx.enter_context
        xnfb = [E(nc.sbuf_tensor(f"pfxn{r}", [128, 1024], BF16)) for r in range(8)]
        wkf = [E(nc.sbuf_tensor(f"pfwf{p}", [128, 2048], F32)) for p in range(2)]
        wkb = [E(nc.sbuf_tensor(f"pfwb{p}", [128, 2048], BF16)) for p in range(2)]
        he = [E(nc.sbuf_tensor(f"pfhe{r}", [128, 1024], BF16)) for r in range(16)]
        moe = [E(nc.sbuf_tensor(f"pfmo{r}", [128, 1024], F32)) for r in range(8)]
        nonab = [E(nc.sbuf_tensor(f"pfnb{r}", [128, 1024], BF16)) for r in range(8)]
        abelb = [E(nc.sbuf_tensor(f"pfab{r}", [128, 1024], BF16)) for r in range(8)]
        basef = [E(nc.sbuf_tensor(f"pfbs{r}", [128, 1024], F32)) for r in range(2)]
        wrepb = E(nc.sbuf_tensor("pfwr", [128, 1024], BF16))
        eb2b = E(nc.sbuf_tensor("pfeb2", [NEXP, D], BF16))
        gcwb = E(nc.sbuf_tensor("pfgcw", [128, 16], BF16))
        grow = E(nc.sbuf_tensor("pfgrow", [1, 1024], F32))
        ghi = E(nc.sbuf_tensor("pfghi", [1, 1024], BF16))
        glo = E(nc.sbuf_tensor("pfglo", [1, 1024], F32))
        glob = E(nc.sbuf_tensor("pfglob", [1, 1024], BF16))
        gyr = abelb + nonab

        pipe = Pipe(kb, "pf", ["y", "t", "s", "v"])
        step = pipe.step

        def ldsm(sy):
            return DmaBatch(kb, sy, "pfld").go([(basef[0][0:NEXP, :], g["eb2_n"][:]),
                                                (basef[1][0:128, 0:16], g["gcw_t"][:])])
        step(y=ldsm)

        def cssm(sc):
            sc.activation(out=eb2b[:], in_=basef[0][0:NEXP, :], func=AF.Copy)
            return sc.activation(out=gcwb[:], in_=basef[1][:, 0:16], func=AF.Copy)
        step(s=cssm)

        for H in range(2):
            hc0 = 1024 * H

            def ldxn(sy, hc0=hc0):
                return DmaBatch(kb, sy, "pfld").go(
                    [(xnfb[r][:], g["xnf_dr"][128 * r:128 * (r + 1), hc0:hc0 + 1024])
                     for r in range(8)])
            step(y=ldxn)

            units = []
            raw = []
            for e in range(NEXP):
                raw.append(("wrep", e))
                for mt in range(16):
                    raw.append(("w1", e, mt))
                raw.append(("wsc", e))
                for mo in range(8):
                    raw.append(("w2", e, mo))
            raw.append(("csmoe",))
            for mo in range(8):
                raw.append(("na", mo))
            raw.append(("tapmoe",))
            for q in range(2):
                for mt in range(32):
                    raw.append(("a1", q, mt))
                for mo in range(8):
                    raw.append(("a2a", q, mo))
                    raw.append(("a2b", q, mo))
            prev = None
            for u in raw:
                grp = u[0] if u[0] not in ("a2a", "a2b") else "a2"
                if prev is not None and grp != prev:
                    units.append(("pad",))
                    units.append(("pad",))
                units.append(u)
                prev = grp
            units.append(("pad",))
            units.append(("pad",))

            ew1r = g["ew1"].ap().rearrange("e (r p) h -> e p r h", p=128)
            ew2r = g["ew2"].ap().rearrange("e (r p) d -> e p r d", p=128)
            aw1r = g["aw1"].ap().rearrange("(r p) h -> p r h", p=128)
            aw2r = g["aw2"].ap().rearrange("(r p) d -> p r d", p=128)

            def u_load(sy, u, p, hc0=hc0):
                kind = u[0]
                d = DmaBatch(kb, sy, "pfld")
                wk3 = wkf[p].ap().rearrange("q (r c) -> q r c", c=128)
                if kind == "w1":
                    _, e, mt = u
                    return d.go([(wk3[:, 4 * j:4 * (j + 1), :],
                                  ew1r[e, :, 4 * j:4 * (j + 1), 128 * mt:128 * (mt + 1)])
                                 for j in range(2)])
                if kind == "w2":
                    _, e, mo = u
                    return d.go([(wk3[:, 8 * j:8 * (j + 1), :],
                                  ew2r[e, :, 8 * j:8 * (j + 1), 128 * mo:128 * (mo + 1)])
                                 for j in range(2)])
                if kind == "a1":
                    _, q_, mt = u
                    return d.go([(wk3[:, 4 * j:4 * (j + 1), :],
                                  aw1r[:, 4 * j:4 * (j + 1), 128 * mt:128 * (mt + 1)])
                                 for j in range(2)])
                if kind == "a2a":
                    _, q_, mo = u
                    return d.go([(wk3[:, 8 * j:8 * (j + 1), :],
                                  aw2r[:, 8 * j:8 * (j + 1), 128 * mo:128 * (mo + 1)])
                                 for j in range(2)])
                if kind == "a2b":
                    _, q_, mo = u
                    return d.go([(wk3[:, 8 * j:8 * (j + 1), :],
                                  aw2r[:, 16 + 8 * j:16 + 8 * (j + 1), 128 * mo:128 * (mo + 1)])
                                 for j in range(2)])
                return None

            def u_cast(sc, u, p):
                kind = u[0]
                if kind in ("w1", "a1"):
                    return sc.activation(out=wkb[p][:, 0:1024], in_=wkf[p][:, 0:1024],
                                         func=AF.Copy)
                if kind in ("w2", "a2a", "a2b"):
                    return sc.activation(out=wkb[p][:], in_=wkf[p][:], func=AF.Copy)
                return None

            def u_mm(tn, u, p, hc0=hc0):
                kind = u[0]
                m = None
                if kind == "wrep":
                    e = u[1]
                    for tt in range(2):
                        m = tn.matmul(PS(4 + tt), sb["oh8_s"][:, 128 * e:128 * (e + 1)],
                                      sb["w6_s"][:, hc0 + 512 * tt:hc0 + 512 * (tt + 1)],
                                      start=True, stop=True)
                elif kind == "w1":
                    bk = 2 * (p % 2)
                    for r in range(8):
                        for tt in range(2):
                            m = tn.matmul(PS(bk + tt), wkb[p][:, 128 * r:128 * (r + 1)],
                                          xnfb[r][:, 512 * tt:512 * (tt + 1)],
                                          start=(r == 0), stop=(r == 7))
                elif kind == "w2":
                    _, e, mo = u
                    bk = 4 + 2 * (p % 2)
                    for r in range(16):
                        for tt in range(2):
                            m = tn.matmul(PS(bk + tt), wkb[p][:, 128 * r:128 * (r + 1)],
                                          he[r][:, 512 * tt:512 * (tt + 1)],
                                          start=(r == 0),
                                          stop=(r == 15 and e != 0))
                    if e == 0:
                        for tt in range(2):
                            m = tn.matmul(
                                PS(bk + tt), eb2b[0:NEXP, 128 * mo:128 * (mo + 1)],
                                sb["w6_s"][0:NEXP, hc0 + 512 * tt:hc0 + 512 * (tt + 1)],
                                start=False, stop=True)
                elif kind == "na":
                    mo = u[1]
                    bk = 4 + 2 * (mo % 2)
                    for tt in range(2):
                        for r in range(8):
                            m = tn.matmul(PS(bk + tt),
                                          sb["rtot_s"][:, D * r + 128 * mo:D * r + 128 * (mo + 1)],
                                          gyr[r][:, 512 * tt:512 * (tt + 1)],
                                          start=(r == 0), stop=(r == 7))
                elif kind == "a1":
                    _, q, mt = u
                    bk = 2 * (p % 2)
                    for r in range(8):
                        m = tn.matmul(PS(bk), wkb[p][:, 128 * r:128 * (r + 1)],
                                      xnfb[r][:, 512 * q:512 * (q + 1)],
                                      start=(r == 0), stop=(r == 7))
                elif kind == "a2a":
                    _, q, mo = u
                    bk = 4 + 2 * (mo % 2)
                    for r in range(16):
                        m = tn.matmul(PS(bk), wkb[p][:, 128 * r:128 * (r + 1)],
                                      he[r % 16][:, 512 * (r // 16):512 * (r // 16) + 512],
                                      start=(r == 0), stop=False)
                elif kind == "a2b":
                    _, q, mo = u
                    bk = 4 + 2 * (mo % 2)
                    for r in range(16):
                        m = tn.matmul(PS(bk), wkb[p][:, 128 * r:128 * (r + 1)],
                                      he[r % 16][:, 512 * (1):512 * (1) + 512]
                                      if False else he[(16 + r) % 16][:, 512:1024],
                                      start=False, stop=(r == 15))
                return m

            def u_post_s(sc, u, p, hc0=hc0):
                kind = u[0]
                m = None
                if kind == "wrep":
                    sc.activation(out=wrepb[:, 0:512], in_=PS(4), func=AF.Copy)
                    m = sc.activation(out=wrepb[:, 512:1024], in_=PS(5), func=AF.Copy)
                elif kind == "w1":
                    _, e, mt = u
                    bk = 2 * (p % 2)
                    for tt in range(2):
                        m = sc.activation(out=he[mt][:, 512 * tt:512 * (tt + 1)],
                                          in_=PS(bk + tt), func=AF.Gelu,
                                          bias=sb["eb1_s"][:, 16 * e + mt:16 * e + mt + 1])
                elif kind == "na":
                    mo = u[1]
                    bk = 4 + 2 * (mo % 2)
                    sc.activation(out=nonab[mo][:, 0:512], in_=PS(bk), func=AF.Copy)
                    m = sc.activation(out=nonab[mo][:, 512:1024], in_=PS(bk + 1), func=AF.Copy)
                elif kind == "a1":
                    _, q, mt = u
                    bk = 2 * (p % 2)
                    h1q = he[mt % 16][:, 512 * (mt // 16):512 * (mt // 16) + 512]
                    m = sc.activation(out=h1q, in_=PS(bk), func=AF.Gelu,
                                      bias=sb["ab1_s"][:, mt:mt + 1])
                elif kind == "a2b":
                    _, q, mo = u
                    bk = 4 + 2 * (mo % 2)
                    m = sc.activation(out=abelb[mo][:, 512 * q:512 * q + 512], in_=PS(bk),
                                      func=AF.Identity, bias=sb["ab2_s"][:, mo:mo + 1])
                elif kind == "csmoe":
                    for mo in range(8):
                        m = sc.activation(out=gyr[mo][:], in_=moe[mo][:], func=AF.Copy)
                return m

            def u_post_v(ve, u, p):
                kind = u[0]
                m = None
                if kind == "wsc":
                    for mt in range(16):
                        m = ve.tensor_mul(he[mt][:], he[mt][:], wrepb[:])
                elif kind == "w2":
                    _, e, mo = u
                    bk = 4 + 2 * (p % 2)
                    for tt in range(2):
                        if e == 0:
                            ve.tensor_copy(out=moe[mo][:, 512 * tt:512 * (tt + 1)],
                                           in_=PS(bk + tt))
                        else:
                            ve.tensor_add(moe[mo][:, 512 * tt:512 * (tt + 1)],
                                          moe[mo][:, 512 * tt:512 * (tt + 1)],
                                          PS(bk + tt))
                    m = None
                return m

            NU = len(units)
            for k in range(NU + 3):
                fns = {}
                if k < NU:
                    u = units[k]
                    if u[0] in ("w1", "w2", "a1", "a2a", "a2b"):
                        fns["y"] = (lambda sy, u=u, p=k % 2, f=u_load: f(sy, u, p))
                    elif u[0] == "tapmoe" and t_moe2 is not None:
                        fns["y"] = (lambda sy, hc0=hc0: DmaBatch(kb, sy, "pfst").go(
                            [(t_moe2[128 * mo:128 * (mo + 1), hc0:hc0 + 1024], gyr[mo][:])
                             for mo in range(8)]))
                uc = units[k - 1] if 1 <= k <= NU else None
                um = units[k - 2] if 2 <= k <= NU + 1 else None
                up = units[k - 3] if 3 <= k <= NU + 2 else None

                def s_slot(sc, uc=uc, up=up, k=k, f1=u_cast, f2=u_post_s):
                    m = None
                    if uc is not None:
                        m = f1(sc, uc, (k - 1) % 2) or m
                    if up is not None:
                        m = f2(sc, up, (k - 3) % 2) or m
                    return m
                if (uc is not None and uc[0] in ("w1", "w2", "a1", "a2a", "a2b")) or \
                   (up is not None and up[0] in ("wrep", "w1", "na", "a1", "a2b", "csmoe")):
                    fns["s"] = s_slot
                if um is not None and um[0] in ("wrep", "w1", "w2", "na", "a1", "a2a", "a2b"):
                    fns["t"] = (lambda tn, um=um, k=k, f=u_mm: f(tn, um, (k - 2) % 2))
                if up is not None and up[0] in ("wsc", "w2"):
                    fns["v"] = (lambda ve, up=up, k=k, f=u_post_v: f(ve, up, (k - 3) % 2))
                if fns:
                    step(**fns)

            def gcm(tn, hc0=hc0):
                m = None
                for tt in range(2):
                    for mo in range(8):
                        tn.matmul(PS(tt, 512, 1), gcwb[:, mo:mo + 1],
                                  abelb[mo][:, 512 * tt:512 * (tt + 1)],
                                  start=(mo == 0), stop=False)
                    for mo in range(8):
                        m = tn.matmul(PS(tt, 512, 1), gcwb[:, 8 + mo:9 + mo],
                                      nonab[mo][:, 512 * tt:512 * (tt + 1)],
                                      start=False, stop=(mo == 7))
                return m
            step(t=gcm)

            def gsig(sc):
                sc.activation(out=grow[0:1, 0:512], in_=PS(0, 512, 1), func=AF.Sigmoid,
                              bias=sb_gcb(sb))
                sc.activation(out=grow[0:1, 512:1024], in_=PS(1, 512, 1), func=AF.Sigmoid,
                              bias=sb_gcb(sb))
                return sc.activation(out=ghi[:], in_=grow[:], func=AF.Copy)
            step(s=gsig)

            def gl(ve):
                return ve.tensor_sub(glo[:], grow[:], ghi[:])
            step(v=gl)

            def glc(sc):
                return sc.activation(out=glob[:], in_=glo[:], func=AF.Copy)
            step(s=glc)

            def gbc(tn):
                m = None
                for tt in range(2):
                    tn.matmul(PS(4 + tt), sb["ones1r_s"][:], ghi[0:1, 512 * tt:512 * (tt + 1)],
                              start=True, stop=False)
                    m = tn.matmul(PS(4 + tt), sb["ones1r_s"][:], glob[0:1, 512 * tt:512 * (tt + 1)],
                                  start=False, stop=True)
                return m
            step(t=gbc)

            def mixv(ve):
                m = None
                for mo in range(8):
                    ve.tensor_copy(out=moe[mo][:], in_=abelb[mo][:])
                    ve.tensor_copy(out=basef[0][:], in_=nonab[mo][:])
                    ve.drain()
                    ve.tensor_sub(basef[0][:], basef[0][:], moe[mo][:])
                    for tt in range(2):
                        ve.tensor_mul(basef[0][:, 512 * tt:512 * (tt + 1)],
                                      basef[0][:, 512 * tt:512 * (tt + 1)], PS(4 + tt))
                    ve.drain()
                    m = ve.tensor_add(moe[mo][:], moe[mo][:], basef[0][:])
                return m
            step(v=mixv)

            if t_rt2 is not None:
                def strt2(sy):
                    return DmaBatch(kb, sy, "pfst").go([(t_rt2[:], sb["rtot_s"][:])])
                step(y=strt2)
            if t_abel is not None:
                def stab(sy, hc0=hc0):
                    return DmaBatch(kb, sy, "pfst").go(
                        [(t_abel[128 * mo:128 * (mo + 1), hc0:hc0 + 1024], abelb[mo][:])
                         for mo in range(8)])
                step(y=stab)
            if t_nona is not None:
                def stna(sy, hc0=hc0):
                    return DmaBatch(kb, sy, "pfst").go(
                        [(t_nona[128 * mo:128 * (mo + 1), hc0:hc0 + 1024], nonab[mo][:])
                         for mo in range(8)])
                step(y=stna)

            for r in range(17):
                fns = {}
                if r < 16:
                    fns["y"] = (lambda sy, r=r: DmaBatch(kb, sy, "pfld").go(
                        [(wkf[r % 2][:, 0:1024], g["out_wT"][128 * r:128 * (r + 1), :])]))
                if r >= 1:
                    fns["s"] = (lambda sc, r=r: sc.activation(
                        out=he[r - 1][:], in_=wkf[(r - 1) % 2][:, 0:1024], func=AF.Copy))
                step(**fns)

            def ldgy(sy, hc0=hc0):
                pairs = [(gyr[r][:],
                          g["gyw_dr"][128 * r:128 * (r + 1), hc0:hc0 + 1024])
                         for r in range(16)]
                return DmaBatch(kb, sy, "pfld").go(pairs)
            step(y=ldgy)

            for mo in range(0, 8, 2):
                def ldbs(sy, mo=mo, hc0=hc0):
                    return DmaBatch(kb, sy, "pfld").go(
                        [(basef[j][:], g["base_dr"][128 * (mo + j):128 * (mo + j + 1),
                                                    hc0:hc0 + 1024]) for j in range(2)])
                step(y=ldbs)

                def mmat(tn, mo=mo):
                    m = None
                    for j in range(2):
                        for r in range(16):
                            for tt in range(2):
                                m = tn.matmul(PS(2 * j + tt), he[r][:, 128 * (mo + j):128 * (mo + j + 1)],
                                              gyr[r][:, 512 * tt:512 * (tt + 1)],
                                              start=(r == 0), stop=(r == 15))
                    return m
                step(t=mmat)

                def fsum(ve, mo=mo):
                    m = None
                    for j in range(2):
                        for tt in range(2):
                            ve.tensor_add(basef[j][:, 512 * tt:512 * (tt + 1)],
                                          basef[j][:, 512 * tt:512 * (tt + 1)],
                                          PS(2 * j + tt))
                        ve.drain()
                        m = ve.tensor_add(basef[j][:], basef[j][:], moe[mo + j][:])
                    return m
                step(v=fsum)

                def stout(sy, mo=mo, hc0=hc0):
                    pairs = [(g["outT"][128 * (mo + j):128 * (mo + j + 1), hc0:hc0 + 1024],
                              basef[j][:]) for j in range(2)]
                    return DmaBatch(kb, sy, "pfst").go(pairs)
                step(y=stout)

        with nc.Block() as blk:
            pipe.emit(blk, None)


def sb_gcb(sb):
    return sb["gcb_s"][0:1, 0:1]


def _stage_inputs(i):
    import ml_dtypes
    bf = ml_dtypes.bfloat16
    resid = i["L0_attn_resid"][0]
    cores = []
    eye128 = np.eye(128, dtype=bf)
    id8f = np.eye(8, dtype=np.float32)
    oh16 = np.zeros((NST, NST * 128), dtype=bf)
    for n in range(NST):
        oh16[n, 128 * n:128 * (n + 1)] = 1
    oh8 = np.zeros((8, 8 * 128), dtype=bf)
    for n in range(8):
        oh8[n, 128 * n:128 * (n + 1)] = 1
    ones1 = np.ones((128, 1), dtype=bf)
    ones1r = np.ones((1, 128), dtype=bf)
    cvals = [1e-5, 1.0 / D, -1.0, 1.0] + [1.0 / (k + 1) for k in range(20)] + [0.0] * 8
    cpool = np.tile(np.array(cvals, np.float32)[None, :], (128, 1))

    def pack_cols(v, ncol):
        return np.ascontiguousarray(v.reshape(ncol, 128).T)

    aT = lambda x: np.ascontiguousarray(np.asarray(x).T)
    C = np.ascontiguousarray
    for c in range(NCORES):
        tsl = slice(TC * c, TC * (c + 1))
        lsl = slice(LC * c, LC * (c + 1))
        m = {
            "residT": aT(resid[tsl]),
            "norm_w_t": pack_cols(i["L0_attn_norm_w"], 8),
            "ln_g_t": pack_cols(i["L0_ffn_ln_g"], 8),
            "ln_b_t": pack_cols(i["L0_ffn_ln_b"], 8),
            "gate_w_t": C(i["L0_ffn_gate_w"].reshape(8, 128, NEXP).transpose(1, 0, 2)
                          .reshape(128, 8 * NEXP)),
            "gate_b_t": C(i["L0_ffn_gate_b"].reshape(NEXP, 1)),
            "in_wT": aT(i["L0_attn_in_w"][lsl]),
            "skip_wT": aT(i["L0_attn_skip_w"][lsl]),
            "wd1T": aT(i["L0_attn_wd1"][:, lsl]),
            "wbT": aT(i["L0_attn_wb"][:, lsl]),
            "wcT": aT(i["L0_attn_wc"][:, lsl]),
            "wd2T": aT(i["L0_attn_wd2"][lsl]),
            "a_log_t": C(i["L0_attn_A_log"][lsl].reshape(2, 128, NST)
                         .transpose(1, 0, 2).reshape(128, 2 * NST)),
            "conv_w_t": C(i["L0_attn_conv_w"][lsl].reshape(2, 128, 4)
                          .transpose(1, 0, 2).reshape(128, 8)),
            "conv_b_t": pack_cols(i["L0_attn_conv_b"][lsl], 2),
            "wd2_b_t": pack_cols(i["L0_attn_wd2_b"][lsl], 2),
            "w_d_t": pack_cols(i["L0_attn_W_D"][lsl], 2),
            "out_wT": aT(i["L0_attn_out_w"]),
            "aw1": C(i["L0_ffn_aw1"]),
            "ab1_t": pack_cols(i["L0_ffn_ab1"], 32),
            "aw2": C(i["L0_ffn_aw2"]),
            "ab2_t": pack_cols(i["L0_ffn_ab2"], 8),
            "ew1": C(i["L0_ffn_ew1"]),
            "eb1_t": C(i["L0_ffn_eb1"].reshape(NEXP, 16, 128)
                       .transpose(2, 0, 1).reshape(128, NEXP * 16)),
            "ew2": C(i["L0_ffn_ew2"]),
            "eb2_n": C(i["L0_ffn_eb2"]),
            "gens_n": C(i["L0_ffn_gens"][c % 4]),
            "gcw_t": pack_cols(i["L0_ffn_gc_w"][:, 0], 16),
            "gc_b_t": C(i["L0_ffn_gc_b"].reshape(1, 1)),
            "ident": eye128, "id8f": id8f, "oh16": oh16, "oh8": oh8,
            "ones1": ones1, "ones1r": ones1r,
            "tok0": np.array([[TC * c]], np.int32),
            "cpool": cpool,
        }
        cores.append(m)
    return cores


_BUILT = {}


def kernel(**inputs):
    taps = tuple(TAPS)
    nc, tap_handles = build(taps)
    in_maps = _stage_inputs(inputs)
    res = run_bass_kernel_spmd(nc, in_maps, core_ids=list(range(NCORES)))
    outs = [np.asarray(r["outT"]) for r in res.results]
    full = np.concatenate([o.T for o in outs], axis=0)[None]
    if taps:
        kernel.tap_results = [{k: np.asarray(r["tap_" + k]) for k in taps}
                              for r in res.results]
    return np.ascontiguousarray(full, dtype=np.float32)
